# version 12
# baseline (speedup 1.0000x reference)
import math
import os
import numpy as np
import ml_dtypes
import concourse.bass as bass
import concourse.mybir as mybir
from concourse.bass_utils import run_bass_kernel_spmd
from contextlib import ExitStack

F32 = mybir.dt.float32
BF16 = mybir.dt.bfloat16
F8 = mybir.dt.float8e4
AF = mybir.ActivationFunctionType
ALU = mybir.AluOpType
AX = mybir.AxisListType

N_CORES = 8
L = 2048
D = 2048
NH, DH, DHR = 16, 128, 64
DC, DCQ = 512, 1536
J = 8192
TT = 512
NTT = L // TT
NTC = L // 128
ND = D // 128
INV_SQRT_DH = 1.0 / math.sqrt(DH)

STAGE_FULL = 99
P6_NH = int(os.environ.get("P6_NH", str(NH)))


class Sem:
    def __init__(self, h):
        self.h = h
        self.n = 0

    def inc(self, ins, k=1):
        ins.then_inc(self.h, k)
        self.n += k
        return ins

    def dma(self, ins):
        return self.inc(ins, 16)


def build_nc(stage=STAGE_FULL, debug=False):
    nc = bass.Bass("TRN2", target_bir_lowering=False, debug=False,
                   detect_race_conditions=False)

    x_d = nc.dram_tensor("x", [L, D], F32, kind="ExternalInput")
    wdq_d = nc.dram_tensor("w_dq", [D, DCQ], F32, kind="ExternalInput")
    wuq_d = nc.dram_tensor("w_uq", [DCQ, NH * DH], F32, kind="ExternalInput")
    wqr_d = nc.dram_tensor("w_qr", [DCQ, NH * DHR], F32, kind="ExternalInput")
    wdkv_d = nc.dram_tensor("w_dkv", [D, DC], F32, kind="ExternalInput")
    wuk_d = nc.dram_tensor("w_uk", [DC, NH * DH], F32, kind="ExternalInput")
    wkr_d = nc.dram_tensor("w_kr", [D, NH * DHR], F32, kind="ExternalInput")
    wuv_d = nc.dram_tensor("w_uv", [DC, NH * DH], F32, kind="ExternalInput")
    wo_d = nc.dram_tensor("w_o", [NH * DH, D], F32, kind="ExternalInput")
    wfc_d = nc.dram_tensor("w_fc", [J, D], F32, kind="ExternalInput")
    wpj_d = nc.dram_tensor("w_pj", [D, J], F32, kind="ExternalInput")
    cos_d = nc.dram_tensor("cos_t", [32, L], F32, kind="ExternalInput")
    sin_d = nc.dram_tensor("sin_t", [32, L], F32, kind="ExternalInput")
    ones_r_d = nc.dram_tensor("ones_r", [1, 128], F32, kind="ExternalInput")
    ones_cb_d = nc.dram_tensor("ones_cb", [128, 128], BF16, kind="ExternalInput")
    rope_rt_d = nc.dram_tensor("rope_rt", [128, 128], F32, kind="ExternalInput")
    y_d = nc.dram_tensor("y", [L, D], F32, kind="ExternalOutput")
    dbg = {}
    if debug:
        dbg["xn"] = nc.dram_tensor("dbg_xn", [L, D], F32, kind="ExternalOutput")
        dbg["sc"] = nc.dram_tensor("dbg_sc", [1, 8], F32, kind="ExternalOutput")
        dbg["q"] = nc.dram_tensor("dbg_q", [2, NH * (DH + DHR), L], BF16, kind="ExternalOutput")
        dbg["k"] = nc.dram_tensor("dbg_k", [2, NH * (DH + DHR), L], BF16, kind="ExternalOutput")
        dbg["v"] = nc.dram_tensor("dbg_v", [2, L, NH * DH], BF16, kind="ExternalOutput")
        dbg["h"] = nc.dram_tensor("dbg_h", [J, L], F32, kind="ExternalOutput")
        dbg["oh"] = nc.dram_tensor("dbg_oh", [2, NH * DH, L], BF16, kind="ExternalOutput")
        dbg["h8"] = nc.dram_tensor("dbg_h8", [J, L], BF16, kind="ExternalOutput")
        dbg["w8p"] = nc.dram_tensor("dbg_w8p", [D, J], BF16, kind="ExternalOutput")
        dbg["sc2"] = nc.dram_tensor("dbg_sc2", [1, 8], F32, kind="ExternalOutput")

    xnhi_s = nc.dram_tensor("xnhi_s", [L, D], BF16)
    xnlo_s = nc.dram_tensor("xnlo_s", [L, D], BF16)
    wsplit = {}
    for nm, dd in (("wdq", (D, DCQ)), ("wuq", (DCQ, NH * DH)), ("wqr", (DCQ, NH * DHR)),
                   ("wdkv", (D, DC)), ("wuk", (DC, NH * DH)), ("wkr", (D, NH * DHR)),
                   ("wuv", (DC, NH * DH)), ("wo", (NH * DH, D))):
        wsplit[nm] = (nc.dram_tensor(nm + "_hi", list(dd), BF16),
                      nc.dram_tensor(nm + "_lo", list(dd), BF16))
    q_s = nc.dram_tensor("q_s", [2, NH * (DH + DHR), L], BF16)
    k_s = nc.dram_tensor("k_s", [2, NH * (DH + DHR), L], BF16)
    v_s = nc.dram_tensor("v_s", [2, L, NH * DH], BF16)
    oh_s = nc.dram_tensor("oh_s", [2, NH * DH, L], BF16)
    w8b_s = nc.dram_tensor("w8b_s", [J, D], BF16)
    w8pb_s = nc.dram_tensor("w8pb_s", [D, J], BF16)
    h_s = nc.dram_tensor("h_s", [J, L], F32)
    h8_s = nc.dram_tensor("h8_s", [J, L], BF16)
    cc1_i = nc.dram_tensor("cc1_i", [1, 8], F32)
    cc1_o = nc.dram_tensor("cc1_o", [1, 8], F32)
    cc2_i = nc.dram_tensor("cc2_i", [1, 8], F32)
    cc2_o = nc.dram_tensor("cc2_o", [1, 8], F32)
    row_s = nc.dram_tensor("row_s", [1, 128], F32)
    bc_s = nc.dram_tensor("bc_s", [1, 8], F32)

    with ExitStack() as es:
        E = es.enter_context
        sL = Sem(E(nc.semaphore("sL")))
        sS = Sem(E(nc.semaphore("sS")))
        sG = Sem(E(nc.semaphore("sG")))
        sM = Sem(E(nc.semaphore("sM")))
        sV = Sem(E(nc.semaphore("sV")))
        sC = Sem(E(nc.semaphore("sC")))
        sY = Sem(E(nc.semaphore("sY")))
        cc = Sem(E(nc.semaphore("cc")))
        sPC = Sem(E(nc.semaphore("sPC")))
        sPV = Sem(E(nc.semaphore("sPV")))
        sPG = Sem(E(nc.semaphore("sPG")))
        sS2 = Sem(E(nc.semaphore("sS2")))
        block = E(nc.Block())

        xnhi = [E(nc.sbuf_tensor(f"xnhi{i}", [128, L], BF16, is_transpose_dest=True)) for i in range(ND)]
        ones_r = E(nc.sbuf_tensor("sb_ones_r", [1, 128], F32))
        ones_cb = E(nc.sbuf_tensor("sb_ones_cb", [128, 128], BF16))
        rope_rt = E(nc.sbuf_tensor("sb_rope_rt", [128, 128], F32))
        scal = E(nc.sbuf_tensor("scal", [128, 8], F32))
        smax = E(nc.sbuf_tensor("smax", [128, 8], F32))
        srow = E(nc.sbuf_tensor("srow", [1, 128], F32))
        sone = E(nc.sbuf_tensor("sone", [1, 8], F32))
        spers = E(nc.sbuf_tensor("spers", [1, 2], F32))
        pA = E(nc.psum_tensor("pA", [128, 512], F32))
        pB = E(nc.psum_tensor("pB", [128, 512], F32))
        pC = E(nc.psum_tensor("pC", [128, 512], F32))
        pE = E(nc.psum_tensor("pE", [128, 512], F32))
        pD = E(nc.psum_tensor("pD", [1, 512], F32))
        pBC = E(nc.psum_tensor("pBC", [128, 512], F32))
        p3_scope = ExitStack()
        xnlo = [p3_scope.enter_context(nc.sbuf_tensor(f"xnlo{i}", [128, L], BF16, is_transpose_dest=True)) for i in range(ND)]
        cos128 = p3_scope.enter_context(nc.sbuf_tensor("cos128", [128, L], F32))
        sin128 = p3_scope.enter_context(nc.sbuf_tensor("sin128", [128, L], F32))

        @block.sync
        def _(sync):
            for r in range(4):
                sL.dma(sync.dma_start(out=cos128[32 * r:32 * (r + 1), :], in_=cos_d[:, :]))
                sL.dma(sync.dma_start(out=sin128[32 * r:32 * (r + 1), :], in_=sin_d[:, :]))
            sL.dma(sync.dma_start(out=ones_r[:, :], in_=ones_r_d[:, :]))
            sL.dma(sync.dma_start(out=ones_cb[:, :], in_=ones_cb_d[:, :]))
            sL.dma(sync.dma_start(out=rope_rt[:, :], in_=rope_rt_d[:, :]))
        p0_consts = sL.n

        @block.vector
        def _(vector):
            vector.memset(smax[:, :], 0.0)
            sV.inc(vector.memset(scal[:, :], 0.0))
        p0_vec = sV.n

        with ExitStack() as p1:
            E1 = p1.enter_context
            xt = [E1(nc.sbuf_tensor(f"p1x{i}", [128, D], F32)) for i in range(2)]
            sq = E1(nc.sbuf_tensor("p1sq", [128, D], F32))
            xnt = [E1(nc.sbuf_tensor(f"p1xn{i}", [128, D], F32)) for i in range(2)]
            hi_t = [E1(nc.sbuf_tensor(f"p1hi{i}", [128, D], BF16)) for i in range(2)]
            lo_t = [E1(nc.sbuf_tensor(f"p1lo{i}", [128, D], BF16)) for i in range(2)]
            st1 = E1(nc.sbuf_tensor("p1s1", [128, 1], F32))
            st2 = E1(nc.sbuf_tensor("p1s2", [128, 1], F32))
            st16 = E1(nc.sbuf_tensor("p1s16", [128, 16], F32))
            st16b = E1(nc.sbuf_tensor("p1s16b", [128, 16], F32))
            mm_ = E1(nc.sbuf_tensor("p1m", [128, 1], F32))
            ms_ = E1(nc.sbuf_tensor("p1ms", [128, 1], F32))
            var_ = E1(nc.sbuf_tensor("p1v", [128, 1], F32))
            var2_ = E1(nc.sbuf_tensor("p1v2", [128, 1], F32))
            var3_ = E1(nc.sbuf_tensor("p1v3", [128, 1], F32))
            junk_ = E1(nc.sbuf_tensor("p1jk", [128, 1], F32))
            junk2_ = E1(nc.sbuf_tensor("p1jk2", [128, 1], F32))
            sd_ = E1(nc.sbuf_tensor("p1sd", [128, 1], F32))
            rstd_ = E1(nc.sbuf_tensor("p1rs", [128, 1], F32))
            tmx = E1(nc.sbuf_tensor("p1mx", [128, 1], F32))

            bL, bS, bV, bC = sL.n, sS.n, sV.n, sC.n
            VPI = 6
            SPI = 3 if debug else 2

            @block.sync
            def _(sync):
                for i in range(NTC):
                    if i >= 2:
                        sync.wait_ge(sV.h, bV + (i - 1) * VPI)
                    sL.dma(sync.dma_start(out=xt[i % 2][:, :], in_=x_d[128 * i:128 * (i + 1), :]))

            @block.scalar
            def _(scalar):
                for i in range(NTC):
                    scalar.wait_ge(sV.h, bV + i * VPI + 1)
                    sC.inc(scalar.activation(out=sq[:, :], in_=xnt[i % 2][:, :], func=AF.Square))
                    scalar.wait_ge(sV.h, bV + i * VPI + 2)
                    sC.inc(scalar.activation(out=sd_[:, :], in_=var3_[:, :], func=AF.Sqrt))
                    scalar.wait_ge(sV.h, bV + i * VPI + 6)
                    sS.dma(scalar.dma_start(out=xnhi_s[128 * i:128 * (i + 1), :], in_=hi_t[i % 2][:, :]))
                    sS.dma(scalar.dma_start(out=xnlo_s[128 * i:128 * (i + 1), :], in_=lo_t[i % 2][:, :]))
                    if debug:
                        sS.dma(scalar.dma_start(out=dbg["xn"][128 * i:128 * (i + 1), :], in_=xnt[i % 2][:, :]))

            @block.vector
            def _(vector):
                vector.memset(junk2_[:, :], 0.0)
                for i in range(NTC):
                    vector.wait_ge(sL.h, bL + 16 * (i + 1))
                    vector.tensor_reduce(out=st16[:, :],
                                         in_=xt[i % 2][:, :].rearrange("p (a b) -> p a b", b=128),
                                         axis=AX.X, op=ALU.add)
                    vector.tensor_copy(out=junk_[:, :], in_=junk2_[:, :])
                    vector.reduce_sum(out=st1[:, :], in_=st16[:, :], axis=AX.X)
                    vector.tensor_copy(out=junk_[:, :], in_=junk2_[:, :])
                    vector.tensor_scalar(out=mm_[:, :], in0=st1[:, :], scalar1=1.0 / D, scalar2=None, op0=ALU.mult)
                    vector.tensor_copy(out=junk_[:, :], in_=junk2_[:, :])
                    if i >= 2:
                        vector.wait_ge(sS.h, bS + (i - 1) * SPI * 16)
                    sV.inc(vector.tensor_scalar(out=xnt[i % 2][:, :], in0=xt[i % 2][:, :], scalar1=mm_[:, :],
                                                scalar2=None, op0=ALU.subtract))
                    vector.wait_ge(sC.h, bC + i * 2 + 1)
                    vector.tensor_reduce(out=st16b[:, :],
                                         in_=sq[:, :].rearrange("p (a b) -> p a b", b=128),
                                         axis=AX.X, op=ALU.add)
                    vector.tensor_copy(out=junk_[:, :], in_=junk2_[:, :])
                    vector.reduce_sum(out=st2[:, :], in_=st16b[:, :], axis=AX.X)
                    vector.tensor_copy(out=junk_[:, :], in_=junk2_[:, :])
                    sV.inc(vector.tensor_scalar(out=var3_[:, :], in0=st2[:, :], scalar1=1.0 / D,
                                                scalar2=1e-5, op0=ALU.mult, op1=ALU.add))
                    vector.wait_ge(sC.h, bC + i * 2 + 2)
                    vector.reciprocal(out=rstd_[:, :], in_=sd_[:, :])
                    vector.tensor_copy(out=junk_[:, :], in_=junk2_[:, :])
                    vector.tensor_tensor(out=var_[:, :], in0=sd_[:, :], in1=rstd_[:, :], op=ALU.mult)
                    vector.tensor_copy(out=junk_[:, :], in_=junk2_[:, :])
                    vector.tensor_scalar(out=var2_[:, :], in0=var_[:, :], scalar1=-1.0,
                                         scalar2=2.0, op0=ALU.mult, op1=ALU.add)
                    vector.tensor_copy(out=junk_[:, :], in_=junk2_[:, :])
                    sV.inc(vector.tensor_tensor(out=mm_[:, :], in0=rstd_[:, :], in1=var2_[:, :],
                                                op=ALU.mult))
                    vector.tensor_copy(out=junk_[:, :], in_=junk2_[:, :])
                    vector.tensor_scalar(out=xnt[i % 2][:, :], in0=xnt[i % 2][:, :], scalar1=mm_[:, :],
                                         scalar2=None, op0=ALU.mult)
                    vector.tensor_copy(out=junk_[:, :], in_=junk2_[:, :])
                    vector.reduce_max(out=tmx[:, :], in_=xnt[i % 2][:, :], axis=AX.X, apply_absolute_value=True)
                    a, b = (0, 1) if i % 2 == 0 else (1, 0)
                    sV.inc(vector.tensor_copy(out=hi_t[i % 2][:, :], in_=xnt[i % 2][:, :]))
                    sV.inc(vector.tensor_tensor(out=smax[:, b:b + 1], in0=smax[:, a:a + 1],
                                                in1=tmx[:, :], op=ALU.max))
                    sV.inc(vector.scalar_tensor_tensor(out=lo_t[i % 2][:, :], in0=xnt[i % 2][:, :], scalar=1.0,
                                                       in1=hi_t[i % 2][:, :], op0=ALU.mult, op1=ALU.subtract))

            p1_spills = sS.n
            p1_vec = sV.n

            @block.sync
            def _(sync):
                sync.wait_ge(sS.h, p1_spills)
                for dc_ in range(ND):
                    sL.dma(sync.dma_start(out=xnhi[dc_][:, :],
                                          in_=xnhi_s[:, 128 * dc_:128 * (dc_ + 1)], transpose=True))
                    sL.dma(sync.dma_start(out=xnlo[dc_][:, :],
                                          in_=xnlo_s[:, 128 * dc_:128 * (dc_ + 1)], transpose=True))
            p1_xn_ready = sL.n

        @block.gpsimd
        def _(gpsimd):
            gpsimd.wait_ge(sV.h, p1_vec)
            sG.dma(gpsimd.dma_start(out=row_s[:, :], in_=smax[:, 0:1]))
            gpsimd.wait_ge(sG.h, sG.n)
            sG.dma(gpsimd.dma_start(out=srow[:, :], in_=row_s[:, :]))
        p2_row = sG.n

        @block.vector
        def _(vector):
            vector.wait_ge(sG.h, p2_row)
            vector.memset(sone[:, :], 0.0)
            sV.inc(vector.reduce_max(out=sone[:, 0:1], in_=srow[:, :], axis=AX.X))
        p2_lmax = sV.n

        @block.gpsimd
        def _(gpsimd):
            gpsimd.wait_ge(sV.h, p2_lmax)
            sG.dma(gpsimd.dma_start(out=cc1_i[:, :], in_=sone[:, :]))
            gpsimd.wait_ge(sG.h, sG.n)
            cc.inc(gpsimd.collective_compute(
                "AllReduce", ALU.max, replica_groups=[list(range(N_CORES))],
                ins=[cc1_i.ap().opt()], outs=[cc1_o.ap().opt()]))
        cc1_done = cc.n

        if stage <= 1:
            @block.gpsimd
            def _(gpsimd):
                gpsimd.wait_ge(sL.h, p1_xn_ready)
                for i in range(NTC):
                    sY.dma(gpsimd.dma_start(out=y_d[128 * i:128 * (i + 1), :],
                                            in_=xnhi[i][:, 0:2048]))
                if debug:
                    gpsimd.wait_ge(cc.h, cc1_done)
                    sG.dma(gpsimd.dma_start(out=sone[:, :], in_=cc1_o[:, :]))
                    gpsimd.wait_ge(sG.h, sG.n)
                    sY.dma(gpsimd.dma_start(out=dbg["sc"][:, :], in_=sone[:, :]))
                gpsimd.wait_ge(sY.h, sY.n)
            return nc

        with ExitStack() as pw:
            EW = pw.enter_context
            wf32 = [EW(nc.sbuf_tensor(f"wsp{i}", [128, 2048], F32)) for i in range(2)]
            whib = [EW(nc.sbuf_tensor(f"wsph{i}", [128, 2048], BF16)) for i in range(2)]
            wlob = [EW(nc.sbuf_tensor(f"wspl{i}", [128, 2048], BF16)) for i in range(2)]
            wjk = EW(nc.sbuf_tensor("wspj", [128, 1], F32))
            wjk2 = EW(nc.sbuf_tensor("wspj2", [128, 1], F32))
            splits = [("wdq", wdq_d, D, DCQ), ("wuq", wuq_d, DCQ, NH * DH), ("wqr", wqr_d, DCQ, NH * DHR),
                      ("wdkv", wdkv_d, D, DC), ("wuk", wuk_d, DC, NH * DH), ("wkr", wkr_d, D, NH * DHR),
                      ("wuv", wuv_d, DC, NH * DH), ("wo", wo_d, NH * DH, D)]
            chunks = []
            for nm, dram, rows, cols in splits:
                for r in range(rows // 128):
                    chunks.append((nm, dram, r, cols))
            NWC = len(chunks)
            bL, bS, bV = sL.n, sS.n, sV.n

            @block.sync
            def _(sync):
                for n, (nm, dram, r, cols) in enumerate(chunks):
                    if n >= 2:
                        sync.wait_ge(sV.h, bV + (n - 1) * 2)
                    sL.dma(sync.dma_start(out=wf32[n % 2][:, 0:cols], in_=dram[128 * r:128 * (r + 1), :]))

            @block.vector
            def _(vector):
                vector.memset(wjk2[:, :], 0.0)
                for n, (nm, dram, r, cols) in enumerate(chunks):
                    vector.wait_ge(sL.h, bL + 16 * (n + 1))
                    if n >= 2:
                        vector.wait_ge(sS.h, bS + (n - 1) * 2 * 16)
                    sV.inc(vector.tensor_copy(out=whib[n % 2][:, 0:cols], in_=wf32[n % 2][:, 0:cols]))
                    vector.tensor_copy(out=wjk[:, :], in_=wjk2[:, :])
                    sV.inc(vector.scalar_tensor_tensor(out=wlob[n % 2][:, 0:cols], in0=wf32[n % 2][:, 0:cols],
                                                       scalar=1.0, in1=whib[n % 2][:, 0:cols],
                                                       op0=ALU.mult, op1=ALU.subtract))

            @block.scalar
            def _(scalar):
                for n, (nm, dram, r, cols) in enumerate(chunks):
                    hi_d, lo_d = wsplit[nm]
                    scalar.wait_ge(sV.h, bV + n * 2 + 2)
                    sS.dma(scalar.dma_start(out=hi_d[128 * r:128 * (r + 1), :], in_=whib[n % 2][:, 0:cols]))
                    sS.dma(scalar.dma_start(out=lo_d[128 * r:128 * (r + 1), :], in_=wlob[n % 2][:, 0:cols]))
            wsplit_done = sS.n

        with ExitStack() as p3:
            E3 = p3.enter_context
            rg_hi = [E3(nc.sbuf_tensor(f"p3rh{i}", [128, 512], BF16)) for i in range(2)]
            rg_lo = [E3(nc.sbuf_tensor(f"p3rl{i}", [128, 512], BF16)) for i in range(2)]
            qc_hi = [E3(nc.sbuf_tensor(f"p3qh{i}", [128, 512], BF16)) for i in range(DCQ // 128)]
            qc_lo = [E3(nc.sbuf_tensor(f"p3ql{i}", [128, 512], BF16)) for i in range(DCQ // 128)]
            c_hi = [E3(nc.sbuf_tensor(f"p3ch{i}", [128, 512], BF16)) for i in range(DC // 128)]
            c_lo = [E3(nc.sbuf_tensor(f"p3cl{i}", [128, 512], BF16)) for i in range(DC // 128)]
            stg_hi = [[E3(nc.sbuf_tensor(f"p3sh{ss}_{i}", [128, 512], BF16)) for i in range(4)] for ss in range(2)]
            stg_lo = [[E3(nc.sbuf_tensor(f"p3sl{ss}_{i}", [128, 512], BF16)) for i in range(4)] for ss in range(2)]
            rtmp = [E3(nc.sbuf_tensor(f"p3rt{i}", [128, 512], F32)) for i in range(2)]
            rb = [E3(nc.sbuf_tensor(f"p3rb{i}", [128, 512], F32)) for i in range(2)]
            pj = E3(nc.sbuf_tensor("p3jk", [128, 1], F32))
            pj2 = E3(nc.sbuf_tensor("p3jk2", [128, 1], F32))
            PS4 = [pA, pB, pC, pE]

            @block.vector
            def _(vector):
                sV.inc(vector.memset(pj2[:, :], 0.0))

            chain = dict(V=sV.n, M=sM.n)

            def emit_proj(tau, wname, K, cols, moving_hi, moving_lo, mslice, evac,
                          row0=0, scr=None, dst=None):
                whi_d, wlo_d = wsplit[wname]
                NG = cols // 512
                tok = tau * TT
                bL0, bM0, bV0, bG0 = sL.n, sM.n, sV.n, sG.n
                pre_V, pre_M = chain["V"], chain["M"]
                VPG = 3 if evac == "rope" else 2
                TPG = K + (4 if evac == "rope" else 0)

                @block.sync
                def _(sync):
                    sync.wait_ge(sS.h, wsplit_done)
                    for g in range(NG):
                        for k in range(K):
                            gk = g * K + k
                            if gk >= 2:
                                g2, k2 = divmod(gk - 2, K)
                                sync.wait_ge(sM.h, bM0 + g2 * TPG + k2 + 1)
                            else:
                                sync.wait_ge(sM.h, bM0)
                            sL.dma(sync.dma_start(out=rg_hi[gk % 2][:, :],
                                                  in_=whi_d[128 * k:128 * (k + 1), 512 * g:512 * (g + 1)]))
                            sL.dma(sync.dma_start(out=rg_lo[gk % 2][:, :],
                                                  in_=wlo_d[128 * k:128 * (k + 1), 512 * g:512 * (g + 1)]))

                @block.tensor
                def _(tensor):
                    tensor.wait_ge(sL.h, p1_xn_ready)
                    tensor.wait_ge(sV.h, pre_V)
                    for g in range(NG):
                        if g >= 1:
                            tensor.wait_ge(sV.h, bV0 + g * VPG)
                        for k in range(K):
                            gk = g * K + k
                            tensor.wait_ge(sL.h, bL0 + 32 * (gk + 1))
                            e = gk % 2
                            mh = moving_hi[k][:, mslice]
                            ml = moving_lo[k][:, mslice]
                            for oc in range(4):
                                wh = rg_hi[e][:, 128 * oc:128 * (oc + 1)]
                                wl = rg_lo[e][:, 128 * oc:128 * (oc + 1)]
                                tensor.matmul(PS4[oc][:, :], wh, mh, start=(k == 0), stop=False)
                                tensor.matmul(PS4[oc][:, :], wh, ml, start=False, stop=False)
                                ins = tensor.matmul(PS4[oc][:, :], wl, mh, start=False, stop=(k == K - 1))
                            sM.inc(ins)
                        if evac == "rope":
                            for half in range(2):
                                tensor.wait_ge(sV.h, bV0 + g * VPG + half + 1)
                                for j in range(2):
                                    sM.inc(tensor.matmul(PS4[2 * half + j][:, :], rope_rt[:, :],
                                                         rtmp[j][:, :], start=True, stop=True))

                @block.vector
                def _(vector):
                    vector.wait_ge(sM.h, pre_M)
                    for g in range(NG):
                        ss = g % 2
                        if evac != "local":
                            vector.wait_ge(sG.h, bG0 if g < 2 else bG0 + (g - 1) * 8 * 16)
                        if evac == "rope":
                            cs = cos128[:, tok:tok + TT]
                            sn = sin128[:, tok:tok + TT]
                            for half in range(2):
                                if half == 0:
                                    vector.wait_ge(sM.h, bM0 + g * TPG + K)
                                o0, o1 = 2 * half, 2 * half + 1
                                vector.tensor_copy(out=rtmp[0][:, :], in_=PS4[o0][:, :])
                                vector.tensor_copy(out=rtmp[1][:, :], in_=PS4[o1][:, :])
                                sV.inc(vector.tensor_copy(out=pj[:, :], in_=pj2[:, :]))
                                vector.wait_ge(sM.h, bM0 + g * TPG + K + 2 * (half + 1))
                                vector.tensor_tensor(out=rtmp[0][:, :], in0=rtmp[0][:, :], in1=cs, op=ALU.mult)
                                vector.tensor_tensor(out=rtmp[1][:, :], in0=rtmp[1][:, :], in1=cs, op=ALU.mult)
                                vector.tensor_tensor(out=rb[0][:, :], in0=PS4[o0][:, :], in1=sn, op=ALU.mult)
                                vector.tensor_tensor(out=rb[1][:, :], in0=PS4[o1][:, :], in1=sn, op=ALU.mult)
                                vector.tensor_tensor(out=rtmp[0][:, :], in0=rtmp[0][:, :], in1=rb[0][:, :], op=ALU.add)
                                vector.tensor_tensor(out=rtmp[1][:, :], in0=rtmp[1][:, :], in1=rb[1][:, :], op=ALU.add)
                                vector.tensor_copy(out=stg_hi[ss][o0][:, :], in_=rtmp[0][:, :])
                                vector.tensor_copy(out=stg_hi[ss][o1][:, :], in_=rtmp[1][:, :])
                                vector.scalar_tensor_tensor(out=stg_lo[ss][o0][:, :], in0=rtmp[0][:, :],
                                                            scalar=1.0, in1=stg_hi[ss][o0][:, :],
                                                            op0=ALU.mult, op1=ALU.subtract)
                                vector.scalar_tensor_tensor(out=stg_lo[ss][o1][:, :], in0=rtmp[1][:, :],
                                                            scalar=1.0, in1=stg_hi[ss][o1][:, :],
                                                            op0=ALU.mult, op1=ALU.subtract)
                            sV.inc(vector.tensor_copy(out=pj[:, :], in_=pj2[:, :]))
                        else:
                            vector.wait_ge(sM.h, bM0 + (g + 1) * TPG)
                            if evac == "local":
                                hv = [dst[0][g * 4 + oc] for oc in range(4)]
                                lv = [dst[1][g * 4 + oc] for oc in range(4)]
                            else:
                                hv, lv = stg_hi[ss], stg_lo[ss]
                            for oc in range(4):
                                vector.tensor_copy(out=hv[oc][:, :], in_=PS4[oc][:, :])
                            sV.inc(vector.tensor_copy(out=pj[:, :], in_=pj2[:, :]))
                            for oc in range(4):
                                vector.scalar_tensor_tensor(out=lv[oc][:, :], in0=PS4[oc][:, :],
                                                            scalar=1.0, in1=hv[oc][:, :],
                                                            op0=ALU.mult, op1=ALU.subtract)
                            sV.inc(vector.tensor_copy(out=pj[:, :], in_=pj2[:, :]))

                if evac != "local":
                    @block.gpsimd
                    def _(gpsimd):
                        for g in range(NG):
                            ss = g % 2
                            gpsimd.wait_ge(sV.h, bV0 + (g + 1) * VPG)
                            for oc in range(4):
                                rr = row0 + 512 * g + 128 * oc
                                sG.dma(gpsimd.dma_start(out=scr[0, rr:rr + 128, tok:tok + TT],
                                                        in_=stg_hi[ss][oc][:, :]))
                                sG.dma(gpsimd.dma_start(out=scr[1, rr:rr + 128, tok:tok + TT],
                                                        in_=stg_lo[ss][oc][:, :]))

                chain["V"], chain["M"] = sV.n, sM.n

            def emit_v(tau):
                whi_d, wlo_d = wsplit["wuv"]
                NG = 4
                K = 4
                tok = tau * TT
                bL0, bM0, bV0, bG0 = sL.n, sM.n, sV.n, sG.n
                pre_V, pre_M = chain["V"], chain["M"]
                VPG, TPG = 2, K

                @block.sync
                def _(sync):
                    sync.wait_ge(sS.h, wsplit_done)
                    for g in range(NG):
                        for k in range(K):
                            gk = g * K + k
                            if gk >= 2:
                                g2, k2 = divmod(gk - 2, K)
                                sync.wait_ge(sM.h, bM0 + g2 * TPG + k2 + 1)
                            else:
                                sync.wait_ge(sM.h, bM0)
                            sL.dma(sync.dma_start(out=rg_hi[gk % 2][:, :],
                                                  in_=whi_d[128 * k:128 * (k + 1), 512 * g:512 * (g + 1)]))
                            sL.dma(sync.dma_start(out=rg_lo[gk % 2][:, :],
                                                  in_=wlo_d[128 * k:128 * (k + 1), 512 * g:512 * (g + 1)]))

                @block.tensor
                def _(tensor):
                    tensor.wait_ge(sV.h, pre_V)
                    for g in range(NG):
                        if g >= 1:
                            tensor.wait_ge(sV.h, bV0 + g * VPG)
                        for k in range(K):
                            gk = g * K + k
                            tensor.wait_ge(sL.h, bL0 + 32 * (gk + 1))
                            e = gk % 2
                            for tc in range(4):
                                ch = c_hi[k][:, 128 * tc:128 * (tc + 1)]
                                cl = c_lo[k][:, 128 * tc:128 * (tc + 1)]
                                tensor.matmul(PS4[tc][:, :], ch, rg_hi[e][:, :], start=(k == 0), stop=False)
                                tensor.matmul(PS4[tc][:, :], cl, rg_hi[e][:, :], start=False, stop=False)
                                ins = tensor.matmul(PS4[tc][:, :], ch, rg_lo[e][:, :], start=False, stop=(k == K - 1))
                            sM.inc(ins)

                @block.vector
                def _(vector):
                    vector.wait_ge(sM.h, pre_M)
                    for g in range(NG):
                        ss = g % 2
                        vector.wait_ge(sG.h, bG0 if g < 2 else bG0 + (g - 1) * 8 * 16)
                        vector.wait_ge(sM.h, bM0 + (g + 1) * TPG)
                        for tc in range(4):
                            vector.tensor_copy(out=stg_hi[ss][tc][:, :], in_=PS4[tc][:, :])
                        sV.inc(vector.tensor_copy(out=pj[:, :], in_=pj2[:, :]))
                        for tc in range(4):
                            vector.scalar_tensor_tensor(out=stg_lo[ss][tc][:, :], in0=PS4[tc][:, :],
                                                        scalar=1.0, in1=stg_hi[ss][tc][:, :],
                                                        op0=ALU.mult, op1=ALU.subtract)
                        sV.inc(vector.tensor_copy(out=pj[:, :], in_=pj2[:, :]))

                @block.gpsimd
                def _(gpsimd):
                    for g in range(NG):
                        ss = g % 2
                        gpsimd.wait_ge(sV.h, bV0 + (g + 1) * VPG)
                        for tc in range(4):
                            tr = tok + 128 * tc
                            sG.dma(gpsimd.dma_start(out=v_s[0, tr:tr + 128, 512 * g:512 * (g + 1)],
                                                    in_=stg_hi[ss][tc][:, :]))
                            sG.dma(gpsimd.dma_start(out=v_s[1, tr:tr + 128, 512 * g:512 * (g + 1)],
                                                    in_=stg_lo[ss][tc][:, :]))

                chain["V"], chain["M"] = sV.n, sG.n if False else sM.n

            ts_all = slice(None)
            for tau in range(NTT):
                msl = slice(tau * TT, tau * TT + TT)
                emit_proj(tau, "wdq", ND, DCQ, xnhi, xnlo, msl, "local", dst=(qc_hi, qc_lo))
                emit_proj(tau, "wuq", DCQ // 128, NH * DH, qc_hi, qc_lo, ts_all, "spill", row0=0, scr=q_s)
                emit_proj(tau, "wqr", DCQ // 128, NH * DHR, qc_hi, qc_lo, ts_all, "rope", row0=2048, scr=q_s)
                emit_proj(tau, "wdkv", ND, DC, xnhi, xnlo, msl, "local", dst=(c_hi, c_lo))
                emit_proj(tau, "wuk", DC // 128, NH * DH, c_hi, c_lo, ts_all, "spill", row0=0, scr=k_s)
                emit_proj(tau, "wkr", ND, NH * DHR, xnhi, xnlo, msl, "rope", row0=2048, scr=k_s)
                emit_v(tau)
            p3_done_G = sG.n
            p3_done_V = sV.n
            p3_done_M = sM.n

        def emit_wscan(w_dram, rows, cols, colpair):
            bL0, bV0 = sL.n, sV.n
            n_r, n_c = rows // 128, cols // 2048
            NCH = n_r * n_c
            a0, b0 = colpair

            @block.sync
            def _(sync):
                sync.wait_ge(sM.h, p3_done_M)
                sync.wait_ge(sG.h, p3_done_G)
                n = 0
                for r in range(n_r):
                    for cgrp in range(n_c):
                        if n >= 3:
                            sync.wait_ge(sV.h, bV0 + n - 2)
                        sL.dma(sync.dma_start(out=wsc[n % 3][:, :],
                                              in_=w_dram[128 * r:128 * (r + 1), 2048 * cgrp:2048 * (cgrp + 1)]))
                        n += 1

            @block.vector
            def _(vector):
                for n in range(NCH):
                    vector.wait_ge(sL.h, bL0 + 16 * (n + 1))
                    sV.inc(vector.reduce_max(out=wmx[n % 2][:, :], in_=wsc[n % 3][:, :],
                                             axis=AX.X, apply_absolute_value=True))
                    if n >= 1:
                        c_ = n - 1
                        aa = a0 if c_ % 2 == 0 else b0
                        bb = b0 if c_ % 2 == 0 else a0
                        vector.tensor_tensor(out=smax[:, bb:bb + 1], in0=smax[:, aa:aa + 1],
                                             in1=wmx[(n - 1) % 2][:, :], op=ALU.max)
                vector.tensor_copy(out=pjk[:, :], in_=pjk2[:, :])
                c_ = NCH - 1
                aa = a0 if c_ % 2 == 0 else b0
                bb = b0 if c_ % 2 == 0 else a0
                sV.inc(vector.tensor_tensor(out=smax[:, bb:bb + 1], in0=smax[:, aa:aa + 1],
                                            in1=wmx[(NCH - 1) % 2][:, :], op=ALU.max))
            return b0 if (NCH - 1) % 2 == 0 else a0

        with ExitStack() as pws:
            EW = pws.enter_context
            wsc = [EW(nc.sbuf_tensor(f"wsc{i}", [128, 2048], F32)) for i in range(3)]
            wmx = [EW(nc.sbuf_tensor(f"wmx{i}", [128, 1], F32)) for i in range(2)]
            sgx = EW(nc.sbuf_tensor("sgx", [1, 8], F32))
            wm1 = EW(nc.sbuf_tensor("wm1", [1, 1], F32))
            wm2 = EW(nc.sbuf_tensor("wm2", [1, 1], F32))
            sxv = EW(nc.sbuf_tensor("sxv", [1, 1], F32))
            sw1v = EW(nc.sbuf_tensor("sw1v", [1, 1], F32))
            sw2v = EW(nc.sbuf_tensor("sw2v", [1, 1], F32))
            rsx = EW(nc.sbuf_tensor("rsx", [1, 1], F32))
            rsw1 = EW(nc.sbuf_tensor("rsw1", [1, 1], F32))
            rsw2 = EW(nc.sbuf_tensor("rsw2", [1, 1], F32))
            scrow = EW(nc.sbuf_tensor("scrow", [1, 8], F32))
            pjk = EW(nc.sbuf_tensor("wsjk", [128, 1], F32))
            pjk2 = EW(nc.sbuf_tensor("wsjk2", [128, 1], F32))

            @block.vector
            def _(vector):
                sV.inc(vector.memset(pjk2[:, :], 0.0))
                vector.memset(scrow[:, :], 0.0)
            col_w1 = emit_wscan(wfc_d, J, D, (4, 5))
            col_w2 = emit_wscan(wpj_d, D, J, (6, 7))
            ws_done_V = sV.n

            @block.gpsimd
            def _(gpsimd):
                gpsimd.wait_ge(sV.h, ws_done_V)
                sG.dma(gpsimd.dma_start(out=row_s[:, :], in_=smax[:, col_w1:col_w1 + 1]))
                gpsimd.wait_ge(sG.h, sG.n)
                sG.dma(gpsimd.dma_start(out=srow[:, :], in_=row_s[:, :]))
                gpsimd.wait_ge(sG.h, sG.n)
                gpsimd.wait_ge(cc.h, cc1_done)
                sG.dma(gpsimd.dma_start(out=sgx[:, :], in_=cc1_o[:, :]))
            sc1_g = sG.n

            @block.vector
            def _(vector):
                vector.wait_ge(sG.h, sc1_g)
                sV.inc(vector.reduce_max(out=wm1[:, :], in_=srow[:, :], axis=AX.X))
            sc1_v = sV.n

            @block.gpsimd
            def _(gpsimd):
                gpsimd.wait_ge(sV.h, sc1_v)
                sG.dma(gpsimd.dma_start(out=row_s[:, :], in_=smax[:, col_w2:col_w2 + 1]))
                gpsimd.wait_ge(sG.h, sG.n)
                sG.dma(gpsimd.dma_start(out=srow[:, :], in_=row_s[:, :]))
            sc2_g = sG.n

            @block.vector
            def _(vector):
                F = lambda: vector.tensor_copy(out=pjk[:, :], in_=pjk2[:, :])
                vector.wait_ge(sG.h, sc2_g)
                vector.reduce_max(out=wm2[:, :], in_=srow[:, :], axis=AX.X)
                vector.tensor_scalar(out=sxv[:, :], in0=sgx[0:1, 0:1], scalar1=1.0 / 448.0,
                                     scalar2=1e-6, op0=ALU.mult, op1=ALU.max)
                vector.tensor_scalar(out=sw1v[:, :], in0=wm1[:, :], scalar1=1.0 / 448.0,
                                     scalar2=1e-6, op0=ALU.mult, op1=ALU.max)
                vector.tensor_scalar(out=sw2v[:, :], in0=wm2[:, :], scalar1=1.0 / 448.0,
                                     scalar2=1e-6, op0=ALU.mult, op1=ALU.max)
                F()
                vector.reciprocal(out=rsx[:, :], in_=sxv[:, :])
                vector.reciprocal(out=rsw1[:, :], in_=sw1v[:, :])
                vector.reciprocal(out=rsw2[:, :], in_=sw2v[:, :])
                F()
                vector.tensor_scalar(out=scrow[0:1, 0:1], in0=rsx[:, :], scalar1=0.5, scalar2=None, op0=ALU.mult)
                vector.tensor_scalar(out=scrow[0:1, 4:5], in0=rsw1[:, :], scalar1=0.5, scalar2=None, op0=ALU.mult)
                vector.tensor_scalar(out=scrow[0:1, 5:6], in0=rsw2[:, :], scalar1=0.5, scalar2=None, op0=ALU.mult)
                vector.tensor_scalar(out=scrow[0:1, 1:2], in0=sxv[:, :], scalar1=sw1v[0:1, 0:1],
                                     scalar2=4.0, op0=ALU.mult, op1=ALU.mult)
                vector.tensor_copy(out=spers[0:1, 0:1], in_=sw2v[:, :])
                vector.tensor_copy(out=spers[0:1, 1:2], in_=scrow[0:1, 5:6])
                F()
                sV.inc(vector.tensor_copy(out=pjk[:, :], in_=pjk2[:, :]))
            sc3_v = sV.n

            @block.gpsimd
            def _(gpsimd):
                gpsimd.wait_ge(sV.h, sc3_v)
                sG.dma(gpsimd.dma_start(out=bc_s[:, :], in_=scrow[:, :]))
                gpsimd.wait_ge(sG.h, sG.n)
                sG.dma(gpsimd.dma_start(out=scal[:, :], in_=bc_s.ap().broadcast_to([128, 8])))
            scales1_g = sG.n

        with ExitStack() as p4a:
            E4a = p4a.enter_context
            x32 = [E4a(nc.sbuf_tensor(f"p4x{i}", [128, L], F32)) for i in range(2)]
            xf8 = [E4a(nc.sbuf_tensor(f"p4f{i}", [128, L], F8)) for i in range(2)]
            bV0, bC0 = sV.n, sC.n

            @block.vector
            def _(vector):
                vector.wait_ge(sG.h, scales1_g)
                vector.wait_ge(sM.h, p3_done_M)
                for d in range(ND):
                    if d >= 2:
                        vector.wait_ge(sC.h, bC0 + d - 1)
                    sV.inc(vector.tensor_tensor(out=x32[d % 2][:, :], in0=xnhi[d][:, :],
                                                in1=xnlo[d][:, :], op=ALU.add))
                    if d >= 1:
                        vector.wait_ge(sC.h, bC0 + d)
                        sV.inc(vector.tensor_copy(out=xnhi[d - 1][:, :], in_=xf8[(d - 1) % 2][:, :]))
                vector.wait_ge(sC.h, bC0 + ND)
                sV.inc(vector.tensor_copy(out=xnhi[ND - 1][:, :], in_=xf8[(ND - 1) % 2][:, :]))

            @block.scalar
            def _(scalar):
                for d in range(ND):
                    scalar.wait_ge(sV.h, bV0 + 2 * d + 1)
                    sC.inc(scalar.activation(out=xf8[d % 2][:, :], in_=x32[d % 2][:, :],
                                             func=AF.Copy, scale=scal[:, 0:1]))
            p4a_V = sV.n
        p3_scope.close()

        with ExitStack() as p4:
            E4 = p4.enter_context
            wf32r = [E4(nc.sbuf_tensor(f"p4w{i}", [128, D], F32)) for i in range(2)]
            wf8r = [E4(nc.sbuf_tensor(f"p4q{i}", [128, D], F8)) for i in range(2)]
            wb16r = [E4(nc.sbuf_tensor(f"p4b{i}", [128, D], BF16)) for i in range(2)]
            w8T = [[E4(nc.sbuf_tensor(f"p4t{ss}_{i}", [128, 512], BF16, is_transpose_dest=True))
                    for i in range(ND)] for ss in range(2)]
            hrelu = [E4(nc.sbuf_tensor(f"p4hr{i}", [128, 512], F32)) for i in range(2)]
            htile = [E4(nc.sbuf_tensor(f"p4ht{i}", [128, 512], F32)) for i in range(2)]
            hmx = [E4(nc.sbuf_tensor(f"p4hm{i}", [128, 1], F32)) for i in range(2)]
            pjk = E4(nc.sbuf_tensor("p4jk", [128, 1], F32))
            pjk2 = E4(nc.sbuf_tensor("p4jk2", [128, 1], F32))

            @block.vector
            def _(vector):
                sV.inc(vector.memset(pjk2[:, :], 0.0))

            bL0, bC0b, bV0b, bG0 = sL.n, sC.n, sV.n, sG.n

            @block.sync
            def _(sync):
                sync.wait_ge(sV.h, p4a_V)
                for j in range(J // 128):
                    if j >= 2:
                        sync.wait_ge(sC.h, bC0b + j - 1)
                    sL.dma(sync.dma_start(out=wf32r[j % 2][:, :], in_=wfc_d[128 * j:128 * (j + 1), :]))

            @block.scalar
            def _(scalar):
                scalar.wait_ge(sG.h, scales1_g)
                for j in range(J // 128):
                    scalar.wait_ge(sL.h, bL0 + 16 * (j + 1))
                    if j >= 2:
                        scalar.wait_ge(sV.h, bV0b + j - 1)
                    sC.inc(scalar.activation(out=wf8r[j % 2][:, :], in_=wf32r[j % 2][:, :],
                                             func=AF.Copy, scale=scal[:, 4:5]))

            @block.vector
            def _(vector):
                for j in range(J // 128):
                    vector.wait_ge(sC.h, bC0b + j + 1)
                    if j >= 2:
                        vector.wait_ge(sG.h, bG0 + 16 * (j - 1))
                    sV.inc(vector.tensor_copy(out=wb16r[j % 2][:, :], in_=wf8r[j % 2][:, :]))

            @block.gpsimd
            def _(gpsimd):
                for j in range(J // 128):
                    gpsimd.wait_ge(sV.h, bV0b + j + 1)
                    sG.dma(gpsimd.dma_start(out=w8b_s[128 * j:128 * (j + 1), :], in_=wb16r[j % 2][:, :]))
            p4b_G = sG.n

            NJG = J // 512
            PPG = 16
            MTOT = NJG * PPG
            GPM = 32 if debug else 16
            bL1, bM1, bV1, bC1, bG1 = sL.n, sM.n, sV.n, sC.n, sG.n

            @block.sync
            def _(sync):
                for g in range(NJG):
                    sync.wait_ge(sG.h, bG0 + 16 * min(4 * (g + 1), J // 128))
                    if g >= 2:
                        sync.wait_ge(sM.h, bM1 + (g - 1) * PPG)
                    for d in range(ND):
                        sL.dma(sync.dma_start(out=w8T[g % 2][d][:, :],
                                              in_=w8b_s[512 * g:512 * (g + 1), 128 * d:128 * (d + 1)],
                                              transpose=True))

            @block.tensor
            def _(tensor):
                tensor.wait_ge(sV.h, p4a_V)
                tensor.wait_ge(sV.h, p3_done_V)
                m = 0
                for g in range(NJG):
                    tensor.wait_ge(sL.h, bL1 + 16 * ND * (g + 1))
                    for jc in range(4):
                        for t in range(NTT):
                            if m >= 2:
                                tensor.wait_ge(sC.h, bC1 + 2 * (m - 2) + 1)
                            p = pA if m % 2 == 0 else pB
                            for d in range(ND):
                                ins = tensor.matmul(p[:, :], w8T[g % 2][d][:, 128 * jc:128 * (jc + 1)],
                                                    xnhi[d][:, 512 * t:512 * (t + 1)],
                                                    start=(d == 0), stop=(d == ND - 1))
                            sM.inc(ins)
                            m += 1

            @block.scalar
            def _(scalar):
                for m in range(MTOT):
                    scalar.wait_ge(sM.h, bM1 + m + 1)
                    p = pA if m % 2 == 0 else pB
                    if m >= 3:
                        scalar.wait_ge(sG.h, bG1 + GPM * (m - 2))
                    sC.inc(scalar.activation(out=hrelu[m % 2][:, :], in_=p[:, :],
                                             func=AF.Relu, scale=scal[:, 1:2]))
                    if m >= 1:
                        sC.inc(scalar.activation(out=htile[(m - 1) % 2][:, :],
                                                 in_=hrelu[(m - 1) % 2][:, :], func=AF.Square))
                scalar.wait_ge(sG.h, bG1 + GPM * (MTOT - 2))
                sC.inc(scalar.activation(out=htile[(MTOT - 1) % 2][:, :],
                                         in_=hrelu[(MTOT - 1) % 2][:, :], func=AF.Square))

            @block.vector
            def _(vector):
                for m in range(MTOT):
                    vector.wait_ge(sC.h, bC1 + min(2 * m + 3, 2 * MTOT))
                    vector.reduce_max(out=hmx[m % 2][:, :], in_=htile[m % 2][:, :], axis=AX.X)
                    aa = 2 if m % 2 == 0 else 3
                    bb = 3 if m % 2 == 0 else 2
                    sV.inc(vector.tensor_tensor(out=smax[:, bb:bb + 1], in0=smax[:, aa:aa + 1],
                                                in1=hmx[(m - 1) % 2][:, :] if m >= 1 else smax[:, aa:aa + 1],
                                                op=ALU.max))
                vector.tensor_copy(out=pjk[:, :], in_=pjk2[:, :])
                m = MTOT
                aa = 2 if m % 2 == 0 else 3
                bb = 3 if m % 2 == 0 else 2
                sV.inc(vector.tensor_tensor(out=smax[:, bb:bb + 1], in0=smax[:, aa:aa + 1],
                                            in1=hmx[(m - 1) % 2][:, :], op=ALU.max))
            p4c_hcol = 3 if MTOT % 2 == 0 else 2

            @block.gpsimd
            def _(gpsimd):
                for m in range(MTOT):
                    g, rem = divmod(m, PPG)
                    jc, t = divmod(rem, NTT)
                    gpsimd.wait_ge(sV.h, bV1 + m + 1)
                    rr = 512 * g + 128 * jc
                    sG.dma(gpsimd.dma_start(out=h_s[rr:rr + 128, 512 * t:512 * (t + 1)],
                                            in_=htile[m % 2][:, :]))
                    if debug:
                        sG.dma(gpsimd.dma_start(out=dbg["h"][rr:rr + 128, 512 * t:512 * (t + 1)],
                                                in_=htile[m % 2][:, :]))
            p4_done_G = sG.n
            p4_done_V = sV.n
            p4_done_C = sC.n
            p4_done_M = sM.n

        @block.gpsimd
        def _(gpsimd):
            gpsimd.wait_ge(sV.h, p4_done_V)
            sG.dma(gpsimd.dma_start(out=row_s[:, :], in_=smax[:, p4c_hcol:p4c_hcol + 1]))
            gpsimd.wait_ge(sG.h, sG.n)
            sG.dma(gpsimd.dma_start(out=srow[:, :], in_=row_s[:, :]))
        p5_row = sG.n

        @block.vector
        def _(vector):
            vector.wait_ge(sG.h, p5_row)
            vector.memset(sone[:, :], 0.0)
            sV.inc(vector.reduce_max(out=sone[0:1, 0:1], in_=srow[:, :], axis=AX.X))
        p5_lmax = sV.n

        @block.gpsimd
        def _(gpsimd):
            gpsimd.wait_ge(sV.h, p5_lmax)
            sG.dma(gpsimd.dma_start(out=cc2_i[:, :], in_=sone[:, :]))
            gpsimd.wait_ge(sG.h, sG.n)
            cc.inc(gpsimd.collective_compute(
                "AllReduce", ALU.max, replica_groups=[list(range(N_CORES))],
                ins=[cc2_i.ap().opt()], outs=[cc2_o.ap().opt()]))
        cc2_done = cc.n

        p8p = ExitStack()
        E8p = p8p.enter_context
        sgx2 = E8p(nc.sbuf_tensor("p8gx", [1, 8], F32))
        shv = E8p(nc.sbuf_tensor("p8sh", [1, 1], F32))
        rshv = E8p(nc.sbuf_tensor("p8rs", [1, 1], F32))
        scrow2 = E8p(nc.sbuf_tensor("p8sr", [1, 8], F32))
        pjk8 = E8p(nc.sbuf_tensor("p8jk", [128, 1], F32))
        pjk82 = E8p(nc.sbuf_tensor("p8jk2", [128, 1], F32))
        pf32 = [E8p(nc.sbuf_tensor(f"p8w{i}", [128, D], F32)) for i in range(2)]
        pf8 = [E8p(nc.sbuf_tensor(f"p8q{i}", [128, D], F8)) for i in range(2)]
        pb16 = [E8p(nc.sbuf_tensor(f"p8b{i}", [128, D], BF16)) for i in range(2)]
        bGpre = sG.n

        @block.vector
        def _(vector):
            vector.wait_ge(sG.h, bGpre)
            sV.inc(vector.memset(pjk82[:, :], 0.0))

        @block.gpsimd
        def _(gpsimd):
            gpsimd.wait_ge(cc.h, cc2_done)
            gpsimd.wait_ge(sV.h, p5_lmax)
            sG.dma(gpsimd.dma_start(out=sgx2[:, :], in_=cc2_o[:, :]))
        p8a_g = sG.n

        @block.vector
        def _(vector):
            F = lambda: vector.tensor_copy(out=pjk8[:, :], in_=pjk82[:, :])
            vector.wait_ge(sG.h, p8a_g)
            vector.memset(scrow2[:, :], 0.0)
            vector.tensor_scalar(out=shv[:, :], in0=sgx2[0:1, 0:1], scalar1=1.0 / 448.0,
                                 scalar2=1e-6, op0=ALU.mult, op1=ALU.max)
            F()
            vector.reciprocal(out=rshv[:, :], in_=shv[:, :])
            F()
            vector.tensor_scalar(out=scrow2[0:1, 2:3], in0=rshv[:, :], scalar1=0.5, scalar2=None, op0=ALU.mult)
            vector.tensor_scalar(out=scrow2[0:1, 3:4], in0=shv[:, :], scalar1=spers[0:1, 0:1],
                                 scalar2=4.0, op0=ALU.mult, op1=ALU.mult)
            vector.tensor_copy(out=scrow2[0:1, 5:6], in_=spers[0:1, 1:2])
            F()
            sV.inc(vector.tensor_copy(out=pjk8[:, :], in_=pjk82[:, :]))
        p8a_v = sV.n

        @block.gpsimd
        def _(gpsimd):
            gpsimd.wait_ge(sV.h, p8a_v)
            sG.dma(gpsimd.dma_start(out=bc_s[:, :], in_=scrow2[:, :]))
            gpsimd.wait_ge(sG.h, sG.n)
            sG.dma(gpsimd.dma_start(out=scal[:, :], in_=bc_s.ap().broadcast_to([128, 8])))
        p8a_scal = sG.n

        def prod_src(j):
            if j < 64:
                return h_s[128 * j:128 * (j + 1), :]
            c = j - 64
            return wpj_d[128 * (c // 4):128 * (c // 4 + 1), 2048 * (c % 4):2048 * (c % 4 + 1)]

        def prod_dst(j):
            if j < 64:
                return h8_s[128 * j:128 * (j + 1), :]
            c = j - 64
            return w8pb_s[128 * (c // 4):128 * (c // 4 + 1), 2048 * (c % 4):2048 * (c % 4 + 1)]

        def prod_col(j):
            return 2 if j < 64 else 5
        bSp = sS.n
        NPROD = J // 128 + 64

        with ExitStack() as p6:
            E6 = p6.enter_context
            qh128 = [[E6(nc.sbuf_tensor(f"q128{b}_{hl}", [128, L], BF16)) for hl in range(2)] for b in range(2)]
            qh64 = [[E6(nc.sbuf_tensor(f"q64{b}_{hl}", [64, L], BF16)) for hl in range(2)] for b in range(2)]
            kh128 = [[E6(nc.sbuf_tensor(f"k128{b}_{hl}", [128, L], BF16)) for hl in range(2)] for b in range(2)]
            kh64 = [[E6(nc.sbuf_tensor(f"k64{b}_{hl}", [64, L], BF16)) for hl in range(2)] for b in range(2)]
            vh = [[E6(nc.sbuf_tensor(f"vh{b}_{hl}", [128, L], BF16)) for hl in range(2)] for b in range(2)]
            p32 = [E6(nc.sbuf_tensor(f"p6p{i}", [128, 512], F32)) for i in range(2)]
            phi = [E6(nc.sbuf_tensor(f"p6h{i}", [128, 512], BF16)) for i in range(2)]
            plo = [E6(nc.sbuf_tensor(f"p6l{i}", [128, 512], BF16)) for i in range(2)]
            rbc = E6(nc.sbuf_tensor("p6rb", [128, 512], F32))
            avn = E6(nc.sbuf_tensor("p6av", [128, 512], F32))
            ohhi = [E6(nc.sbuf_tensor(f"p6oh{i}", [128, 512], BF16)) for i in range(2)]
            ohlo = [E6(nc.sbuf_tensor(f"p6ol{i}", [128, 512], BF16)) for i in range(2)]
            pjk = E6(nc.sbuf_tensor("p6jk", [128, 1], F32))
            pjk2 = E6(nc.sbuf_tensor("p6jk2", [128, 1], F32))

            @block.vector
            def _(vector):
                sV.inc(vector.memset(pjk2[:, :], 0.0))

            bL6, bM6, bV6, bC6, bG6 = sL.n, sM.n, sV.n, sC.n, sG.n
            MPH = 4 * 17
            CPH = 4 * 16
            VPQ = 16 + 3
            VPH = 4 * VPQ
            GPH = 8

            @block.sync
            def _(sync):
                sync.wait_ge(sG.h, p4_done_G)
                sync.wait_ge(sM.h, p4_done_M)
                sync.wait_ge(sC.h, p4_done_C)
                for h in range(P6_NH):
                    b = h % 2
                    if h >= 1:
                        sync.wait_ge(sM.h, bM6 + (h - 1) * MPH + 1)
                    for hl in range(2):
                        sL.dma(sync.dma_start(out=qh128[b][hl][:, :], in_=q_s[hl, 128 * h:128 * (h + 1), :]))
                        sL.dma(sync.dma_start(out=qh64[b][hl][:, :], in_=q_s[hl, 2048 + 64 * h:2048 + 64 * (h + 1), :]))
                        sL.dma(sync.dma_start(out=kh128[b][hl][:, :], in_=k_s[hl, 128 * h:128 * (h + 1), :]))
                        sL.dma(sync.dma_start(out=kh64[b][hl][:, :], in_=k_s[hl, 2048 + 64 * h:2048 + 64 * (h + 1), :]))
                        for c4 in range(4):
                            sL.dma(sync.dma_start(
                                out=vh[b][hl][:, 512 * c4:512 * (c4 + 1)].rearrange("p (c f) -> p c f", f=128),
                                in_=v_s[hl, 512 * c4:512 * (c4 + 1), 128 * h:128 * (h + 1)].rearrange("(c p) f -> p c f", p=128)))

            @block.tensor
            def _(tensor):
                tensor.wait_ge(sV.h, p4_done_V)
                tensor.wait_ge(sC.h, p4_done_C)
                for h in range(P6_NH):
                    b = h % 2
                    tensor.wait_ge(sL.h, bL6 + 16 * 16 * (h + 1))
                    for qt in range(4):
                        qsl = slice(512 * qt, 512 * (qt + 1))
                        for kt in range(16):
                            ksl = slice(128 * kt, 128 * (kt + 1))
                            ps = pA if kt % 2 == 0 else pB
                            if kt >= 2:
                                tensor.wait_ge(sC.h, bC6 + h * CPH + qt * 16 + kt - 1)
                            tensor.matmul(ps[:, :], kh128[b][0][:, ksl], qh128[b][0][:, qsl], start=True, stop=False)
                            tensor.matmul(ps[:, :], kh128[b][0][:, ksl], qh128[b][1][:, qsl], start=False, stop=False)
                            tensor.matmul(ps[:, :], kh128[b][1][:, ksl], qh128[b][0][:, qsl], start=False, stop=False)
                            tensor.matmul(ps[:, :], kh64[b][0][:, ksl], qh64[b][0][:, qsl], start=False, stop=False)
                            tensor.matmul(ps[:, :], kh64[b][0][:, ksl], qh64[b][1][:, qsl], start=False, stop=False)
                            sM.inc(tensor.matmul(ps[:, :], kh64[b][1][:, ksl], qh64[b][0][:, qsl], start=False, stop=True))
                            if kt >= 1:
                                kp = kt - 1
                                kpsl = slice(128 * kp, 128 * (kp + 1))
                                tensor.wait_ge(sV.h, bV6 + h * VPH + qt * VPQ + kp + 1)
                                e = kp % 2
                                tensor.matmul(pBC[:, :], ones_cb[:, :], phi[e][:, :],
                                              start=(kp == 0), stop=False)
                                tensor.matmul(pBC[:, :], ones_cb[:, :], plo[e][:, :],
                                              start=False, stop=(kp == 15))
                                tensor.matmul(pC[:, :], vh[b][0][:, kpsl], phi[e][:, :],
                                              start=(kp == 0), stop=False)
                                tensor.matmul(pC[:, :], vh[b][1][:, kpsl], phi[e][:, :], start=False, stop=False)
                                tensor.matmul(pC[:, :], vh[b][0][:, kpsl], plo[e][:, :],
                                              start=False, stop=(kp == 15))
                        kp = 15
                        kpsl = slice(128 * kp, 128 * (kp + 1))
                        tensor.wait_ge(sV.h, bV6 + h * VPH + qt * VPQ + kp + 1)
                        e = kp % 2
                        tensor.matmul(pBC[:, :], ones_cb[:, :], phi[e][:, :], start=(kp == 0), stop=False)
                        tensor.matmul(pBC[:, :], ones_cb[:, :], plo[e][:, :], start=False, stop=(kp == 15))
                        tensor.matmul(pC[:, :], vh[b][0][:, kpsl], phi[e][:, :], start=(kp == 0), stop=False)
                        tensor.matmul(pC[:, :], vh[b][1][:, kpsl], phi[e][:, :], start=False, stop=False)
                        sM.inc(tensor.matmul(pC[:, :], vh[b][0][:, kpsl], plo[e][:, :],
                                             start=False, stop=(kp == 15)))

            @block.scalar
            def _(scalar):
                for h in range(P6_NH):
                    for qt in range(4):
                        for kt in range(16):
                            scalar.wait_ge(sM.h, bM6 + h * MPH + qt * 17 + kt + 1)
                            if kt >= 2:
                                scalar.wait_ge(sV.h, bV6 + h * VPH + qt * VPQ + kt - 1)
                            ps = pA if kt % 2 == 0 else pB
                            sC.inc(scalar.activation(out=p32[kt % 2][:, :], in_=ps[:, :],
                                                     func=AF.Exp, scale=INV_SQRT_DH))
                        s_ = h * 4 + qt
                        if s_ == 0:
                            scalar.wait_ge(sG.h, p8a_scal)
                            for j in (0, 1):
                                sS.dma(scalar.dma_start(out=pf32[j % 2][:, :], in_=prod_src(j)))
                        for j in (2 * s_, 2 * s_ + 1):
                            scalar.wait_ge(sS.h, bSp + 16 * (2 * s_ + 2))
                            if j >= 2:
                                scalar.wait_ge(sPV.h, j - 1)
                            sPC.inc(scalar.activation(out=pf8[j % 2][:, :], in_=pf32[j % 2][:, :],
                                                      func=AF.Copy,
                                                      scale=scal[:, prod_col(j):prod_col(j) + 1]))
                        if s_ + 1 < NPROD // 2:
                            for j in (2 * s_ + 2, 2 * s_ + 3):
                                scalar.wait_ge(sPC.h, j - 1)
                                sS.dma(scalar.dma_start(out=pf32[j % 2][:, :], in_=prod_src(j)))

            @block.vector
            def _(vector):
                for h in range(P6_NH):
                    for qt in range(4):
                        for kt in range(16):
                            vector.wait_ge(sC.h, bC6 + h * CPH + qt * 16 + kt + 1)
                            e = kt % 2
                            vector.tensor_copy(out=phi[e][:, :], in_=p32[e][:, :])
                            vector.tensor_copy(out=pjk[:, :], in_=pjk2[:, :])
                            sV.inc(vector.scalar_tensor_tensor(out=plo[e][:, :], in0=p32[e][:, :],
                                                               scalar=1.0, in1=phi[e][:, :],
                                                               op0=ALU.mult, op1=ALU.subtract))
                        vector.wait_ge(sM.h, bM6 + h * MPH + qt * 17 + 17)
                        sV.inc(vector.reciprocal(out=rbc[:, :], in_=pBC[:, :]))
                        vector.tensor_copy(out=pjk[:, :], in_=pjk2[:, :])
                        sV.inc(vector.tensor_tensor(out=avn[:, :], in0=pC[:, :], in1=rbc[:, :],
                                                    op=ALU.mult))
                        if h >= 1 or qt >= 2:
                            vector.wait_ge(sG.h, bG6 + 2 * 16 * (h * 4 + qt))
                        hq = h * 4 + qt
                        vector.tensor_copy(out=ohhi[hq % 2][:, :], in_=avn[:, :])
                        vector.tensor_copy(out=pjk[:, :], in_=pjk2[:, :])
                        sV.inc(vector.scalar_tensor_tensor(out=ohlo[hq % 2][:, :], in0=avn[:, :], scalar=1.0,
                                                           in1=ohhi[hq % 2][:, :], op0=ALU.mult,
                                                           op1=ALU.subtract))
                        for j in (2 * hq, 2 * hq + 1):
                            vector.wait_ge(sPC.h, j + 1)
                            if j >= 2:
                                vector.wait_ge(sPG.h, 16 * j)
                            sPV.inc(vector.tensor_copy(out=pb16[j % 2][:, :], in_=pf8[j % 2][:, :]))

            @block.gpsimd
            def _(gpsimd):
                for h in range(P6_NH):
                    for qt in range(4):
                        hq = h * 4 + qt
                        gpsimd.wait_ge(sV.h, bV6 + h * VPH + qt * VPQ + VPQ)
                        qsl = slice(512 * qt, 512 * (qt + 1))
                        sG.dma(gpsimd.dma_start(out=oh_s[0, 128 * h:128 * (h + 1), qsl], in_=ohhi[hq % 2][:, :]))
                        sG.dma(gpsimd.dma_start(out=oh_s[1, 128 * h:128 * (h + 1), qsl], in_=ohlo[hq % 2][:, :]))
                        for j in (2 * hq, 2 * hq + 1):
                            gpsimd.wait_ge(sPV.h, j + 1)
                            sPG.dma(gpsimd.dma_start(out=prod_dst(j), in_=pb16[j % 2][:, :]))
            p6_done_G = sG.n
            p6_done_M = sM.n
            p6_done_V = sV.n
            p6_done_C = sC.n
        p8p.close()

        with ExitStack() as p7:
            E7 = p7.enter_context
            oht = [[E7(nc.sbuf_tensor(f"p7o{hl}_{i}", [128, 512], BF16)) for i in range(16)] for hl in range(2)]
            worh = [E7(nc.sbuf_tensor(f"p7wh{i}", [128, 512], BF16)) for i in range(2)]
            worl = [E7(nc.sbuf_tensor(f"p7wl{i}", [128, 512], BF16)) for i in range(2)]
            xch = [E7(nc.sbuf_tensor(f"p7x{i}", [128, D], F32)) for i in range(4)]
            ost = [E7(nc.sbuf_tensor(f"p7s{i}", [128, 512], F32)) for i in range(4)]
            bL7, bM7, bV7, bY7 = sL.n, sM.n, sV.n, sY.n
            wo_hi_d, wo_lo_d = wsplit["wo"]
            LPT = 164 * 16

            @block.sync
            def _(sync):
                sync.wait_ge(sG.h, p6_done_G)
                sync.wait_ge(sM.h, p6_done_M)
                sync.wait_ge(sC.h, p6_done_C)
                sync.wait_ge(sV.h, p6_done_V)
                sync.wait_ge(sPG.h, 16 * NPROD)
                for tau in range(NTT):
                    tok = tau * TT
                    if tau >= 1:
                        sync.wait_ge(sV.h, bV7 + 4 * tau)
                    for hd in range(16):
                        for hl in range(2):
                            sL.dma(sync.dma_start(out=oht[hl][hd][:, :],
                                                  in_=oh_s[hl, 128 * hd:128 * (hd + 1), tok:tok + TT]))
                    for tc in range(4):
                        sL.dma(sync.dma_start(out=xch[tc][:, :],
                                              in_=x_d[tok + 128 * tc:tok + 128 * (tc + 1), :]))
                    for dt in range(4):
                        for hd in range(16):
                            gk = dt * 16 + hd
                            if gk >= 2:
                                g2, k2 = divmod(gk - 2, 16)
                                sync.wait_ge(sM.h, bM7 + tau * 64 + g2 * 16 + k2 + 1)
                            else:
                                sync.wait_ge(sM.h, bM7 + tau * 64)
                            sL.dma(sync.dma_start(out=worh[gk % 2][:, :],
                                                  in_=wo_hi_d[128 * hd:128 * (hd + 1), 512 * dt:512 * (dt + 1)]))
                            sL.dma(sync.dma_start(out=worl[gk % 2][:, :],
                                                  in_=wo_lo_d[128 * hd:128 * (hd + 1), 512 * dt:512 * (dt + 1)]))

            @block.tensor
            def _(tensor):
                tensor.wait_ge(sV.h, p6_done_V)
                tensor.wait_ge(sC.h, p6_done_C)
                for tau in range(NTT):
                    tensor.wait_ge(sL.h, bL7 + LPT * tau + 36 * 16)
                    for dt in range(4):
                        gdt = tau * 4 + dt
                        if gdt >= 1:
                            tensor.wait_ge(sV.h, bV7 + gdt)
                        for hd in range(16):
                            gk = dt * 16 + hd
                            tensor.wait_ge(sL.h, bL7 + LPT * tau + 36 * 16 + 32 * (gk + 1))
                            e = gk % 2
                            last = None
                            for tc in range(4):
                                ohh = oht[0][hd][:, 128 * tc:128 * (tc + 1)]
                                ohl = oht[1][hd][:, 128 * tc:128 * (tc + 1)]
                                tensor.matmul(PS4[tc][:, :], ohh, worh[e][:, :], start=(hd == 0), stop=False)
                                tensor.matmul(PS4[tc][:, :], ohl, worh[e][:, :], start=False, stop=False)
                                last = tensor.matmul(PS4[tc][:, :], ohh, worl[e][:, :],
                                                     start=False, stop=(hd == 15))
                            sM.inc(last)

            @block.vector
            def _(vector):
                for tau in range(NTT):
                    for dt in range(4):
                        gdt = tau * 4 + dt
                        vector.wait_ge(sM.h, bM7 + (gdt + 1) * 16)
                        if gdt >= 1:
                            vector.wait_ge(sY.h, bY7 + 64 * gdt)
                        for tc in range(4):
                            vector.tensor_tensor(out=ost[tc][:, :], in0=PS4[tc][:, :],
                                                 in1=xch[tc][:, 512 * dt:512 * (dt + 1)], op=ALU.add)
                        sV.inc(vector.tensor_copy(out=ost[0][0:1, 0:1], in_=ost[0][0:1, 0:1]))

            @block.gpsimd
            def _(gpsimd):
                for tau in range(NTT):
                    for dt in range(4):
                        gdt = tau * 4 + dt
                        gpsimd.wait_ge(sV.h, bV7 + gdt + 1)
                        for tc in range(4):
                            r0 = tau * TT + 128 * tc
                            sY.dma(gpsimd.dma_start(out=y_d[r0:r0 + 128, 512 * dt:512 * (dt + 1)],
                                                    in_=ost[tc][:, :]))
            p7_Y = sY.n
            p7_M = sM.n
            p7_V = sV.n

        if stage <= 5:
            @block.gpsimd
            def _(gpsimd):
                gpsimd.wait_ge(sY.h, sY.n)
            return nc

        with ExitStack() as p8:
            E8 = p8.enter_context
            w8pT = [[E8(nc.sbuf_tensor(f"p8t{ss}_{i}", [128, 512], BF16, is_transpose_dest=True)) for i in range(32)] for ss in range(2)]
            h8r = [E8(nc.sbuf_tensor(f"p8h{i}", [128, 512], BF16)) for i in range(8)]
            acc = [E8(nc.sbuf_tensor(f"p8a{i}", [128, 512], F32)) for i in range(16)]
            stg8 = [E8(nc.sbuf_tensor(f"p8o{i}", [128, 512], F32)) for i in range(4)]
            tmp8 = [E8(nc.sbuf_tensor(f"p8tm{i}", [128, 512], F32)) for i in range(2)]

            pjk9 = E8(nc.sbuf_tensor("p8jk9", [128, 1], F32))
            pjk92 = E8(nc.sbuf_tensor("p8jk92", [128, 1], F32))

            @block.vector
            def _(vector):
                vector.wait_ge(sV.h, p7_V)
                sV.inc(vector.memset(pjk92[:, :], 0.0))

            bL8, bM8, bV8, bY8 = sL.n, sM.n, sV.n, sY.n
            sSp = [sS, sS2]
            bS8p = [sS.n, sS2.n]
            HPT = 32 * 16

            @block.scalar
            def _(scalar):
                scalar.wait_ge(sG.h, p6_done_G)
                scalar.wait_ge(sY.h, p7_Y)
                scalar.wait_ge(sM.h, p7_M)
                scalar.wait_ge(sV.h, p7_V)
                for dt in range(4):
                    for jh in range(2):
                        djh = dt * 2 + jh
                        if djh == 0:
                            scalar.wait_ge(sPG.h, 16 * NPROD)
                        elif djh >= 2:
                            scalar.wait_ge(sM.h, bM8 + (djh - 1) * 128)
                        for jc in range(32):
                            jg = jh * 32 + jc
                            sSp[djh % 2].dma(scalar.dma_start(out=w8pT[djh % 2][jc][:, :],
                                                              in_=w8pb_s[512 * dt:512 * (dt + 1), 128 * jg:128 * (jg + 1)],
                                                              transpose=True))

            @block.sync
            def _(sync):
                sync.wait_ge(sG.h, p6_done_G)
                sync.wait_ge(sY.h, p7_Y)
                sync.wait_ge(sM.h, p7_M)
                sync.wait_ge(sV.h, p7_V)
                sync.wait_ge(sPG.h, 16 * NPROD)
                ldx = 0
                for dt in range(4):
                    for jh in range(2):
                        for tau in range(NTT):
                            for jc in range(32):
                                if ldx >= 8:
                                    g2 = ldx - 8
                                    sync.wait_ge(sM.h, bM8 + g2 + 1)
                                jg = jh * 32 + jc
                                sL.dma(sync.dma_start(out=h8r[ldx % 8][:, :],
                                                      in_=h8_s[128 * jg:128 * (jg + 1),
                                                               512 * tau:512 * (tau + 1)]))
                                ldx += 1


            @block.tensor
            def _(tensor):
                tensor.wait_ge(sV.h, p7_V)
                gidx = 0
                ldx = 0
                for dt in range(4):
                    for jh in range(2):
                        djh = dt * 2 + jh
                        tensor.wait_ge(sSp[djh % 2].h, bS8p[djh % 2] + 32 * 16 * (djh // 2 + 1))
                        for tau in range(NTT):
                            if gidx >= 1:
                                tensor.wait_ge(sV.h, bV8 + gidx)
                            for jc in range(32):
                                tensor.wait_ge(sL.h, bL8 + 16 * min(ldx + 8, 1024))
                                last = None
                                for tc in range(4):
                                    last = tensor.matmul(PS4[tc][:, :],
                                                         h8r[ldx % 8][:, 128 * tc:128 * (tc + 1)],
                                                         w8pT[djh % 2][jc][:, :],
                                                         start=(jc == 0), stop=(jc == 31))
                                sM.inc(last)
                                ldx += 1
                            gidx += 1

            @block.vector
            def _(vector):
                gidx = 0
                for dt in range(4):
                    for jh in range(2):
                        for tau in range(NTT):
                            vector.wait_ge(sM.h, bM8 + (gidx + 1) * 32)
                            if jh == 0:
                                for tc in range(4):
                                    vector.tensor_copy(out=acc[tau * 4 + tc][:, :], in_=PS4[tc][:, :])
                            else:
                                g1 = dt * 4 + tau
                                if g1 >= 1:
                                    vector.wait_ge(sY.h, bY8 + 64 * g1)
                                for pr in range(2):
                                    t0, t1 = 2 * pr, 2 * pr + 1
                                    vector.tensor_tensor(out=tmp8[0][:, :], in0=PS4[t0][:, :],
                                                         in1=acc[tau * 4 + t0][:, :], op=ALU.add)
                                    vector.tensor_tensor(out=tmp8[1][:, :], in0=PS4[t1][:, :],
                                                         in1=acc[tau * 4 + t1][:, :], op=ALU.add)
                                    vector.tensor_scalar(out=stg8[t0][:, :], in0=tmp8[0][:, :],
                                                         scalar1=scal[:, 3:4], scalar2=None, op0=ALU.mult)
                                    vector.tensor_scalar(out=stg8[t1][:, :], in0=tmp8[1][:, :],
                                                         scalar1=scal[:, 3:4], scalar2=None, op0=ALU.mult)
                            sV.inc(vector.tensor_copy(out=pjk9[:, :], in_=pjk92[:, :]))
                            gidx += 1

            @block.gpsimd
            def _(gpsimd):
                gpsimd.wait_ge(sY.h, p7_Y)
                gidx = 0
                for dt in range(4):
                    for jh in range(2):
                        for tau in range(NTT):
                            gidx += 1
                            if jh == 0:
                                continue
                            gpsimd.wait_ge(sV.h, bV8 + gidx)
                            for tc in range(4):
                                r0 = tau * TT + 128 * tc
                                sY.dma(gpsimd.dma_start(out=y_d[r0:r0 + 128, 512 * dt:512 * (dt + 1)],
                                                        in_=stg8[tc][:, :], accum_op=ALU.add))

        @block.gpsimd
        def _(gpsimd):
            gpsimd.wait_ge(sY.h, sY.n)
            if debug:
                for i in range(J // 128):
                    sY.dma(gpsimd.dma_start(out=dbg["h8"][128 * i:128 * (i + 1), :],
                                            in_=h8_s[128 * i:128 * (i + 1), :]))
                for i in range(D // 128):
                    sY.dma(gpsimd.dma_start(out=dbg["w8p"][128 * i:128 * (i + 1), :],
                                            in_=w8pb_s[128 * i:128 * (i + 1), :]))
                sY.dma(gpsimd.dma_start(out=dbg["sc2"][:, :], in_=scal[0:1, 0:8]))
                for hl in range(2):
                    for r in range(NTC):
                        sY.dma(gpsimd.dma_start(out=dbg["oh"][hl, 128 * r:128 * (r + 1), :],
                                                in_=oh_s[hl, 128 * r:128 * (r + 1), :]))
                    for r in range(NH * (DH + DHR) // 128):
                        sY.dma(gpsimd.dma_start(out=dbg["q"][hl, 128 * r:128 * (r + 1), :],
                                                in_=q_s[hl, 128 * r:128 * (r + 1), :]))
                        sY.dma(gpsimd.dma_start(out=dbg["k"][hl, 128 * r:128 * (r + 1), :],
                                                in_=k_s[hl, 128 * r:128 * (r + 1), :]))
                    for r in range(L // 128):
                        sY.dma(gpsimd.dma_start(out=dbg["v"][hl, 128 * r:128 * (r + 1), :],
                                                in_=v_s[hl, 128 * r:128 * (r + 1), :]))
                gpsimd.wait_ge(sY.h, sY.n)
        return nc

        if stage <= 4:
            @block.gpsimd
            def _(gpsimd):
                gpsimd.wait_ge(sG.h, p6_done_G)
                for i in range(NTC):
                    sY.dma(gpsimd.dma_start(out=y_d[128 * i:128 * (i + 1), :],
                                            in_=xnhi[i][:, 0:2048]))
                if debug:
                    for hl in range(2):
                        for r in range(NTC):
                            sY.dma(gpsimd.dma_start(out=dbg["oh"][hl, 128 * r:128 * (r + 1), :],
                                                    in_=oh_s[hl, 128 * r:128 * (r + 1), :]))
                gpsimd.wait_ge(sY.h, sY.n)
            return nc

        if stage <= 3:
            @block.gpsimd
            def _(gpsimd):
                gpsimd.wait_ge(sG.h, p4_done_G)
                for i in range(NTC):
                    sY.dma(gpsimd.dma_start(out=y_d[128 * i:128 * (i + 1), :],
                                            in_=xnhi[i][:, 0:2048]))
                gpsimd.wait_ge(sY.h, sY.n)
            return nc

        if stage <= 2:
            @block.gpsimd
            def _(gpsimd):
                gpsimd.wait_ge(sG.h, p3_done_G)
                for i in range(NTC):
                    sY.dma(gpsimd.dma_start(out=y_d[128 * i:128 * (i + 1), :],
                                            in_=xnhi[i][:, 0:2048]))
                if debug:
                    for hl in range(2):
                        for r in range(NH * (DH + DHR) // 128):
                            sY.dma(gpsimd.dma_start(out=dbg["q"][hl, 128 * r:128 * (r + 1), :],
                                                    in_=q_s[hl, 128 * r:128 * (r + 1), :]))
                            sY.dma(gpsimd.dma_start(out=dbg["k"][hl, 128 * r:128 * (r + 1), :],
                                                    in_=k_s[hl, 128 * r:128 * (r + 1), :]))
                        for r in range(NTC):
                            sY.dma(gpsimd.dma_start(out=dbg["v"][hl, 128 * r:128 * (r + 1), :],
                                                    in_=v_s[hl, 128 * r:128 * (r + 1), :]))
                    gpsimd.wait_ge(cc.h, cc1_done)
                    sG.dma(gpsimd.dma_start(out=sone[:, :], in_=cc1_o[:, :]))
                    gpsimd.wait_ge(sG.h, sG.n)
                    sY.dma(gpsimd.dma_start(out=dbg["sc"][:, :], in_=sone[:, :]))
                gpsimd.wait_ge(sY.h, sY.n)
            return nc

        raise NotImplementedError("stages beyond 1 not yet implemented")

    return nc



def _rope_rt():
    rt = np.zeros((128, 128), np.float32)
    for blk in (0, 64):
        for i in range(32):
            rt[blk + i + 32, blk + i] = -1.0
            rt[blk + i, blk + i + 32] = 1.0
    return rt


def _rope_tables():
    inv = (1.0 / (10000.0 ** (np.arange(0, DHR, 2, dtype=np.float32) / DHR))).astype(np.float32)
    ang = (np.arange(L, dtype=np.float32)[None, :] * inv[:, None]).astype(np.float32)
    return np.cos(ang).astype(np.float32), np.sin(ang).astype(np.float32)


def make_inputs(L0_attn_x, L0_attn_W_D_Q, L0_attn_W_U_Q, L0_attn_W_Q_R, L0_attn_W_D_KV,
                L0_attn_W_U_K, L0_attn_W_K_R, L0_attn_W_U_V, L0_attn_W_O,
                L0_ffn_w_fc, L0_ffn_w_proj):
    cos_t, sin_t = _rope_tables()
    consts = dict(
        w_dq=np.ascontiguousarray(L0_attn_W_D_Q, np.float32),
        w_uq=np.ascontiguousarray(L0_attn_W_U_Q, np.float32),
        w_qr=np.ascontiguousarray(L0_attn_W_Q_R, np.float32),
        w_dkv=np.ascontiguousarray(L0_attn_W_D_KV, np.float32),
        w_uk=np.ascontiguousarray(L0_attn_W_U_K, np.float32),
        w_kr=np.ascontiguousarray(L0_attn_W_K_R, np.float32),
        w_uv=np.ascontiguousarray(L0_attn_W_U_V, np.float32),
        w_o=np.ascontiguousarray(L0_attn_W_O, np.float32),
        w_fc=np.ascontiguousarray(L0_ffn_w_fc, np.float32),
        w_pj=np.ascontiguousarray(L0_ffn_w_proj, np.float32),
        cos_t=cos_t, sin_t=sin_t,
        ones_r=np.ones((1, 128), np.float32),
        ones_cb=np.ones((128, 128), ml_dtypes.bfloat16),
        rope_rt=_rope_rt(),
    )
    x = np.asarray(L0_attn_x, dtype=np.float32)
    return [dict(consts, x=np.ascontiguousarray(x[c])) for c in range(N_CORES)]


def kernel(**inputs):
    in_maps = make_inputs(**inputs)
    nc = build_nc()
    res = run_bass_kernel_spmd(nc, in_maps, core_ids=list(range(N_CORES)))
    return np.stack([res.results[c]["y"] for c in range(N_CORES)], axis=0)


# version 14
# speedup vs baseline: 1.2840x; 1.2840x over previous
import math
import os
import numpy as np
import ml_dtypes
import concourse.bass as bass
import concourse.mybir as mybir
from concourse.bass_utils import run_bass_kernel_spmd
from contextlib import ExitStack

F32 = mybir.dt.float32
BF16 = mybir.dt.bfloat16
F8 = mybir.dt.float8e4
AF = mybir.ActivationFunctionType
ALU = mybir.AluOpType
AX = mybir.AxisListType

N_CORES = 8
L = 2048
D = 2048
NH, DH, DHR = 16, 128, 64
DC, DCQ = 512, 1536
J = 8192
TT = 512
NTT = L // TT
NTC = L // 128
ND = D // 128
INV_SQRT_DH = 1.0 / math.sqrt(DH)

STAGE_FULL = 99
P6_NH = int(os.environ.get("P6_NH", str(NH)))


class Sem:
    def __init__(self, h):
        self.h = h
        self.n = 0

    def inc(self, ins, k=1):
        ins.then_inc(self.h, k)
        self.n += k
        return ins

    def dma(self, ins):
        return self.inc(ins, 16)


def build_nc(stage=STAGE_FULL, debug=False):
    nc = bass.Bass("TRN2", target_bir_lowering=False, debug=False,
                   detect_race_conditions=False)

    x_d = nc.dram_tensor("x", [L, D], F32, kind="ExternalInput")
    wdq_d = nc.dram_tensor("w_dq", [D, DCQ], F32, kind="ExternalInput")
    wuq_d = nc.dram_tensor("w_uq", [DCQ, NH * DH], F32, kind="ExternalInput")
    wqr_d = nc.dram_tensor("w_qr", [DCQ, NH * DHR], F32, kind="ExternalInput")
    wdkv_d = nc.dram_tensor("w_dkv", [D, DC], F32, kind="ExternalInput")
    wuk_d = nc.dram_tensor("w_uk", [DC, NH * DH], F32, kind="ExternalInput")
    wkr_d = nc.dram_tensor("w_kr", [D, NH * DHR], F32, kind="ExternalInput")
    wuv_d = nc.dram_tensor("w_uv", [DC, NH * DH], F32, kind="ExternalInput")
    wo_d = nc.dram_tensor("w_o", [NH * DH, D], F32, kind="ExternalInput")
    wfc_d = nc.dram_tensor("w_fc", [J, D], F32, kind="ExternalInput")
    wpj_d = nc.dram_tensor("w_pj", [D, J], F32, kind="ExternalInput")
    cos_d = nc.dram_tensor("cos_t", [32, L], F32, kind="ExternalInput")
    sin_d = nc.dram_tensor("sin_t", [32, L], F32, kind="ExternalInput")
    ones_r_d = nc.dram_tensor("ones_r", [1, 128], F32, kind="ExternalInput")
    ones_cb_d = nc.dram_tensor("ones_cb", [128, 128], BF16, kind="ExternalInput")
    rope_rt_d = nc.dram_tensor("rope_rt", [128, 128], F32, kind="ExternalInput")
    y_d = nc.dram_tensor("y", [L, D], F32, kind="ExternalOutput")
    dbg = {}
    if debug:
        dbg["xn"] = nc.dram_tensor("dbg_xn", [L, D], F32, kind="ExternalOutput")
        dbg["sc"] = nc.dram_tensor("dbg_sc", [1, 8], F32, kind="ExternalOutput")
        dbg["q"] = nc.dram_tensor("dbg_q", [2, NH * (DH + DHR), L], BF16, kind="ExternalOutput")
        dbg["k"] = nc.dram_tensor("dbg_k", [2, NH * (DH + DHR), L], BF16, kind="ExternalOutput")
        dbg["v"] = nc.dram_tensor("dbg_v", [2, L, NH * DH], BF16, kind="ExternalOutput")
        dbg["h"] = nc.dram_tensor("dbg_h", [J, L], F32, kind="ExternalOutput")
        dbg["oh"] = nc.dram_tensor("dbg_oh", [2, NH * DH, L], BF16, kind="ExternalOutput")
        dbg["h8"] = nc.dram_tensor("dbg_h8", [J, L], BF16, kind="ExternalOutput")
        dbg["w8p"] = nc.dram_tensor("dbg_w8p", [D, J], BF16, kind="ExternalOutput")
        dbg["sc2"] = nc.dram_tensor("dbg_sc2", [1, 8], F32, kind="ExternalOutput")

    xnhi_s = nc.dram_tensor("xnhi_s", [L, D], BF16)
    xnlo_s = nc.dram_tensor("xnlo_s", [L, D], BF16)
    wsplit = {}
    for nm, dd in (("wdq", (D, DCQ)), ("wuq", (DCQ, NH * DH)), ("wqr", (DCQ, NH * DHR)),
                   ("wdkv", (D, DC)), ("wuk", (DC, NH * DH)), ("wkr", (D, NH * DHR)),
                   ("wuv", (DC, NH * DH)), ("wo", (NH * DH, D))):
        wsplit[nm] = (nc.dram_tensor(nm + "_hi", list(dd), BF16),
                      nc.dram_tensor(nm + "_lo", list(dd), BF16))
    q_s = nc.dram_tensor("q_s", [2, NH * (DH + DHR), L], BF16)
    k_s = nc.dram_tensor("k_s", [2, NH * (DH + DHR), L], BF16)
    v_s = nc.dram_tensor("v_s", [2, L, NH * DH], BF16)
    oh_s = nc.dram_tensor("oh_s", [2, NH * DH, L], BF16)
    w8b_s = nc.dram_tensor("w8b_s", [J, D], BF16)
    w8pb_s = nc.dram_tensor("w8pb_s", [D, J], BF16)
    h_s = nc.dram_tensor("h_s", [J, L], F32)
    h8_s = nc.dram_tensor("h8_s", [J, L], BF16)
    cc1_i = nc.dram_tensor("cc1_i", [1, 8], F32)
    cc1_o = nc.dram_tensor("cc1_o", [1, 8], F32)
    cc2_i = nc.dram_tensor("cc2_i", [1, 8], F32)
    cc2_o = nc.dram_tensor("cc2_o", [1, 8], F32)
    row_s = nc.dram_tensor("row_s", [1, 128], F32)
    bc_s = nc.dram_tensor("bc_s", [1, 8], F32)

    with ExitStack() as es:
        E = es.enter_context
        sL = Sem(E(nc.semaphore("sL")))
        sS = Sem(E(nc.semaphore("sS")))
        sG = Sem(E(nc.semaphore("sG")))
        sM = Sem(E(nc.semaphore("sM")))
        sV = Sem(E(nc.semaphore("sV")))
        sC = Sem(E(nc.semaphore("sC")))
        sY = Sem(E(nc.semaphore("sY")))
        cc = Sem(E(nc.semaphore("cc")))
        sPC = Sem(E(nc.semaphore("sPC")))
        sPV = Sem(E(nc.semaphore("sPV")))
        sPG = Sem(E(nc.semaphore("sPG")))
        sS2 = Sem(E(nc.semaphore("sS2")))
        sHp = [Sem(E(nc.semaphore(f"sH{i}"))) for i in range(4)]
        block = E(nc.Block())

        xnhi = [E(nc.sbuf_tensor(f"xnhi{i}", [128, L], BF16, is_transpose_dest=True)) for i in range(ND)]
        ones_r = E(nc.sbuf_tensor("sb_ones_r", [1, 128], F32))
        ones_cb = E(nc.sbuf_tensor("sb_ones_cb", [128, 128], BF16))
        rope_rt = E(nc.sbuf_tensor("sb_rope_rt", [128, 128], F32))
        scal = E(nc.sbuf_tensor("scal", [128, 8], F32))
        smax = E(nc.sbuf_tensor("smax", [128, 8], F32))
        srow = E(nc.sbuf_tensor("srow", [1, 128], F32))
        sone = E(nc.sbuf_tensor("sone", [1, 8], F32))
        spers = E(nc.sbuf_tensor("spers", [1, 2], F32))
        pA = E(nc.psum_tensor("pA", [128, 512], F32))
        pB = E(nc.psum_tensor("pB", [128, 512], F32))
        pC = E(nc.psum_tensor("pC", [128, 512], F32))
        pE = E(nc.psum_tensor("pE", [128, 512], F32))
        pD = E(nc.psum_tensor("pD", [1, 512], F32))
        pBC = E(nc.psum_tensor("pBC", [128, 512], F32))
        p3_scope = ExitStack()
        xnlo = [p3_scope.enter_context(nc.sbuf_tensor(f"xnlo{i}", [128, L], BF16, is_transpose_dest=True)) for i in range(ND)]
        cos128 = p3_scope.enter_context(nc.sbuf_tensor("cos128", [128, L], F32))
        sin128 = p3_scope.enter_context(nc.sbuf_tensor("sin128", [128, L], F32))

        @block.sync
        def _(sync):
            for r in range(4):
                sL.dma(sync.dma_start(out=cos128[32 * r:32 * (r + 1), :], in_=cos_d[:, :]))
                sL.dma(sync.dma_start(out=sin128[32 * r:32 * (r + 1), :], in_=sin_d[:, :]))
            sL.dma(sync.dma_start(out=ones_r[:, :], in_=ones_r_d[:, :]))
            sL.dma(sync.dma_start(out=ones_cb[:, :], in_=ones_cb_d[:, :]))
            sL.dma(sync.dma_start(out=rope_rt[:, :], in_=rope_rt_d[:, :]))
        p0_consts = sL.n

        @block.vector
        def _(vector):
            vector.memset(smax[:, :], 0.0)
            sV.inc(vector.memset(scal[:, :], 0.0))
        p0_vec = sV.n

        with ExitStack() as p1:
            E1 = p1.enter_context
            xt = [E1(nc.sbuf_tensor(f"p1x{i}", [128, D], F32)) for i in range(2)]
            sq = E1(nc.sbuf_tensor("p1sq", [128, D], F32))
            xnt = [E1(nc.sbuf_tensor(f"p1xn{i}", [128, D], F32)) for i in range(2)]
            hi_t = [E1(nc.sbuf_tensor(f"p1hi{i}", [128, D], BF16)) for i in range(2)]
            lo_t = [E1(nc.sbuf_tensor(f"p1lo{i}", [128, D], BF16)) for i in range(2)]
            st1 = E1(nc.sbuf_tensor("p1s1", [128, 1], F32))
            st2 = E1(nc.sbuf_tensor("p1s2", [128, 1], F32))
            st16 = E1(nc.sbuf_tensor("p1s16", [128, 16], F32))
            st16b = E1(nc.sbuf_tensor("p1s16b", [128, 16], F32))
            mm_ = E1(nc.sbuf_tensor("p1m", [128, 1], F32))
            ms_ = E1(nc.sbuf_tensor("p1ms", [128, 1], F32))
            var_ = E1(nc.sbuf_tensor("p1v", [128, 1], F32))
            var2_ = E1(nc.sbuf_tensor("p1v2", [128, 1], F32))
            var3_ = E1(nc.sbuf_tensor("p1v3", [128, 1], F32))
            junk_ = E1(nc.sbuf_tensor("p1jk", [128, 1], F32))
            junk2_ = E1(nc.sbuf_tensor("p1jk2", [128, 1], F32))
            sd_ = E1(nc.sbuf_tensor("p1sd", [128, 1], F32))
            rstd_ = E1(nc.sbuf_tensor("p1rs", [128, 1], F32))
            tmx = E1(nc.sbuf_tensor("p1mx", [128, 1], F32))

            bL, bS, bV, bC = sL.n, sS.n, sV.n, sC.n
            VPI = 6
            SPI = 3 if debug else 2

            @block.sync
            def _(sync):
                for i in range(NTC):
                    if i >= 2:
                        sync.wait_ge(sV.h, bV + (i - 1) * VPI)
                    sL.dma(sync.dma_start(out=xt[i % 2][:, :], in_=x_d[128 * i:128 * (i + 1), :]))

            @block.scalar
            def _(scalar):
                for i in range(NTC):
                    scalar.wait_ge(sV.h, bV + i * VPI + 1)
                    sC.inc(scalar.activation(out=sq[:, :], in_=xnt[i % 2][:, :], func=AF.Square))
                    scalar.wait_ge(sV.h, bV + i * VPI + 2)
                    sC.inc(scalar.activation(out=sd_[:, :], in_=var3_[:, :], func=AF.Sqrt))
                    scalar.wait_ge(sV.h, bV + i * VPI + 6)
                    sS.dma(scalar.dma_start(out=xnhi_s[128 * i:128 * (i + 1), :], in_=hi_t[i % 2][:, :]))
                    sS.dma(scalar.dma_start(out=xnlo_s[128 * i:128 * (i + 1), :], in_=lo_t[i % 2][:, :]))
                    if debug:
                        sS.dma(scalar.dma_start(out=dbg["xn"][128 * i:128 * (i + 1), :], in_=xnt[i % 2][:, :]))

            @block.vector
            def _(vector):
                vector.memset(junk2_[:, :], 0.0)
                for i in range(NTC):
                    vector.wait_ge(sL.h, bL + 16 * (i + 1))
                    vector.tensor_reduce(out=st16[:, :],
                                         in_=xt[i % 2][:, :].rearrange("p (a b) -> p a b", b=128),
                                         axis=AX.X, op=ALU.add)
                    vector.tensor_copy(out=junk_[:, :], in_=junk2_[:, :])
                    vector.reduce_sum(out=st1[:, :], in_=st16[:, :], axis=AX.X)
                    vector.tensor_copy(out=junk_[:, :], in_=junk2_[:, :])
                    vector.tensor_scalar(out=mm_[:, :], in0=st1[:, :], scalar1=1.0 / D, scalar2=None, op0=ALU.mult)
                    vector.tensor_copy(out=junk_[:, :], in_=junk2_[:, :])
                    if i >= 2:
                        vector.wait_ge(sS.h, bS + (i - 1) * SPI * 16)
                    sV.inc(vector.tensor_scalar(out=xnt[i % 2][:, :], in0=xt[i % 2][:, :], scalar1=mm_[:, :],
                                                scalar2=None, op0=ALU.subtract))
                    vector.wait_ge(sC.h, bC + i * 2 + 1)
                    vector.tensor_reduce(out=st16b[:, :],
                                         in_=sq[:, :].rearrange("p (a b) -> p a b", b=128),
                                         axis=AX.X, op=ALU.add)
                    vector.tensor_copy(out=junk_[:, :], in_=junk2_[:, :])
                    vector.reduce_sum(out=st2[:, :], in_=st16b[:, :], axis=AX.X)
                    vector.tensor_copy(out=junk_[:, :], in_=junk2_[:, :])
                    sV.inc(vector.tensor_scalar(out=var3_[:, :], in0=st2[:, :], scalar1=1.0 / D,
                                                scalar2=1e-5, op0=ALU.mult, op1=ALU.add))
                    vector.wait_ge(sC.h, bC + i * 2 + 2)
                    vector.reciprocal(out=rstd_[:, :], in_=sd_[:, :])
                    vector.tensor_copy(out=junk_[:, :], in_=junk2_[:, :])
                    vector.tensor_tensor(out=var_[:, :], in0=sd_[:, :], in1=rstd_[:, :], op=ALU.mult)
                    vector.tensor_copy(out=junk_[:, :], in_=junk2_[:, :])
                    vector.tensor_scalar(out=var2_[:, :], in0=var_[:, :], scalar1=-1.0,
                                         scalar2=2.0, op0=ALU.mult, op1=ALU.add)
                    vector.tensor_copy(out=junk_[:, :], in_=junk2_[:, :])
                    sV.inc(vector.tensor_tensor(out=mm_[:, :], in0=rstd_[:, :], in1=var2_[:, :],
                                                op=ALU.mult))
                    vector.tensor_copy(out=junk_[:, :], in_=junk2_[:, :])
                    vector.tensor_scalar(out=xnt[i % 2][:, :], in0=xnt[i % 2][:, :], scalar1=mm_[:, :],
                                         scalar2=None, op0=ALU.mult)
                    vector.tensor_copy(out=junk_[:, :], in_=junk2_[:, :])
                    vector.reduce_max(out=tmx[:, :], in_=xnt[i % 2][:, :], axis=AX.X, apply_absolute_value=True)
                    a, b = (0, 1) if i % 2 == 0 else (1, 0)
                    sV.inc(vector.tensor_copy(out=hi_t[i % 2][:, :], in_=xnt[i % 2][:, :]))
                    sV.inc(vector.tensor_tensor(out=smax[:, b:b + 1], in0=smax[:, a:a + 1],
                                                in1=tmx[:, :], op=ALU.max))
                    sV.inc(vector.scalar_tensor_tensor(out=lo_t[i % 2][:, :], in0=xnt[i % 2][:, :], scalar=1.0,
                                                       in1=hi_t[i % 2][:, :], op0=ALU.mult, op1=ALU.subtract))

            p1_spills = sS.n
            p1_vec = sV.n

            @block.sync
            def _(sync):
                sync.wait_ge(sS.h, p1_spills)
                for dc_ in range(ND):
                    sL.dma(sync.dma_start(out=xnhi[dc_][:, :],
                                          in_=xnhi_s[:, 128 * dc_:128 * (dc_ + 1)], transpose=True))
                    sL.dma(sync.dma_start(out=xnlo[dc_][:, :],
                                          in_=xnlo_s[:, 128 * dc_:128 * (dc_ + 1)], transpose=True))
            p1_xn_ready = sL.n

        @block.gpsimd
        def _(gpsimd):
            gpsimd.wait_ge(sV.h, p1_vec)
            sG.dma(gpsimd.dma_start(out=row_s[:, :], in_=smax[:, 0:1]))
            gpsimd.wait_ge(sG.h, sG.n)
            sG.dma(gpsimd.dma_start(out=srow[:, :], in_=row_s[:, :]))
        p2_row = sG.n

        @block.vector
        def _(vector):
            vector.wait_ge(sG.h, p2_row)
            vector.memset(sone[:, :], 0.0)
            sV.inc(vector.reduce_max(out=sone[:, 0:1], in_=srow[:, :], axis=AX.X))
        p2_lmax = sV.n

        @block.gpsimd
        def _(gpsimd):
            gpsimd.wait_ge(sV.h, p2_lmax)
            sG.dma(gpsimd.dma_start(out=cc1_i[:, :], in_=sone[:, :]))
            gpsimd.wait_ge(sG.h, sG.n)
            cc.inc(gpsimd.collective_compute(
                "AllReduce", ALU.max, replica_groups=[list(range(N_CORES))],
                ins=[cc1_i.ap().opt()], outs=[cc1_o.ap().opt()]))
        cc1_done = cc.n

        if stage <= 1:
            @block.gpsimd
            def _(gpsimd):
                gpsimd.wait_ge(sL.h, p1_xn_ready)
                for i in range(NTC):
                    sY.dma(gpsimd.dma_start(out=y_d[128 * i:128 * (i + 1), :],
                                            in_=xnhi[i][:, 0:2048]))
                if debug:
                    gpsimd.wait_ge(cc.h, cc1_done)
                    sG.dma(gpsimd.dma_start(out=sone[:, :], in_=cc1_o[:, :]))
                    gpsimd.wait_ge(sG.h, sG.n)
                    sY.dma(gpsimd.dma_start(out=dbg["sc"][:, :], in_=sone[:, :]))
                gpsimd.wait_ge(sY.h, sY.n)
            return nc

        with ExitStack() as pw:
            EW = pw.enter_context
            wf32 = [EW(nc.sbuf_tensor(f"wsp{i}", [128, 2048], F32)) for i in range(2)]
            whib = [EW(nc.sbuf_tensor(f"wsph{i}", [128, 2048], BF16)) for i in range(2)]
            wlob = [EW(nc.sbuf_tensor(f"wspl{i}", [128, 2048], BF16)) for i in range(2)]
            wjk = EW(nc.sbuf_tensor("wspj", [128, 1], F32))
            wjk2 = EW(nc.sbuf_tensor("wspj2", [128, 1], F32))
            splits = [("wdq", wdq_d, D, DCQ), ("wuq", wuq_d, DCQ, NH * DH), ("wqr", wqr_d, DCQ, NH * DHR),
                      ("wdkv", wdkv_d, D, DC), ("wuk", wuk_d, DC, NH * DH), ("wkr", wkr_d, D, NH * DHR),
                      ("wuv", wuv_d, DC, NH * DH), ("wo", wo_d, NH * DH, D)]
            chunks = []
            for nm, dram, rows, cols in splits:
                for r in range(rows // 128):
                    chunks.append((nm, dram, r, cols))
            NWC = len(chunks)
            bL, bS, bV = sL.n, sS.n, sV.n

            @block.sync
            def _(sync):
                for n, (nm, dram, r, cols) in enumerate(chunks):
                    if n >= 2:
                        sync.wait_ge(sV.h, bV + (n - 1) * 2)
                    sL.dma(sync.dma_start(out=wf32[n % 2][:, 0:cols], in_=dram[128 * r:128 * (r + 1), :]))

            @block.vector
            def _(vector):
                vector.memset(wjk2[:, :], 0.0)
                for n, (nm, dram, r, cols) in enumerate(chunks):
                    vector.wait_ge(sL.h, bL + 16 * (n + 1))
                    if n >= 2:
                        vector.wait_ge(sS.h, bS + (n - 1) * 2 * 16)
                    sV.inc(vector.tensor_copy(out=whib[n % 2][:, 0:cols], in_=wf32[n % 2][:, 0:cols]))
                    vector.tensor_copy(out=wjk[:, :], in_=wjk2[:, :])
                    sV.inc(vector.scalar_tensor_tensor(out=wlob[n % 2][:, 0:cols], in0=wf32[n % 2][:, 0:cols],
                                                       scalar=1.0, in1=whib[n % 2][:, 0:cols],
                                                       op0=ALU.mult, op1=ALU.subtract))

            @block.scalar
            def _(scalar):
                for n, (nm, dram, r, cols) in enumerate(chunks):
                    hi_d, lo_d = wsplit[nm]
                    scalar.wait_ge(sV.h, bV + n * 2 + 2)
                    sS.dma(scalar.dma_start(out=hi_d[128 * r:128 * (r + 1), :], in_=whib[n % 2][:, 0:cols]))
                    sS.dma(scalar.dma_start(out=lo_d[128 * r:128 * (r + 1), :], in_=wlob[n % 2][:, 0:cols]))
            wsplit_done = sS.n

        with ExitStack() as p3:
            E3 = p3.enter_context
            rg_hi = [E3(nc.sbuf_tensor(f"p3rh{i}", [128, 512], BF16)) for i in range(2)]
            rg_lo = [E3(nc.sbuf_tensor(f"p3rl{i}", [128, 512], BF16)) for i in range(2)]
            qc_hi = [E3(nc.sbuf_tensor(f"p3qh{i}", [128, 512], BF16)) for i in range(DCQ // 128)]
            qc_lo = [E3(nc.sbuf_tensor(f"p3ql{i}", [128, 512], BF16)) for i in range(DCQ // 128)]
            c_hi = [E3(nc.sbuf_tensor(f"p3ch{i}", [128, 512], BF16)) for i in range(DC // 128)]
            c_lo = [E3(nc.sbuf_tensor(f"p3cl{i}", [128, 512], BF16)) for i in range(DC // 128)]
            stg_hi = [[E3(nc.sbuf_tensor(f"p3sh{ss}_{i}", [128, 512], BF16)) for i in range(4)] for ss in range(2)]
            stg_lo = [[E3(nc.sbuf_tensor(f"p3sl{ss}_{i}", [128, 512], BF16)) for i in range(4)] for ss in range(2)]
            rtmp = [E3(nc.sbuf_tensor(f"p3rt{i}", [128, 512], F32)) for i in range(2)]
            rb = [E3(nc.sbuf_tensor(f"p3rb{i}", [128, 512], F32)) for i in range(2)]
            pj = E3(nc.sbuf_tensor("p3jk", [128, 1], F32))
            pj2 = E3(nc.sbuf_tensor("p3jk2", [128, 1], F32))
            PS4 = [pA, pB, pC, pE]

            @block.vector
            def _(vector):
                sV.inc(vector.memset(pj2[:, :], 0.0))

            chain = dict(V=sV.n, M=sM.n)

            def emit_proj(tau, wname, K, cols, moving_hi, moving_lo, mslice, evac,
                          row0=0, scr=None, dst=None):
                whi_d, wlo_d = wsplit[wname]
                NG = cols // 512
                tok = tau * TT
                bL0, bM0, bV0, bG0 = sL.n, sM.n, sV.n, sG.n
                pre_V, pre_M = chain["V"], chain["M"]
                VPG = 3 if evac == "rope" else 2
                TPG = K + (4 if evac == "rope" else 0)

                @block.sync
                def _(sync):
                    sync.wait_ge(sS.h, wsplit_done)
                    for g in range(NG):
                        for k in range(K):
                            gk = g * K + k
                            if gk >= 2:
                                g2, k2 = divmod(gk - 2, K)
                                sync.wait_ge(sM.h, bM0 + g2 * TPG + k2 + 1)
                            else:
                                sync.wait_ge(sM.h, bM0)
                            sL.dma(sync.dma_start(out=rg_hi[gk % 2][:, :],
                                                  in_=whi_d[128 * k:128 * (k + 1), 512 * g:512 * (g + 1)]))
                            sL.dma(sync.dma_start(out=rg_lo[gk % 2][:, :],
                                                  in_=wlo_d[128 * k:128 * (k + 1), 512 * g:512 * (g + 1)]))

                @block.tensor
                def _(tensor):
                    tensor.wait_ge(sL.h, p1_xn_ready)
                    tensor.wait_ge(sV.h, pre_V)
                    for g in range(NG):
                        if g >= 1:
                            tensor.wait_ge(sV.h, bV0 + g * VPG)
                        for k in range(K):
                            gk = g * K + k
                            tensor.wait_ge(sL.h, bL0 + 32 * (gk + 1))
                            e = gk % 2
                            mh = moving_hi[k][:, mslice]
                            ml = moving_lo[k][:, mslice]
                            for oc in range(4):
                                wh = rg_hi[e][:, 128 * oc:128 * (oc + 1)]
                                wl = rg_lo[e][:, 128 * oc:128 * (oc + 1)]
                                tensor.matmul(PS4[oc][:, :], wh, mh, start=(k == 0), stop=False)
                                tensor.matmul(PS4[oc][:, :], wh, ml, start=False, stop=False)
                                ins = tensor.matmul(PS4[oc][:, :], wl, mh, start=False, stop=(k == K - 1))
                            sM.inc(ins)
                        if evac == "rope":
                            for half in range(2):
                                tensor.wait_ge(sV.h, bV0 + g * VPG + half + 1)
                                for j in range(2):
                                    sM.inc(tensor.matmul(PS4[2 * half + j][:, :], rope_rt[:, :],
                                                         rtmp[j][:, :], start=True, stop=True))

                @block.vector
                def _(vector):
                    vector.wait_ge(sM.h, pre_M)
                    for g in range(NG):
                        ss = g % 2
                        if evac != "local":
                            vector.wait_ge(sG.h, bG0 if g < 2 else bG0 + (g - 1) * 8 * 16)
                        if evac == "rope":
                            cs = cos128[:, tok:tok + TT]
                            sn = sin128[:, tok:tok + TT]
                            for half in range(2):
                                if half == 0:
                                    vector.wait_ge(sM.h, bM0 + g * TPG + K)
                                o0, o1 = 2 * half, 2 * half + 1
                                vector.tensor_copy(out=rtmp[0][:, :], in_=PS4[o0][:, :])
                                vector.tensor_copy(out=rtmp[1][:, :], in_=PS4[o1][:, :])
                                sV.inc(vector.tensor_copy(out=pj[:, :], in_=pj2[:, :]))
                                vector.wait_ge(sM.h, bM0 + g * TPG + K + 2 * (half + 1))
                                vector.tensor_tensor(out=rtmp[0][:, :], in0=rtmp[0][:, :], in1=cs, op=ALU.mult)
                                vector.tensor_tensor(out=rtmp[1][:, :], in0=rtmp[1][:, :], in1=cs, op=ALU.mult)
                                vector.tensor_tensor(out=rb[0][:, :], in0=PS4[o0][:, :], in1=sn, op=ALU.mult)
                                vector.tensor_tensor(out=rb[1][:, :], in0=PS4[o1][:, :], in1=sn, op=ALU.mult)
                                vector.tensor_tensor(out=rtmp[0][:, :], in0=rtmp[0][:, :], in1=rb[0][:, :], op=ALU.add)
                                vector.tensor_tensor(out=rtmp[1][:, :], in0=rtmp[1][:, :], in1=rb[1][:, :], op=ALU.add)
                                vector.tensor_copy(out=stg_hi[ss][o0][:, :], in_=rtmp[0][:, :])
                                vector.tensor_copy(out=stg_hi[ss][o1][:, :], in_=rtmp[1][:, :])
                                vector.scalar_tensor_tensor(out=stg_lo[ss][o0][:, :], in0=rtmp[0][:, :],
                                                            scalar=1.0, in1=stg_hi[ss][o0][:, :],
                                                            op0=ALU.mult, op1=ALU.subtract)
                                vector.scalar_tensor_tensor(out=stg_lo[ss][o1][:, :], in0=rtmp[1][:, :],
                                                            scalar=1.0, in1=stg_hi[ss][o1][:, :],
                                                            op0=ALU.mult, op1=ALU.subtract)
                            sV.inc(vector.tensor_copy(out=pj[:, :], in_=pj2[:, :]))
                        else:
                            vector.wait_ge(sM.h, bM0 + (g + 1) * TPG)
                            if evac == "local":
                                hv = [dst[0][g * 4 + oc] for oc in range(4)]
                                lv = [dst[1][g * 4 + oc] for oc in range(4)]
                            else:
                                hv, lv = stg_hi[ss], stg_lo[ss]
                            for oc in range(4):
                                vector.tensor_copy(out=hv[oc][:, :], in_=PS4[oc][:, :])
                            sV.inc(vector.tensor_copy(out=pj[:, :], in_=pj2[:, :]))
                            for oc in range(4):
                                vector.scalar_tensor_tensor(out=lv[oc][:, :], in0=PS4[oc][:, :],
                                                            scalar=1.0, in1=hv[oc][:, :],
                                                            op0=ALU.mult, op1=ALU.subtract)
                            sV.inc(vector.tensor_copy(out=pj[:, :], in_=pj2[:, :]))

                if evac != "local":
                    @block.gpsimd
                    def _(gpsimd):
                        for g in range(NG):
                            ss = g % 2
                            gpsimd.wait_ge(sV.h, bV0 + (g + 1) * VPG)
                            for oc in range(4):
                                rr = row0 + 512 * g + 128 * oc
                                sG.dma(gpsimd.dma_start(out=scr[0, rr:rr + 128, tok:tok + TT],
                                                        in_=stg_hi[ss][oc][:, :]))
                                sG.dma(gpsimd.dma_start(out=scr[1, rr:rr + 128, tok:tok + TT],
                                                        in_=stg_lo[ss][oc][:, :]))

                chain["V"], chain["M"] = sV.n, sM.n

            def emit_v(tau):
                whi_d, wlo_d = wsplit["wuv"]
                NG = 4
                K = 4
                tok = tau * TT
                bL0, bM0, bV0, bG0 = sL.n, sM.n, sV.n, sG.n
                pre_V, pre_M = chain["V"], chain["M"]
                VPG, TPG = 2, K

                @block.sync
                def _(sync):
                    sync.wait_ge(sS.h, wsplit_done)
                    for g in range(NG):
                        for k in range(K):
                            gk = g * K + k
                            if gk >= 2:
                                g2, k2 = divmod(gk - 2, K)
                                sync.wait_ge(sM.h, bM0 + g2 * TPG + k2 + 1)
                            else:
                                sync.wait_ge(sM.h, bM0)
                            sL.dma(sync.dma_start(out=rg_hi[gk % 2][:, :],
                                                  in_=whi_d[128 * k:128 * (k + 1), 512 * g:512 * (g + 1)]))
                            sL.dma(sync.dma_start(out=rg_lo[gk % 2][:, :],
                                                  in_=wlo_d[128 * k:128 * (k + 1), 512 * g:512 * (g + 1)]))

                @block.tensor
                def _(tensor):
                    tensor.wait_ge(sV.h, pre_V)
                    for g in range(NG):
                        if g >= 1:
                            tensor.wait_ge(sV.h, bV0 + g * VPG)
                        for k in range(K):
                            gk = g * K + k
                            tensor.wait_ge(sL.h, bL0 + 32 * (gk + 1))
                            e = gk % 2
                            for tc in range(4):
                                ch = c_hi[k][:, 128 * tc:128 * (tc + 1)]
                                cl = c_lo[k][:, 128 * tc:128 * (tc + 1)]
                                tensor.matmul(PS4[tc][:, :], ch, rg_hi[e][:, :], start=(k == 0), stop=False)
                                tensor.matmul(PS4[tc][:, :], cl, rg_hi[e][:, :], start=False, stop=False)
                                ins = tensor.matmul(PS4[tc][:, :], ch, rg_lo[e][:, :], start=False, stop=(k == K - 1))
                            sM.inc(ins)

                @block.vector
                def _(vector):
                    vector.wait_ge(sM.h, pre_M)
                    for g in range(NG):
                        ss = g % 2
                        vector.wait_ge(sG.h, bG0 if g < 2 else bG0 + (g - 1) * 8 * 16)
                        vector.wait_ge(sM.h, bM0 + (g + 1) * TPG)
                        for tc in range(4):
                            vector.tensor_copy(out=stg_hi[ss][tc][:, :], in_=PS4[tc][:, :])
                        sV.inc(vector.tensor_copy(out=pj[:, :], in_=pj2[:, :]))
                        for tc in range(4):
                            vector.scalar_tensor_tensor(out=stg_lo[ss][tc][:, :], in0=PS4[tc][:, :],
                                                        scalar=1.0, in1=stg_hi[ss][tc][:, :],
                                                        op0=ALU.mult, op1=ALU.subtract)
                        sV.inc(vector.tensor_copy(out=pj[:, :], in_=pj2[:, :]))

                @block.gpsimd
                def _(gpsimd):
                    for g in range(NG):
                        ss = g % 2
                        gpsimd.wait_ge(sV.h, bV0 + (g + 1) * VPG)
                        for tc in range(4):
                            tr = tok + 128 * tc
                            sG.dma(gpsimd.dma_start(out=v_s[0, tr:tr + 128, 512 * g:512 * (g + 1)],
                                                    in_=stg_hi[ss][tc][:, :]))
                            sG.dma(gpsimd.dma_start(out=v_s[1, tr:tr + 128, 512 * g:512 * (g + 1)],
                                                    in_=stg_lo[ss][tc][:, :]))

                chain["V"], chain["M"] = sV.n, sG.n if False else sM.n

            ts_all = slice(None)
            for tau in range(NTT):
                msl = slice(tau * TT, tau * TT + TT)
                emit_proj(tau, "wdq", ND, DCQ, xnhi, xnlo, msl, "local", dst=(qc_hi, qc_lo))
                emit_proj(tau, "wuq", DCQ // 128, NH * DH, qc_hi, qc_lo, ts_all, "spill", row0=0, scr=q_s)
                emit_proj(tau, "wqr", DCQ // 128, NH * DHR, qc_hi, qc_lo, ts_all, "rope", row0=2048, scr=q_s)
                emit_proj(tau, "wdkv", ND, DC, xnhi, xnlo, msl, "local", dst=(c_hi, c_lo))
                emit_proj(tau, "wuk", DC // 128, NH * DH, c_hi, c_lo, ts_all, "spill", row0=0, scr=k_s)
                emit_proj(tau, "wkr", ND, NH * DHR, xnhi, xnlo, msl, "rope", row0=2048, scr=k_s)
                emit_v(tau)
            p3_done_G = sG.n
            p3_done_V = sV.n
            p3_done_M = sM.n

        def emit_wscan(w_dram, rows, cols, colpair):
            bL0, bV0 = sL.n, sV.n
            n_r, n_c = rows // 128, cols // 2048
            NCH = n_r * n_c
            a0, b0 = colpair

            @block.sync
            def _(sync):
                sync.wait_ge(sM.h, p3_done_M)
                sync.wait_ge(sG.h, p3_done_G)
                n = 0
                for r in range(n_r):
                    for cgrp in range(n_c):
                        if n >= 3:
                            sync.wait_ge(sV.h, bV0 + n - 2)
                        sL.dma(sync.dma_start(out=wsc[n % 3][:, :],
                                              in_=w_dram[128 * r:128 * (r + 1), 2048 * cgrp:2048 * (cgrp + 1)]))
                        n += 1

            @block.vector
            def _(vector):
                for n in range(NCH):
                    vector.wait_ge(sL.h, bL0 + 16 * (n + 1))
                    sV.inc(vector.reduce_max(out=wmx[n % 2][:, :], in_=wsc[n % 3][:, :],
                                             axis=AX.X, apply_absolute_value=True))
                    if n >= 1:
                        c_ = n - 1
                        aa = a0 if c_ % 2 == 0 else b0
                        bb = b0 if c_ % 2 == 0 else a0
                        vector.tensor_tensor(out=smax[:, bb:bb + 1], in0=smax[:, aa:aa + 1],
                                             in1=wmx[(n - 1) % 2][:, :], op=ALU.max)
                vector.tensor_copy(out=pjk[:, :], in_=pjk2[:, :])
                c_ = NCH - 1
                aa = a0 if c_ % 2 == 0 else b0
                bb = b0 if c_ % 2 == 0 else a0
                sV.inc(vector.tensor_tensor(out=smax[:, bb:bb + 1], in0=smax[:, aa:aa + 1],
                                            in1=wmx[(NCH - 1) % 2][:, :], op=ALU.max))
            return b0 if (NCH - 1) % 2 == 0 else a0

        with ExitStack() as pws:
            EW = pws.enter_context
            wsc = [EW(nc.sbuf_tensor(f"wsc{i}", [128, 2048], F32)) for i in range(3)]
            wmx = [EW(nc.sbuf_tensor(f"wmx{i}", [128, 1], F32)) for i in range(2)]
            sgx = EW(nc.sbuf_tensor("sgx", [1, 8], F32))
            wm1 = EW(nc.sbuf_tensor("wm1", [1, 1], F32))
            wm2 = EW(nc.sbuf_tensor("wm2", [1, 1], F32))
            sxv = EW(nc.sbuf_tensor("sxv", [1, 1], F32))
            sw1v = EW(nc.sbuf_tensor("sw1v", [1, 1], F32))
            sw2v = EW(nc.sbuf_tensor("sw2v", [1, 1], F32))
            rsx = EW(nc.sbuf_tensor("rsx", [1, 1], F32))
            rsw1 = EW(nc.sbuf_tensor("rsw1", [1, 1], F32))
            rsw2 = EW(nc.sbuf_tensor("rsw2", [1, 1], F32))
            scrow = EW(nc.sbuf_tensor("scrow", [1, 8], F32))
            pjk = EW(nc.sbuf_tensor("wsjk", [128, 1], F32))
            pjk2 = EW(nc.sbuf_tensor("wsjk2", [128, 1], F32))

            @block.vector
            def _(vector):
                sV.inc(vector.memset(pjk2[:, :], 0.0))
                vector.memset(scrow[:, :], 0.0)
            col_w1 = emit_wscan(wfc_d, J, D, (4, 5))
            col_w2 = emit_wscan(wpj_d, D, J, (6, 7))
            ws_done_V = sV.n

            @block.gpsimd
            def _(gpsimd):
                gpsimd.wait_ge(sV.h, ws_done_V)
                sG.dma(gpsimd.dma_start(out=row_s[:, :], in_=smax[:, col_w1:col_w1 + 1]))
                gpsimd.wait_ge(sG.h, sG.n)
                sG.dma(gpsimd.dma_start(out=srow[:, :], in_=row_s[:, :]))
                gpsimd.wait_ge(sG.h, sG.n)
                gpsimd.wait_ge(cc.h, cc1_done)
                sG.dma(gpsimd.dma_start(out=sgx[:, :], in_=cc1_o[:, :]))
            sc1_g = sG.n

            @block.vector
            def _(vector):
                vector.wait_ge(sG.h, sc1_g)
                sV.inc(vector.reduce_max(out=wm1[:, :], in_=srow[:, :], axis=AX.X))
            sc1_v = sV.n

            @block.gpsimd
            def _(gpsimd):
                gpsimd.wait_ge(sV.h, sc1_v)
                sG.dma(gpsimd.dma_start(out=row_s[:, :], in_=smax[:, col_w2:col_w2 + 1]))
                gpsimd.wait_ge(sG.h, sG.n)
                sG.dma(gpsimd.dma_start(out=srow[:, :], in_=row_s[:, :]))
            sc2_g = sG.n

            @block.vector
            def _(vector):
                F = lambda: vector.tensor_copy(out=pjk[:, :], in_=pjk2[:, :])
                vector.wait_ge(sG.h, sc2_g)
                vector.reduce_max(out=wm2[:, :], in_=srow[:, :], axis=AX.X)
                vector.tensor_scalar(out=sxv[:, :], in0=sgx[0:1, 0:1], scalar1=1.0 / 448.0,
                                     scalar2=1e-6, op0=ALU.mult, op1=ALU.max)
                vector.tensor_scalar(out=sw1v[:, :], in0=wm1[:, :], scalar1=1.0 / 448.0,
                                     scalar2=1e-6, op0=ALU.mult, op1=ALU.max)
                vector.tensor_scalar(out=sw2v[:, :], in0=wm2[:, :], scalar1=1.0 / 448.0,
                                     scalar2=1e-6, op0=ALU.mult, op1=ALU.max)
                F()
                vector.reciprocal(out=rsx[:, :], in_=sxv[:, :])
                vector.reciprocal(out=rsw1[:, :], in_=sw1v[:, :])
                vector.reciprocal(out=rsw2[:, :], in_=sw2v[:, :])
                F()
                vector.tensor_scalar(out=scrow[0:1, 0:1], in0=rsx[:, :], scalar1=0.5, scalar2=None, op0=ALU.mult)
                vector.tensor_scalar(out=scrow[0:1, 4:5], in0=rsw1[:, :], scalar1=0.5, scalar2=None, op0=ALU.mult)
                vector.tensor_scalar(out=scrow[0:1, 5:6], in0=rsw2[:, :], scalar1=0.5, scalar2=None, op0=ALU.mult)
                vector.tensor_scalar(out=scrow[0:1, 1:2], in0=sxv[:, :], scalar1=sw1v[0:1, 0:1],
                                     scalar2=4.0, op0=ALU.mult, op1=ALU.mult)
                vector.tensor_copy(out=spers[0:1, 0:1], in_=sw2v[:, :])
                vector.tensor_copy(out=spers[0:1, 1:2], in_=scrow[0:1, 5:6])
                F()
                sV.inc(vector.tensor_copy(out=pjk[:, :], in_=pjk2[:, :]))
            sc3_v = sV.n

            @block.gpsimd
            def _(gpsimd):
                gpsimd.wait_ge(sV.h, sc3_v)
                sG.dma(gpsimd.dma_start(out=bc_s[:, :], in_=scrow[:, :]))
                gpsimd.wait_ge(sG.h, sG.n)
                sG.dma(gpsimd.dma_start(out=scal[:, :], in_=bc_s.ap().broadcast_to([128, 8])))
            scales1_g = sG.n

        with ExitStack() as p4a:
            E4a = p4a.enter_context
            x32 = [E4a(nc.sbuf_tensor(f"p4x{i}", [128, L], F32)) for i in range(2)]
            xf8 = [E4a(nc.sbuf_tensor(f"p4f{i}", [128, L], F8)) for i in range(2)]
            bV0, bC0 = sV.n, sC.n

            @block.vector
            def _(vector):
                vector.wait_ge(sG.h, scales1_g)
                vector.wait_ge(sM.h, p3_done_M)
                for d in range(ND):
                    if d >= 2:
                        vector.wait_ge(sC.h, bC0 + d - 1)
                    sV.inc(vector.tensor_tensor(out=x32[d % 2][:, :], in0=xnhi[d][:, :],
                                                in1=xnlo[d][:, :], op=ALU.add))
                    if d >= 1:
                        vector.wait_ge(sC.h, bC0 + d)
                        sV.inc(vector.tensor_copy(out=xnhi[d - 1][:, :], in_=xf8[(d - 1) % 2][:, :]))
                vector.wait_ge(sC.h, bC0 + ND)
                sV.inc(vector.tensor_copy(out=xnhi[ND - 1][:, :], in_=xf8[(ND - 1) % 2][:, :]))

            @block.scalar
            def _(scalar):
                for d in range(ND):
                    scalar.wait_ge(sV.h, bV0 + 2 * d + 1)
                    sC.inc(scalar.activation(out=xf8[d % 2][:, :], in_=x32[d % 2][:, :],
                                             func=AF.Copy, scale=scal[:, 0:1]))
            p4a_V = sV.n
        p3_scope.close()

        with ExitStack() as p4:
            E4 = p4.enter_context
            wf32r = [E4(nc.sbuf_tensor(f"p4w{i}", [128, D], F32)) for i in range(2)]
            wf8r = [E4(nc.sbuf_tensor(f"p4q{i}", [128, D], F8)) for i in range(2)]
            wb16r = [E4(nc.sbuf_tensor(f"p4b{i}", [128, D], BF16)) for i in range(2)]
            w8T = [[E4(nc.sbuf_tensor(f"p4t{ss}_{i}", [128, 512], BF16, is_transpose_dest=True))
                    for i in range(ND)] for ss in range(2)]
            hrelu = [E4(nc.sbuf_tensor(f"p4hr{i}", [128, 512], F32)) for i in range(2)]
            htile = [E4(nc.sbuf_tensor(f"p4ht{i}", [128, 512], F32)) for i in range(2)]
            hmx = [E4(nc.sbuf_tensor(f"p4hm{i}", [128, 1], F32)) for i in range(2)]
            pjk = E4(nc.sbuf_tensor("p4jk", [128, 1], F32))
            pjk2 = E4(nc.sbuf_tensor("p4jk2", [128, 1], F32))

            @block.vector
            def _(vector):
                sV.inc(vector.memset(pjk2[:, :], 0.0))

            bL0, bC0b, bV0b, bG0 = sL.n, sC.n, sV.n, sG.n

            @block.sync
            def _(sync):
                sync.wait_ge(sV.h, p4a_V)
                for j in range(J // 128):
                    if j >= 2:
                        sync.wait_ge(sC.h, bC0b + j - 1)
                    sL.dma(sync.dma_start(out=wf32r[j % 2][:, :], in_=wfc_d[128 * j:128 * (j + 1), :]))

            @block.scalar
            def _(scalar):
                scalar.wait_ge(sG.h, scales1_g)
                for j in range(J // 128):
                    scalar.wait_ge(sL.h, bL0 + 16 * (j + 1))
                    if j >= 2:
                        scalar.wait_ge(sV.h, bV0b + j - 1)
                    sC.inc(scalar.activation(out=wf8r[j % 2][:, :], in_=wf32r[j % 2][:, :],
                                             func=AF.Copy, scale=scal[:, 4:5]))

            @block.vector
            def _(vector):
                for j in range(J // 128):
                    vector.wait_ge(sC.h, bC0b + j + 1)
                    if j >= 2:
                        vector.wait_ge(sG.h, bG0 + 16 * (j - 1))
                    sV.inc(vector.tensor_copy(out=wb16r[j % 2][:, :], in_=wf8r[j % 2][:, :]))

            @block.gpsimd
            def _(gpsimd):
                for j in range(J // 128):
                    gpsimd.wait_ge(sV.h, bV0b + j + 1)
                    sG.dma(gpsimd.dma_start(out=w8b_s[128 * j:128 * (j + 1), :], in_=wb16r[j % 2][:, :]))
            p4b_G = sG.n

            NJG = J // 512
            PPG = 16
            MTOT = NJG * PPG
            GPM = 32 if debug else 16
            bL1, bM1, bV1, bC1, bG1 = sL.n, sM.n, sV.n, sC.n, sG.n

            @block.sync
            def _(sync):
                for g in range(NJG):
                    sync.wait_ge(sG.h, bG0 + 16 * min(4 * (g + 1), J // 128))
                    if g >= 2:
                        sync.wait_ge(sM.h, bM1 + (g - 1) * PPG)
                    for d in range(ND):
                        sL.dma(sync.dma_start(out=w8T[g % 2][d][:, :],
                                              in_=w8b_s[512 * g:512 * (g + 1), 128 * d:128 * (d + 1)],
                                              transpose=True))

            @block.tensor
            def _(tensor):
                tensor.wait_ge(sV.h, p4a_V)
                tensor.wait_ge(sV.h, p3_done_V)
                m = 0
                for g in range(NJG):
                    tensor.wait_ge(sL.h, bL1 + 16 * ND * (g + 1))
                    for jc in range(4):
                        for t in range(NTT):
                            if m >= 2:
                                tensor.wait_ge(sC.h, bC1 + 2 * (m - 2) + 1)
                            p = pA if m % 2 == 0 else pB
                            for d in range(ND):
                                ins = tensor.matmul(p[:, :], w8T[g % 2][d][:, 128 * jc:128 * (jc + 1)],
                                                    xnhi[d][:, 512 * t:512 * (t + 1)],
                                                    start=(d == 0), stop=(d == ND - 1))
                            sM.inc(ins)
                            m += 1

            @block.scalar
            def _(scalar):
                for m in range(MTOT):
                    scalar.wait_ge(sM.h, bM1 + m + 1)
                    p = pA if m % 2 == 0 else pB
                    if m >= 3:
                        scalar.wait_ge(sG.h, bG1 + GPM * (m - 2))
                    sC.inc(scalar.activation(out=hrelu[m % 2][:, :], in_=p[:, :],
                                             func=AF.Relu, scale=scal[:, 1:2]))
                    if m >= 1:
                        sC.inc(scalar.activation(out=htile[(m - 1) % 2][:, :],
                                                 in_=hrelu[(m - 1) % 2][:, :], func=AF.Square))
                scalar.wait_ge(sG.h, bG1 + GPM * (MTOT - 2))
                sC.inc(scalar.activation(out=htile[(MTOT - 1) % 2][:, :],
                                         in_=hrelu[(MTOT - 1) % 2][:, :], func=AF.Square))

            @block.vector
            def _(vector):
                for m in range(MTOT):
                    vector.wait_ge(sC.h, bC1 + min(2 * m + 3, 2 * MTOT))
                    vector.reduce_max(out=hmx[m % 2][:, :], in_=htile[m % 2][:, :], axis=AX.X)
                    aa = 2 if m % 2 == 0 else 3
                    bb = 3 if m % 2 == 0 else 2
                    sV.inc(vector.tensor_tensor(out=smax[:, bb:bb + 1], in0=smax[:, aa:aa + 1],
                                                in1=hmx[(m - 1) % 2][:, :] if m >= 1 else smax[:, aa:aa + 1],
                                                op=ALU.max))
                vector.tensor_copy(out=pjk[:, :], in_=pjk2[:, :])
                m = MTOT
                aa = 2 if m % 2 == 0 else 3
                bb = 3 if m % 2 == 0 else 2
                sV.inc(vector.tensor_tensor(out=smax[:, bb:bb + 1], in0=smax[:, aa:aa + 1],
                                            in1=hmx[(m - 1) % 2][:, :], op=ALU.max))
            p4c_hcol = 3 if MTOT % 2 == 0 else 2

            @block.gpsimd
            def _(gpsimd):
                for m in range(MTOT):
                    g, rem = divmod(m, PPG)
                    jc, t = divmod(rem, NTT)
                    gpsimd.wait_ge(sV.h, bV1 + m + 1)
                    rr = 512 * g + 128 * jc
                    sG.dma(gpsimd.dma_start(out=h_s[rr:rr + 128, 512 * t:512 * (t + 1)],
                                            in_=htile[m % 2][:, :]))
                    if debug:
                        sG.dma(gpsimd.dma_start(out=dbg["h"][rr:rr + 128, 512 * t:512 * (t + 1)],
                                                in_=htile[m % 2][:, :]))
            p4_done_G = sG.n
            p4_done_V = sV.n
            p4_done_C = sC.n
            p4_done_M = sM.n

        @block.gpsimd
        def _(gpsimd):
            gpsimd.wait_ge(sV.h, p4_done_V)
            sG.dma(gpsimd.dma_start(out=row_s[:, :], in_=smax[:, p4c_hcol:p4c_hcol + 1]))
            gpsimd.wait_ge(sG.h, sG.n)
            sG.dma(gpsimd.dma_start(out=srow[:, :], in_=row_s[:, :]))
        p5_row = sG.n

        @block.vector
        def _(vector):
            vector.wait_ge(sG.h, p5_row)
            vector.memset(sone[:, :], 0.0)
            sV.inc(vector.reduce_max(out=sone[0:1, 0:1], in_=srow[:, :], axis=AX.X))
        p5_lmax = sV.n

        @block.gpsimd
        def _(gpsimd):
            gpsimd.wait_ge(sV.h, p5_lmax)
            sG.dma(gpsimd.dma_start(out=cc2_i[:, :], in_=sone[:, :]))
            gpsimd.wait_ge(sG.h, sG.n)
            cc.inc(gpsimd.collective_compute(
                "AllReduce", ALU.max, replica_groups=[list(range(N_CORES))],
                ins=[cc2_i.ap().opt()], outs=[cc2_o.ap().opt()]))
        cc2_done = cc.n

        p8p = ExitStack()
        E8p = p8p.enter_context
        sgx2 = E8p(nc.sbuf_tensor("p8gx", [1, 8], F32))
        shv = E8p(nc.sbuf_tensor("p8sh", [1, 1], F32))
        rshv = E8p(nc.sbuf_tensor("p8rs", [1, 1], F32))
        scrow2 = E8p(nc.sbuf_tensor("p8sr", [1, 8], F32))
        pjk8 = E8p(nc.sbuf_tensor("p8jk", [128, 1], F32))
        pjk82 = E8p(nc.sbuf_tensor("p8jk2", [128, 1], F32))
        pf32 = [E8p(nc.sbuf_tensor(f"p8w{i}", [128, D], F32)) for i in range(2)]
        pf8 = [E8p(nc.sbuf_tensor(f"p8q{i}", [128, D], F8)) for i in range(2)]
        pb16 = [E8p(nc.sbuf_tensor(f"p8b{i}", [128, D], BF16)) for i in range(2)]
        bGpre = sG.n

        @block.vector
        def _(vector):
            vector.wait_ge(sG.h, bGpre)
            sV.inc(vector.memset(pjk82[:, :], 0.0))

        @block.gpsimd
        def _(gpsimd):
            gpsimd.wait_ge(cc.h, cc2_done)
            gpsimd.wait_ge(sV.h, p5_lmax)
            sG.dma(gpsimd.dma_start(out=sgx2[:, :], in_=cc2_o[:, :]))
        p8a_g = sG.n

        @block.vector
        def _(vector):
            F = lambda: vector.tensor_copy(out=pjk8[:, :], in_=pjk82[:, :])
            vector.wait_ge(sG.h, p8a_g)
            vector.memset(scrow2[:, :], 0.0)
            vector.tensor_scalar(out=shv[:, :], in0=sgx2[0:1, 0:1], scalar1=1.0 / 448.0,
                                 scalar2=1e-6, op0=ALU.mult, op1=ALU.max)
            F()
            vector.reciprocal(out=rshv[:, :], in_=shv[:, :])
            F()
            vector.tensor_scalar(out=scrow2[0:1, 2:3], in0=rshv[:, :], scalar1=0.5, scalar2=None, op0=ALU.mult)
            vector.tensor_scalar(out=scrow2[0:1, 3:4], in0=shv[:, :], scalar1=spers[0:1, 0:1],
                                 scalar2=4.0, op0=ALU.mult, op1=ALU.mult)
            vector.tensor_copy(out=scrow2[0:1, 5:6], in_=spers[0:1, 1:2])
            F()
            sV.inc(vector.tensor_copy(out=pjk8[:, :], in_=pjk82[:, :]))
        p8a_v = sV.n

        @block.gpsimd
        def _(gpsimd):
            gpsimd.wait_ge(sV.h, p8a_v)
            sG.dma(gpsimd.dma_start(out=bc_s[:, :], in_=scrow2[:, :]))
            gpsimd.wait_ge(sG.h, sG.n)
            sG.dma(gpsimd.dma_start(out=scal[:, :], in_=bc_s.ap().broadcast_to([128, 8])))
        p8a_scal = sG.n

        def prod_src(j):
            if j < 64:
                return h_s[128 * j:128 * (j + 1), :]
            c = j - 64
            return wpj_d[128 * (c // 4):128 * (c // 4 + 1), 2048 * (c % 4):2048 * (c % 4 + 1)]

        def prod_dst(j):
            if j < 64:
                return h8_s[128 * j:128 * (j + 1), :]
            c = j - 64
            return w8pb_s[128 * (c // 4):128 * (c // 4 + 1), 2048 * (c % 4):2048 * (c % 4 + 1)]

        def prod_col(j):
            return 2 if j < 64 else 5
        bSp = sS.n
        NPROD = J // 128 + 64

        with ExitStack() as p6:
            E6 = p6.enter_context
            qh128 = [[E6(nc.sbuf_tensor(f"q128{b}_{hl}", [128, L], BF16)) for hl in range(2)] for b in range(2)]
            qh64 = [[E6(nc.sbuf_tensor(f"q64{b}_{hl}", [64, L], BF16)) for hl in range(2)] for b in range(2)]
            kh128 = [[E6(nc.sbuf_tensor(f"k128{b}_{hl}", [128, L], BF16)) for hl in range(2)] for b in range(2)]
            kh64 = [[E6(nc.sbuf_tensor(f"k64{b}_{hl}", [64, L], BF16)) for hl in range(2)] for b in range(2)]
            vh = [[E6(nc.sbuf_tensor(f"vh{b}_{hl}", [128, L], BF16)) for hl in range(2)] for b in range(2)]
            p32 = [E6(nc.sbuf_tensor(f"p6p{i}", [128, 512], F32)) for i in range(2)]
            phi = [E6(nc.sbuf_tensor(f"p6h{i}", [128, 512], BF16)) for i in range(2)]
            plo = [E6(nc.sbuf_tensor(f"p6l{i}", [128, 512], BF16)) for i in range(2)]
            rbc = E6(nc.sbuf_tensor("p6rb", [128, 512], F32))
            avn = E6(nc.sbuf_tensor("p6av", [128, 512], F32))
            ohhi = [E6(nc.sbuf_tensor(f"p6oh{i}", [128, 512], BF16)) for i in range(2)]
            ohlo = [E6(nc.sbuf_tensor(f"p6ol{i}", [128, 512], BF16)) for i in range(2)]
            pjk = E6(nc.sbuf_tensor("p6jk", [128, 1], F32))
            pjk2 = E6(nc.sbuf_tensor("p6jk2", [128, 1], F32))

            @block.vector
            def _(vector):
                sV.inc(vector.memset(pjk2[:, :], 0.0))

            bL6, bM6, bV6, bC6, bG6 = sL.n, sM.n, sV.n, sC.n, sG.n
            MPH = 4 * 17
            CPH = 4 * 16
            VPQ = 16 + 3
            VPH = 4 * VPQ
            GPH = 8

            @block.sync
            def _(sync):
                sync.wait_ge(sG.h, p4_done_G)
                sync.wait_ge(sM.h, p4_done_M)
                sync.wait_ge(sC.h, p4_done_C)
                for h in range(P6_NH):
                    b = h % 2
                    if h >= 1:
                        sync.wait_ge(sM.h, bM6 + (h - 1) * MPH + 1)
                    for hl in range(2):
                        sL.dma(sync.dma_start(out=qh128[b][hl][:, :], in_=q_s[hl, 128 * h:128 * (h + 1), :]))
                        sL.dma(sync.dma_start(out=qh64[b][hl][:, :], in_=q_s[hl, 2048 + 64 * h:2048 + 64 * (h + 1), :]))
                        sL.dma(sync.dma_start(out=kh128[b][hl][:, :], in_=k_s[hl, 128 * h:128 * (h + 1), :]))
                        sL.dma(sync.dma_start(out=kh64[b][hl][:, :], in_=k_s[hl, 2048 + 64 * h:2048 + 64 * (h + 1), :]))
                        for c4 in range(4):
                            sL.dma(sync.dma_start(
                                out=vh[b][hl][:, 512 * c4:512 * (c4 + 1)].rearrange("p (c f) -> p c f", f=128),
                                in_=v_s[hl, 512 * c4:512 * (c4 + 1), 128 * h:128 * (h + 1)].rearrange("(c p) f -> p c f", p=128)))

            @block.tensor
            def _(tensor):
                tensor.wait_ge(sV.h, p4_done_V)
                tensor.wait_ge(sC.h, p4_done_C)
                for h in range(P6_NH):
                    b = h % 2
                    tensor.wait_ge(sL.h, bL6 + 16 * 16 * (h + 1))
                    for qt in range(4):
                        qsl = slice(512 * qt, 512 * (qt + 1))
                        for kt in range(16):
                            ksl = slice(128 * kt, 128 * (kt + 1))
                            ps = pA if kt % 2 == 0 else pB
                            if kt >= 2:
                                tensor.wait_ge(sC.h, bC6 + h * CPH + qt * 16 + kt - 1)
                            tensor.matmul(ps[:, :], kh128[b][0][:, ksl], qh128[b][0][:, qsl], start=True, stop=False)
                            tensor.matmul(ps[:, :], kh128[b][0][:, ksl], qh128[b][1][:, qsl], start=False, stop=False)
                            tensor.matmul(ps[:, :], kh128[b][1][:, ksl], qh128[b][0][:, qsl], start=False, stop=False)
                            tensor.matmul(ps[:, :], kh64[b][0][:, ksl], qh64[b][0][:, qsl], start=False, stop=False)
                            tensor.matmul(ps[:, :], kh64[b][0][:, ksl], qh64[b][1][:, qsl], start=False, stop=False)
                            sM.inc(tensor.matmul(ps[:, :], kh64[b][1][:, ksl], qh64[b][0][:, qsl], start=False, stop=True))
                            if kt >= 1:
                                kp = kt - 1
                                kpsl = slice(128 * kp, 128 * (kp + 1))
                                tensor.wait_ge(sV.h, bV6 + h * VPH + qt * VPQ + kp + 1)
                                e = kp % 2
                                tensor.matmul(pBC[:, :], ones_cb[:, :], phi[e][:, :],
                                              start=(kp == 0), stop=False)
                                tensor.matmul(pBC[:, :], ones_cb[:, :], plo[e][:, :],
                                              start=False, stop=(kp == 15))
                                tensor.matmul(pC[:, :], vh[b][0][:, kpsl], phi[e][:, :],
                                              start=(kp == 0), stop=False)
                                tensor.matmul(pC[:, :], vh[b][1][:, kpsl], phi[e][:, :], start=False, stop=False)
                                tensor.matmul(pC[:, :], vh[b][0][:, kpsl], plo[e][:, :],
                                              start=False, stop=(kp == 15))
                        kp = 15
                        kpsl = slice(128 * kp, 128 * (kp + 1))
                        tensor.wait_ge(sV.h, bV6 + h * VPH + qt * VPQ + kp + 1)
                        e = kp % 2
                        tensor.matmul(pBC[:, :], ones_cb[:, :], phi[e][:, :], start=(kp == 0), stop=False)
                        tensor.matmul(pBC[:, :], ones_cb[:, :], plo[e][:, :], start=False, stop=(kp == 15))
                        tensor.matmul(pC[:, :], vh[b][0][:, kpsl], phi[e][:, :], start=(kp == 0), stop=False)
                        tensor.matmul(pC[:, :], vh[b][1][:, kpsl], phi[e][:, :], start=False, stop=False)
                        sM.inc(tensor.matmul(pC[:, :], vh[b][0][:, kpsl], plo[e][:, :],
                                             start=False, stop=(kp == 15)))

            @block.scalar
            def _(scalar):
                for h in range(P6_NH):
                    for qt in range(4):
                        for kt in range(16):
                            scalar.wait_ge(sM.h, bM6 + h * MPH + qt * 17 + kt + 1)
                            if kt >= 2:
                                scalar.wait_ge(sV.h, bV6 + h * VPH + qt * VPQ + kt - 1)
                            ps = pA if kt % 2 == 0 else pB
                            sC.inc(scalar.activation(out=p32[kt % 2][:, :], in_=ps[:, :],
                                                     func=AF.Exp, scale=INV_SQRT_DH))
                        s_ = h * 4 + qt
                        if s_ == 0:
                            scalar.wait_ge(sG.h, p8a_scal)
                            for j in (0, 1):
                                sS.dma(scalar.dma_start(out=pf32[j % 2][:, :], in_=prod_src(j)))
                        for j in (2 * s_, 2 * s_ + 1):
                            scalar.wait_ge(sS.h, bSp + 16 * (2 * s_ + 2))
                            if j >= 2:
                                scalar.wait_ge(sPV.h, j - 1)
                            sPC.inc(scalar.activation(out=pf8[j % 2][:, :], in_=pf32[j % 2][:, :],
                                                      func=AF.Copy,
                                                      scale=scal[:, prod_col(j):prod_col(j) + 1]))
                        if s_ + 1 < NPROD // 2:
                            for j in (2 * s_ + 2, 2 * s_ + 3):
                                scalar.wait_ge(sPC.h, j - 1)
                                sS.dma(scalar.dma_start(out=pf32[j % 2][:, :], in_=prod_src(j)))

            @block.vector
            def _(vector):
                for h in range(P6_NH):
                    for qt in range(4):
                        for kt in range(16):
                            vector.wait_ge(sC.h, bC6 + h * CPH + qt * 16 + kt + 1)
                            e = kt % 2
                            vector.tensor_copy(out=phi[e][:, :], in_=p32[e][:, :])
                            vector.tensor_copy(out=pjk[:, :], in_=pjk2[:, :])
                            sV.inc(vector.scalar_tensor_tensor(out=plo[e][:, :], in0=p32[e][:, :],
                                                               scalar=1.0, in1=phi[e][:, :],
                                                               op0=ALU.mult, op1=ALU.subtract))
                        vector.wait_ge(sM.h, bM6 + h * MPH + qt * 17 + 17)
                        sV.inc(vector.reciprocal(out=rbc[:, :], in_=pBC[:, :]))
                        vector.tensor_copy(out=pjk[:, :], in_=pjk2[:, :])
                        sV.inc(vector.tensor_tensor(out=avn[:, :], in0=pC[:, :], in1=rbc[:, :],
                                                    op=ALU.mult))
                        if h >= 1 or qt >= 2:
                            vector.wait_ge(sG.h, bG6 + 2 * 16 * (h * 4 + qt))
                        hq = h * 4 + qt
                        vector.tensor_copy(out=ohhi[hq % 2][:, :], in_=avn[:, :])
                        vector.tensor_copy(out=pjk[:, :], in_=pjk2[:, :])
                        sV.inc(vector.scalar_tensor_tensor(out=ohlo[hq % 2][:, :], in0=avn[:, :], scalar=1.0,
                                                           in1=ohhi[hq % 2][:, :], op0=ALU.mult,
                                                           op1=ALU.subtract))
                        for j in (2 * hq, 2 * hq + 1):
                            vector.wait_ge(sPC.h, j + 1)
                            if j >= 2:
                                vector.wait_ge(sPG.h, 16 * j)
                            sPV.inc(vector.tensor_copy(out=pb16[j % 2][:, :], in_=pf8[j % 2][:, :]))

            @block.gpsimd
            def _(gpsimd):
                for h in range(P6_NH):
                    for qt in range(4):
                        hq = h * 4 + qt
                        gpsimd.wait_ge(sV.h, bV6 + h * VPH + qt * VPQ + VPQ)
                        qsl = slice(512 * qt, 512 * (qt + 1))
                        sG.dma(gpsimd.dma_start(out=oh_s[0, 128 * h:128 * (h + 1), qsl], in_=ohhi[hq % 2][:, :]))
                        sG.dma(gpsimd.dma_start(out=oh_s[1, 128 * h:128 * (h + 1), qsl], in_=ohlo[hq % 2][:, :]))
                        for j in (2 * hq, 2 * hq + 1):
                            gpsimd.wait_ge(sPV.h, j + 1)
                            sPG.dma(gpsimd.dma_start(out=prod_dst(j), in_=pb16[j % 2][:, :]))
            p6_done_G = sG.n
            p6_done_M = sM.n
            p6_done_V = sV.n
            p6_done_C = sC.n
        p8p.close()

        with ExitStack() as p7:
            E7 = p7.enter_context
            oht = [[E7(nc.sbuf_tensor(f"p7o{hl}_{i}", [128, 512], BF16)) for i in range(16)] for hl in range(2)]
            worh = [E7(nc.sbuf_tensor(f"p7wh{i}", [128, 512], BF16)) for i in range(2)]
            worl = [E7(nc.sbuf_tensor(f"p7wl{i}", [128, 512], BF16)) for i in range(2)]
            xch = [E7(nc.sbuf_tensor(f"p7x{i}", [128, D], F32)) for i in range(4)]
            ost = [E7(nc.sbuf_tensor(f"p7s{i}", [128, 512], F32)) for i in range(4)]
            bL7, bM7, bV7, bY7 = sL.n, sM.n, sV.n, sY.n
            wo_hi_d, wo_lo_d = wsplit["wo"]
            LPT = 164 * 16

            @block.sync
            def _(sync):
                sync.wait_ge(sG.h, p6_done_G)
                sync.wait_ge(sM.h, p6_done_M)
                sync.wait_ge(sC.h, p6_done_C)
                sync.wait_ge(sV.h, p6_done_V)
                sync.wait_ge(sPG.h, 16 * NPROD)
                for tau in range(NTT):
                    tok = tau * TT
                    if tau >= 1:
                        sync.wait_ge(sV.h, bV7 + 4 * tau)
                    for hd in range(16):
                        for hl in range(2):
                            sL.dma(sync.dma_start(out=oht[hl][hd][:, :],
                                                  in_=oh_s[hl, 128 * hd:128 * (hd + 1), tok:tok + TT]))
                    for tc in range(4):
                        sL.dma(sync.dma_start(out=xch[tc][:, :],
                                              in_=x_d[tok + 128 * tc:tok + 128 * (tc + 1), :]))
                    for dt in range(4):
                        for hd in range(16):
                            gk = dt * 16 + hd
                            if gk >= 2:
                                g2, k2 = divmod(gk - 2, 16)
                                sync.wait_ge(sM.h, bM7 + tau * 64 + g2 * 16 + k2 + 1)
                            else:
                                sync.wait_ge(sM.h, bM7 + tau * 64)
                            sL.dma(sync.dma_start(out=worh[gk % 2][:, :],
                                                  in_=wo_hi_d[128 * hd:128 * (hd + 1), 512 * dt:512 * (dt + 1)]))
                            sL.dma(sync.dma_start(out=worl[gk % 2][:, :],
                                                  in_=wo_lo_d[128 * hd:128 * (hd + 1), 512 * dt:512 * (dt + 1)]))

            @block.tensor
            def _(tensor):
                tensor.wait_ge(sV.h, p6_done_V)
                tensor.wait_ge(sC.h, p6_done_C)
                for tau in range(NTT):
                    tensor.wait_ge(sL.h, bL7 + LPT * tau + 36 * 16)
                    for dt in range(4):
                        gdt = tau * 4 + dt
                        if gdt >= 1:
                            tensor.wait_ge(sV.h, bV7 + gdt)
                        for hd in range(16):
                            gk = dt * 16 + hd
                            tensor.wait_ge(sL.h, bL7 + LPT * tau + 36 * 16 + 32 * (gk + 1))
                            e = gk % 2
                            last = None
                            for tc in range(4):
                                ohh = oht[0][hd][:, 128 * tc:128 * (tc + 1)]
                                ohl = oht[1][hd][:, 128 * tc:128 * (tc + 1)]
                                tensor.matmul(PS4[tc][:, :], ohh, worh[e][:, :], start=(hd == 0), stop=False)
                                tensor.matmul(PS4[tc][:, :], ohl, worh[e][:, :], start=False, stop=False)
                                last = tensor.matmul(PS4[tc][:, :], ohh, worl[e][:, :],
                                                     start=False, stop=(hd == 15))
                            sM.inc(last)

            @block.vector
            def _(vector):
                for tau in range(NTT):
                    for dt in range(4):
                        gdt = tau * 4 + dt
                        vector.wait_ge(sM.h, bM7 + (gdt + 1) * 16)
                        if gdt >= 1:
                            vector.wait_ge(sY.h, bY7 + 64 * gdt)
                        for tc in range(4):
                            vector.tensor_tensor(out=ost[tc][:, :], in0=PS4[tc][:, :],
                                                 in1=xch[tc][:, 512 * dt:512 * (dt + 1)], op=ALU.add)
                        sV.inc(vector.tensor_copy(out=ost[0][0:1, 0:1], in_=ost[0][0:1, 0:1]))

            @block.gpsimd
            def _(gpsimd):
                for tau in range(NTT):
                    for dt in range(4):
                        gdt = tau * 4 + dt
                        gpsimd.wait_ge(sV.h, bV7 + gdt + 1)
                        for tc in range(4):
                            r0 = tau * TT + 128 * tc
                            sY.dma(gpsimd.dma_start(out=y_d[r0:r0 + 128, 512 * dt:512 * (dt + 1)],
                                                    in_=ost[tc][:, :]))
            p7_Y = sY.n
            p7_M = sM.n
            p7_V = sV.n

        if stage <= 5:
            @block.gpsimd
            def _(gpsimd):
                gpsimd.wait_ge(sY.h, sY.n)
            return nc

        with ExitStack() as p8:
            E8 = p8.enter_context
            w8pT = [[E8(nc.sbuf_tensor(f"p8t{ss}_{i}", [128, 512], BF16, is_transpose_dest=True)) for i in range(32)] for ss in range(2)]
            h8slab = E8(nc.sbuf_tensor("p8hs", [128, 4096], BF16))
            acc = [E8(nc.sbuf_tensor(f"p8a{i}", [128, 512], F32)) for i in range(16)]
            stg8 = [E8(nc.sbuf_tensor(f"p8o{i}", [128, 512], F32)) for i in range(4)]
            tmp8 = [E8(nc.sbuf_tensor(f"p8tm{i}", [128, 512], F32)) for i in range(2)]

            pjk9 = E8(nc.sbuf_tensor("p8jk9", [128, 1], F32))
            pjk92 = E8(nc.sbuf_tensor("p8jk92", [128, 1], F32))

            @block.vector
            def _(vector):
                vector.wait_ge(sV.h, p7_V)
                sV.inc(vector.memset(pjk92[:, :], 0.0))

            bL8, bM8, bV8, bY8 = sL.n, sM.n, sV.n, sY.n
            sSp = [sS, sS2]
            bS8p = [sS.n, sS2.n]
            bHp = [s.n for s in sHp]
            HPT = 32 * 16

            @block.scalar
            def _(scalar):
                scalar.wait_ge(sG.h, p6_done_G)
                scalar.wait_ge(sY.h, p7_Y)
                scalar.wait_ge(sM.h, p7_M)
                scalar.wait_ge(sV.h, p7_V)
                for dt in range(4):
                    for jh in range(2):
                        djh = dt * 2 + jh
                        if djh == 0:
                            scalar.wait_ge(sPG.h, 16 * NPROD)
                        elif djh >= 2:
                            scalar.wait_ge(sM.h, bM8 + (djh - 1) * 128)
                        for jc in range(32):
                            jg = jh * 32 + jc
                            sSp[djh % 2].dma(scalar.dma_start(out=w8pT[djh % 2][jc][:, :],
                                                              in_=w8pb_s[512 * dt:512 * (dt + 1), 128 * jg:128 * (jg + 1)],
                                                              transpose=True))

            @block.sync
            def _(sync):
                sync.wait_ge(sG.h, p6_done_G)
                sync.wait_ge(sY.h, p7_Y)
                sync.wait_ge(sM.h, p7_M)
                sync.wait_ge(sV.h, p7_V)
                sync.wait_ge(sPG.h, 16 * NPROD)
                g = 0
                for dt in range(4):
                    for jh in range(2):
                        for tau in range(NTT):
                            for jc2 in range(16):
                                if g >= 4:
                                    sync.wait_ge(sM.h, bM8 + 2 * g - 6)
                                jg = jh * 32 + 2 * jc2
                                sHp[g % 4].dma(sync.dma_start(
                                    out=h8slab[:, 1024 * (g % 4):1024 * (g % 4 + 1)].rearrange(
                                        "p (c f) -> p c f", f=512),
                                    in_=h8_s[128 * jg:128 * (jg + 2),
                                             512 * tau:512 * (tau + 1)].rearrange(
                                        "(c p) f -> p c f", p=128)))
                                g += 1


            @block.tensor
            def _(tensor):
                tensor.wait_ge(sV.h, p7_V)
                gidx = 0
                ldx = 0
                for dt in range(4):
                    for jh in range(2):
                        djh = dt * 2 + jh
                        tensor.wait_ge(sSp[djh % 2].h, bS8p[djh % 2] + 32 * 16 * (djh // 2 + 1))
                        for tau in range(NTT):
                            if gidx >= 1:
                                tensor.wait_ge(sV.h, bV8 + gidx)
                            for jc in range(32):
                                if ldx % 2 == 0:
                                    gg = ldx // 2
                                    tensor.wait_ge(sHp[gg % 4].h,
                                                   bHp[gg % 4] + 16 * (gg // 4 + 1))
                                last = None
                                sb = 512 * (ldx % 8)
                                for tc in range(4):
                                    last = tensor.matmul(PS4[tc][:, :],
                                                         h8slab[:, sb + 128 * tc:sb + 128 * (tc + 1)],
                                                         w8pT[djh % 2][jc][:, :],
                                                         start=(jc == 0), stop=(jc == 31))
                                sM.inc(last)
                                ldx += 1
                            gidx += 1

            @block.vector
            def _(vector):
                gidx = 0
                for dt in range(4):
                    for jh in range(2):
                        for tau in range(NTT):
                            vector.wait_ge(sM.h, bM8 + (gidx + 1) * 32)
                            if jh == 0:
                                for tc in range(4):
                                    vector.tensor_copy(out=acc[tau * 4 + tc][:, :], in_=PS4[tc][:, :])
                            else:
                                g1 = dt * 4 + tau
                                if g1 >= 1:
                                    vector.wait_ge(sY.h, bY8 + 64 * g1)
                                for pr in range(2):
                                    t0, t1 = 2 * pr, 2 * pr + 1
                                    vector.tensor_tensor(out=tmp8[0][:, :], in0=PS4[t0][:, :],
                                                         in1=acc[tau * 4 + t0][:, :], op=ALU.add)
                                    vector.tensor_tensor(out=tmp8[1][:, :], in0=PS4[t1][:, :],
                                                         in1=acc[tau * 4 + t1][:, :], op=ALU.add)
                                    vector.tensor_scalar(out=stg8[t0][:, :], in0=tmp8[0][:, :],
                                                         scalar1=scal[:, 3:4], scalar2=None, op0=ALU.mult)
                                    vector.tensor_scalar(out=stg8[t1][:, :], in0=tmp8[1][:, :],
                                                         scalar1=scal[:, 3:4], scalar2=None, op0=ALU.mult)
                            sV.inc(vector.tensor_copy(out=pjk9[:, :], in_=pjk92[:, :]))
                            gidx += 1

            @block.gpsimd
            def _(gpsimd):
                gpsimd.wait_ge(sY.h, p7_Y)
                gidx = 0
                for dt in range(4):
                    for jh in range(2):
                        for tau in range(NTT):
                            gidx += 1
                            if jh == 0:
                                continue
                            gpsimd.wait_ge(sV.h, bV8 + gidx)
                            for tc in range(4):
                                r0 = tau * TT + 128 * tc
                                sY.dma(gpsimd.dma_start(out=y_d[r0:r0 + 128, 512 * dt:512 * (dt + 1)],
                                                        in_=stg8[tc][:, :], accum_op=ALU.add))

        @block.gpsimd
        def _(gpsimd):
            gpsimd.wait_ge(sY.h, sY.n)
            if debug:
                for i in range(J // 128):
                    sY.dma(gpsimd.dma_start(out=dbg["h8"][128 * i:128 * (i + 1), :],
                                            in_=h8_s[128 * i:128 * (i + 1), :]))
                for i in range(D // 128):
                    sY.dma(gpsimd.dma_start(out=dbg["w8p"][128 * i:128 * (i + 1), :],
                                            in_=w8pb_s[128 * i:128 * (i + 1), :]))
                sY.dma(gpsimd.dma_start(out=dbg["sc2"][:, :], in_=scal[0:1, 0:8]))
                for hl in range(2):
                    for r in range(NTC):
                        sY.dma(gpsimd.dma_start(out=dbg["oh"][hl, 128 * r:128 * (r + 1), :],
                                                in_=oh_s[hl, 128 * r:128 * (r + 1), :]))
                    for r in range(NH * (DH + DHR) // 128):
                        sY.dma(gpsimd.dma_start(out=dbg["q"][hl, 128 * r:128 * (r + 1), :],
                                                in_=q_s[hl, 128 * r:128 * (r + 1), :]))
                        sY.dma(gpsimd.dma_start(out=dbg["k"][hl, 128 * r:128 * (r + 1), :],
                                                in_=k_s[hl, 128 * r:128 * (r + 1), :]))
                    for r in range(L // 128):
                        sY.dma(gpsimd.dma_start(out=dbg["v"][hl, 128 * r:128 * (r + 1), :],
                                                in_=v_s[hl, 128 * r:128 * (r + 1), :]))
                gpsimd.wait_ge(sY.h, sY.n)
        return nc

        if stage <= 4:
            @block.gpsimd
            def _(gpsimd):
                gpsimd.wait_ge(sG.h, p6_done_G)
                for i in range(NTC):
                    sY.dma(gpsimd.dma_start(out=y_d[128 * i:128 * (i + 1), :],
                                            in_=xnhi[i][:, 0:2048]))
                if debug:
                    for hl in range(2):
                        for r in range(NTC):
                            sY.dma(gpsimd.dma_start(out=dbg["oh"][hl, 128 * r:128 * (r + 1), :],
                                                    in_=oh_s[hl, 128 * r:128 * (r + 1), :]))
                gpsimd.wait_ge(sY.h, sY.n)
            return nc

        if stage <= 3:
            @block.gpsimd
            def _(gpsimd):
                gpsimd.wait_ge(sG.h, p4_done_G)
                for i in range(NTC):
                    sY.dma(gpsimd.dma_start(out=y_d[128 * i:128 * (i + 1), :],
                                            in_=xnhi[i][:, 0:2048]))
                gpsimd.wait_ge(sY.h, sY.n)
            return nc

        if stage <= 2:
            @block.gpsimd
            def _(gpsimd):
                gpsimd.wait_ge(sG.h, p3_done_G)
                for i in range(NTC):
                    sY.dma(gpsimd.dma_start(out=y_d[128 * i:128 * (i + 1), :],
                                            in_=xnhi[i][:, 0:2048]))
                if debug:
                    for hl in range(2):
                        for r in range(NH * (DH + DHR) // 128):
                            sY.dma(gpsimd.dma_start(out=dbg["q"][hl, 128 * r:128 * (r + 1), :],
                                                    in_=q_s[hl, 128 * r:128 * (r + 1), :]))
                            sY.dma(gpsimd.dma_start(out=dbg["k"][hl, 128 * r:128 * (r + 1), :],
                                                    in_=k_s[hl, 128 * r:128 * (r + 1), :]))
                        for r in range(NTC):
                            sY.dma(gpsimd.dma_start(out=dbg["v"][hl, 128 * r:128 * (r + 1), :],
                                                    in_=v_s[hl, 128 * r:128 * (r + 1), :]))
                    gpsimd.wait_ge(cc.h, cc1_done)
                    sG.dma(gpsimd.dma_start(out=sone[:, :], in_=cc1_o[:, :]))
                    gpsimd.wait_ge(sG.h, sG.n)
                    sY.dma(gpsimd.dma_start(out=dbg["sc"][:, :], in_=sone[:, :]))
                gpsimd.wait_ge(sY.h, sY.n)
            return nc

        raise NotImplementedError("stages beyond 1 not yet implemented")

    return nc



def _rope_rt():
    rt = np.zeros((128, 128), np.float32)
    for blk in (0, 64):
        for i in range(32):
            rt[blk + i + 32, blk + i] = -1.0
            rt[blk + i, blk + i + 32] = 1.0
    return rt


def _rope_tables():
    inv = (1.0 / (10000.0 ** (np.arange(0, DHR, 2, dtype=np.float32) / DHR))).astype(np.float32)
    ang = (np.arange(L, dtype=np.float32)[None, :] * inv[:, None]).astype(np.float32)
    return np.cos(ang).astype(np.float32), np.sin(ang).astype(np.float32)


def make_inputs(L0_attn_x, L0_attn_W_D_Q, L0_attn_W_U_Q, L0_attn_W_Q_R, L0_attn_W_D_KV,
                L0_attn_W_U_K, L0_attn_W_K_R, L0_attn_W_U_V, L0_attn_W_O,
                L0_ffn_w_fc, L0_ffn_w_proj):
    cos_t, sin_t = _rope_tables()
    consts = dict(
        w_dq=np.ascontiguousarray(L0_attn_W_D_Q, np.float32),
        w_uq=np.ascontiguousarray(L0_attn_W_U_Q, np.float32),
        w_qr=np.ascontiguousarray(L0_attn_W_Q_R, np.float32),
        w_dkv=np.ascontiguousarray(L0_attn_W_D_KV, np.float32),
        w_uk=np.ascontiguousarray(L0_attn_W_U_K, np.float32),
        w_kr=np.ascontiguousarray(L0_attn_W_K_R, np.float32),
        w_uv=np.ascontiguousarray(L0_attn_W_U_V, np.float32),
        w_o=np.ascontiguousarray(L0_attn_W_O, np.float32),
        w_fc=np.ascontiguousarray(L0_ffn_w_fc, np.float32),
        w_pj=np.ascontiguousarray(L0_ffn_w_proj, np.float32),
        cos_t=cos_t, sin_t=sin_t,
        ones_r=np.ones((1, 128), np.float32),
        ones_cb=np.ones((128, 128), ml_dtypes.bfloat16),
        rope_rt=_rope_rt(),
    )
    x = np.asarray(L0_attn_x, dtype=np.float32)
    return [dict(consts, x=np.ascontiguousarray(x[c])) for c in range(N_CORES)]


def kernel(**inputs):
    in_maps = make_inputs(**inputs)
    nc = build_nc()
    res = run_bass_kernel_spmd(nc, in_maps, core_ids=list(range(N_CORES)))
    return np.stack([res.results[c]["y"] for c in range(N_CORES)], axis=0)


# version 18
# speedup vs baseline: 1.2956x; 1.0090x over previous
import math
import os
import numpy as np
import ml_dtypes
import concourse.bass as bass
import concourse.mybir as mybir
from concourse.bass_utils import run_bass_kernel_spmd
from contextlib import ExitStack

F32 = mybir.dt.float32
BF16 = mybir.dt.bfloat16
F8 = mybir.dt.float8e4
AF = mybir.ActivationFunctionType
ALU = mybir.AluOpType
AX = mybir.AxisListType

N_CORES = 8
L = 2048
D = 2048
NH, DH, DHR = 16, 128, 64
DC, DCQ = 512, 1536
J = 8192
TT = 512
NTT = L // TT
NTC = L // 128
ND = D // 128
INV_SQRT_DH = 1.0 / math.sqrt(DH)

STAGE_FULL = 99
P6_NH = int(os.environ.get("P6_NH", str(NH)))


class Sem:
    def __init__(self, h):
        self.h = h
        self.n = 0

    def inc(self, ins, k=1):
        ins.then_inc(self.h, k)
        self.n += k
        return ins

    def dma(self, ins):
        return self.inc(ins, 16)


def build_nc(stage=STAGE_FULL, debug=False):
    nc = bass.Bass("TRN2", target_bir_lowering=False, debug=False,
                   detect_race_conditions=False)

    x_d = nc.dram_tensor("x", [L, D], F32, kind="ExternalInput")
    wdq_d = nc.dram_tensor("w_dq", [D, DCQ], F32, kind="ExternalInput")
    wuq_d = nc.dram_tensor("w_uq", [DCQ, NH * DH], F32, kind="ExternalInput")
    wqr_d = nc.dram_tensor("w_qr", [DCQ, NH * DHR], F32, kind="ExternalInput")
    wdkv_d = nc.dram_tensor("w_dkv", [D, DC], F32, kind="ExternalInput")
    wuk_d = nc.dram_tensor("w_uk", [DC, NH * DH], F32, kind="ExternalInput")
    wkr_d = nc.dram_tensor("w_kr", [D, NH * DHR], F32, kind="ExternalInput")
    wuv_d = nc.dram_tensor("w_uv", [DC, NH * DH], F32, kind="ExternalInput")
    wo_d = nc.dram_tensor("w_o", [NH * DH, D], F32, kind="ExternalInput")
    wfc_d = nc.dram_tensor("w_fc", [J, D], F32, kind="ExternalInput")
    wpj_d = nc.dram_tensor("w_pj", [D, J], F32, kind="ExternalInput")
    cos_d = nc.dram_tensor("cos_t", [32, L], F32, kind="ExternalInput")
    sin_d = nc.dram_tensor("sin_t", [32, L], F32, kind="ExternalInput")
    ones_r_d = nc.dram_tensor("ones_r", [1, 128], F32, kind="ExternalInput")
    ones_cb_d = nc.dram_tensor("ones_cb", [128, 128], BF16, kind="ExternalInput")
    rope_rt_d = nc.dram_tensor("rope_rt", [128, 128], F32, kind="ExternalInput")
    y_d = nc.dram_tensor("y", [L, D], F32, kind="ExternalOutput")
    dbg = {}
    if debug:
        dbg["xn"] = nc.dram_tensor("dbg_xn", [L, D], F32, kind="ExternalOutput")
        dbg["sc"] = nc.dram_tensor("dbg_sc", [1, 8], F32, kind="ExternalOutput")
        dbg["q"] = nc.dram_tensor("dbg_q", [2, NH * (DH + DHR), L], BF16, kind="ExternalOutput")
        dbg["k"] = nc.dram_tensor("dbg_k", [2, NH * (DH + DHR), L], BF16, kind="ExternalOutput")
        dbg["v"] = nc.dram_tensor("dbg_v", [2, L, NH * DH], BF16, kind="ExternalOutput")
        dbg["h"] = nc.dram_tensor("dbg_h", [J, L], F32, kind="ExternalOutput")
        dbg["oh"] = nc.dram_tensor("dbg_oh", [2, NH * DH, L], BF16, kind="ExternalOutput")
        dbg["h8"] = nc.dram_tensor("dbg_h8", [J, L], BF16, kind="ExternalOutput")
        dbg["w8p"] = nc.dram_tensor("dbg_w8p", [D, J], BF16, kind="ExternalOutput")
        dbg["sc2"] = nc.dram_tensor("dbg_sc2", [1, 8], F32, kind="ExternalOutput")

    xnhi_s = nc.dram_tensor("xnhi_s", [L, D], BF16)
    xnlo_s = nc.dram_tensor("xnlo_s", [L, D], BF16)
    wsplit = {}
    for nm, dd in (("wdq", (D, DCQ)), ("wuq", (DCQ, NH * DH)), ("wqr", (DCQ, NH * DHR)),
                   ("wdkv", (D, DC)), ("wuk", (DC, NH * DH)), ("wkr", (D, NH * DHR)),
                   ("wuv", (DC, NH * DH)), ("wo", (NH * DH, D))):
        wsplit[nm] = (nc.dram_tensor(nm + "_hi", list(dd), BF16),
                      nc.dram_tensor(nm + "_lo", list(dd), BF16))
    q_s = nc.dram_tensor("q_s", [2, NH * (DH + DHR), L], BF16)
    k_s = nc.dram_tensor("k_s", [2, NH * (DH + DHR), L], BF16)
    v_s = nc.dram_tensor("v_s", [2, L, NH * DH], BF16)
    oh_s = nc.dram_tensor("oh_s", [2, NH * DH, L], BF16)
    w8b_s = nc.dram_tensor("w8b_s", [J, D], BF16)
    w8pb_s = nc.dram_tensor("w8pb_s", [D, J], BF16)
    h_s = nc.dram_tensor("h_s", [J, L], F32)
    h8_s = nc.dram_tensor("h8_s", [J, L], BF16)
    cc1_i = nc.dram_tensor("cc1_i", [1, 8], F32)
    cc1_o = nc.dram_tensor("cc1_o", [1, 8], F32)
    cc2_i = nc.dram_tensor("cc2_i", [1, 8], F32)
    cc2_o = nc.dram_tensor("cc2_o", [1, 8], F32)
    row_s = nc.dram_tensor("row_s", [1, 128], F32)
    bc_s = nc.dram_tensor("bc_s", [1, 8], F32)

    with ExitStack() as es:
        E = es.enter_context
        sL = Sem(E(nc.semaphore("sL")))
        sS = Sem(E(nc.semaphore("sS")))
        sG = Sem(E(nc.semaphore("sG")))
        sM = Sem(E(nc.semaphore("sM")))
        sV = Sem(E(nc.semaphore("sV")))
        sC = Sem(E(nc.semaphore("sC")))
        sY = Sem(E(nc.semaphore("sY")))
        cc = Sem(E(nc.semaphore("cc")))
        sPC = Sem(E(nc.semaphore("sPC")))
        sPV = Sem(E(nc.semaphore("sPV")))
        sPG = Sem(E(nc.semaphore("sPG")))
        sS2 = Sem(E(nc.semaphore("sS2")))
        sHp = [Sem(E(nc.semaphore(f"sH{i}"))) for i in range(4)]
        block = E(nc.Block())

        xnhi = [E(nc.sbuf_tensor(f"xnhi{i}", [128, L], BF16, is_transpose_dest=True)) for i in range(ND)]
        ones_r = E(nc.sbuf_tensor("sb_ones_r", [1, 128], F32))
        ones_cb = E(nc.sbuf_tensor("sb_ones_cb", [128, 128], BF16))
        rope_rt = E(nc.sbuf_tensor("sb_rope_rt", [128, 128], F32))
        scal = E(nc.sbuf_tensor("scal", [128, 8], F32))
        smax = E(nc.sbuf_tensor("smax", [128, 8], F32))
        srow = E(nc.sbuf_tensor("srow", [1, 128], F32))
        sone = E(nc.sbuf_tensor("sone", [1, 8], F32))
        spers = E(nc.sbuf_tensor("spers", [1, 2], F32))
        pA = E(nc.psum_tensor("pA", [128, 512], F32))
        pB = E(nc.psum_tensor("pB", [128, 512], F32))
        pC = E(nc.psum_tensor("pC", [128, 512], F32))
        pE = E(nc.psum_tensor("pE", [128, 512], F32))
        pD = E(nc.psum_tensor("pD", [1, 512], F32))
        pBC = E(nc.psum_tensor("pBC", [128, 512], F32))
        p3_scope = ExitStack()
        xnlo = [p3_scope.enter_context(nc.sbuf_tensor(f"xnlo{i}", [128, L], BF16, is_transpose_dest=True)) for i in range(ND)]
        cos128 = p3_scope.enter_context(nc.sbuf_tensor("cos128", [128, L], F32))
        sin128 = p3_scope.enter_context(nc.sbuf_tensor("sin128", [128, L], F32))

        @block.sync
        def _(sync):
            for r in range(4):
                sL.dma(sync.dma_start(out=cos128[32 * r:32 * (r + 1), :], in_=cos_d[:, :]))
                sL.dma(sync.dma_start(out=sin128[32 * r:32 * (r + 1), :], in_=sin_d[:, :]))
            sL.dma(sync.dma_start(out=ones_r[:, :], in_=ones_r_d[:, :]))
            sL.dma(sync.dma_start(out=ones_cb[:, :], in_=ones_cb_d[:, :]))
            sL.dma(sync.dma_start(out=rope_rt[:, :], in_=rope_rt_d[:, :]))
        p0_consts = sL.n

        @block.vector
        def _(vector):
            vector.memset(smax[:, :], 0.0)
            sV.inc(vector.memset(scal[:, :], 0.0))
        p0_vec = sV.n

        with ExitStack() as p1:
            E1 = p1.enter_context
            xt = [E1(nc.sbuf_tensor(f"p1x{i}", [128, D], F32)) for i in range(2)]
            sq = E1(nc.sbuf_tensor("p1sq", [128, D], F32))
            xnt = [E1(nc.sbuf_tensor(f"p1xn{i}", [128, D], F32)) for i in range(2)]
            hi_t = [E1(nc.sbuf_tensor(f"p1hi{i}", [128, D], BF16)) for i in range(2)]
            lo_t = [E1(nc.sbuf_tensor(f"p1lo{i}", [128, D], BF16)) for i in range(2)]
            st1 = E1(nc.sbuf_tensor("p1s1", [128, 1], F32))
            st2 = E1(nc.sbuf_tensor("p1s2", [128, 1], F32))
            st16 = E1(nc.sbuf_tensor("p1s16", [128, 16], F32))
            st16b = E1(nc.sbuf_tensor("p1s16b", [128, 16], F32))
            mm_ = E1(nc.sbuf_tensor("p1m", [128, 1], F32))
            ms_ = E1(nc.sbuf_tensor("p1ms", [128, 1], F32))
            var_ = E1(nc.sbuf_tensor("p1v", [128, 1], F32))
            var2_ = E1(nc.sbuf_tensor("p1v2", [128, 1], F32))
            var3_ = E1(nc.sbuf_tensor("p1v3", [128, 1], F32))
            junk_ = E1(nc.sbuf_tensor("p1jk", [128, 1], F32))
            junk2_ = E1(nc.sbuf_tensor("p1jk2", [128, 1], F32))
            sd_ = E1(nc.sbuf_tensor("p1sd", [128, 1], F32))
            rstd_ = E1(nc.sbuf_tensor("p1rs", [128, 1], F32))
            tmx = E1(nc.sbuf_tensor("p1mx", [128, 1], F32))

            bL, bS, bV, bC = sL.n, sS.n, sV.n, sC.n
            VPI = 6
            SPI = 3 if debug else 2

            @block.sync
            def _(sync):
                for i in range(NTC):
                    if i >= 2:
                        sync.wait_ge(sV.h, bV + (i - 1) * VPI)
                    sL.dma(sync.dma_start(out=xt[i % 2][:, :], in_=x_d[128 * i:128 * (i + 1), :]))

            @block.scalar
            def _(scalar):
                for i in range(NTC):
                    scalar.wait_ge(sV.h, bV + i * VPI + 1)
                    sC.inc(scalar.activation(out=sq[:, :], in_=xnt[i % 2][:, :], func=AF.Square))
                    scalar.wait_ge(sV.h, bV + i * VPI + 2)
                    sC.inc(scalar.activation(out=sd_[:, :], in_=var3_[:, :], func=AF.Sqrt))
                    scalar.wait_ge(sV.h, bV + i * VPI + 6)
                    sS.dma(scalar.dma_start(out=xnhi_s[128 * i:128 * (i + 1), :], in_=hi_t[i % 2][:, :]))
                    sS.dma(scalar.dma_start(out=xnlo_s[128 * i:128 * (i + 1), :], in_=lo_t[i % 2][:, :]))
                    if debug:
                        sS.dma(scalar.dma_start(out=dbg["xn"][128 * i:128 * (i + 1), :], in_=xnt[i % 2][:, :]))

            @block.vector
            def _(vector):
                vector.memset(junk2_[:, :], 0.0)
                for i in range(NTC):
                    vector.wait_ge(sL.h, bL + 16 * (i + 1))
                    vector.tensor_reduce(out=st16[:, :],
                                         in_=xt[i % 2][:, :].rearrange("p (a b) -> p a b", b=128),
                                         axis=AX.X, op=ALU.add)
                    vector.tensor_copy(out=junk_[:, :], in_=junk2_[:, :])
                    vector.reduce_sum(out=st1[:, :], in_=st16[:, :], axis=AX.X)
                    vector.tensor_copy(out=junk_[:, :], in_=junk2_[:, :])
                    vector.tensor_scalar(out=mm_[:, :], in0=st1[:, :], scalar1=1.0 / D, scalar2=None, op0=ALU.mult)
                    vector.tensor_copy(out=junk_[:, :], in_=junk2_[:, :])
                    if i >= 2:
                        vector.wait_ge(sS.h, bS + (i - 1) * SPI * 16)
                    sV.inc(vector.tensor_scalar(out=xnt[i % 2][:, :], in0=xt[i % 2][:, :], scalar1=mm_[:, :],
                                                scalar2=None, op0=ALU.subtract))
                    vector.wait_ge(sC.h, bC + i * 2 + 1)
                    vector.tensor_reduce(out=st16b[:, :],
                                         in_=sq[:, :].rearrange("p (a b) -> p a b", b=128),
                                         axis=AX.X, op=ALU.add)
                    vector.tensor_copy(out=junk_[:, :], in_=junk2_[:, :])
                    vector.reduce_sum(out=st2[:, :], in_=st16b[:, :], axis=AX.X)
                    vector.tensor_copy(out=junk_[:, :], in_=junk2_[:, :])
                    sV.inc(vector.tensor_scalar(out=var3_[:, :], in0=st2[:, :], scalar1=1.0 / D,
                                                scalar2=1e-5, op0=ALU.mult, op1=ALU.add))
                    vector.wait_ge(sC.h, bC + i * 2 + 2)
                    vector.reciprocal(out=rstd_[:, :], in_=sd_[:, :])
                    vector.tensor_copy(out=junk_[:, :], in_=junk2_[:, :])
                    vector.tensor_tensor(out=var_[:, :], in0=sd_[:, :], in1=rstd_[:, :], op=ALU.mult)
                    vector.tensor_copy(out=junk_[:, :], in_=junk2_[:, :])
                    vector.tensor_scalar(out=var2_[:, :], in0=var_[:, :], scalar1=-1.0,
                                         scalar2=2.0, op0=ALU.mult, op1=ALU.add)
                    vector.tensor_copy(out=junk_[:, :], in_=junk2_[:, :])
                    sV.inc(vector.tensor_tensor(out=mm_[:, :], in0=rstd_[:, :], in1=var2_[:, :],
                                                op=ALU.mult))
                    vector.tensor_copy(out=junk_[:, :], in_=junk2_[:, :])
                    vector.tensor_scalar(out=xnt[i % 2][:, :], in0=xnt[i % 2][:, :], scalar1=mm_[:, :],
                                         scalar2=None, op0=ALU.mult)
                    vector.tensor_copy(out=junk_[:, :], in_=junk2_[:, :])
                    vector.reduce_max(out=tmx[:, :], in_=xnt[i % 2][:, :], axis=AX.X, apply_absolute_value=True)
                    a, b = (0, 1) if i % 2 == 0 else (1, 0)
                    sV.inc(vector.tensor_copy(out=hi_t[i % 2][:, :], in_=xnt[i % 2][:, :]))
                    sV.inc(vector.tensor_tensor(out=smax[:, b:b + 1], in0=smax[:, a:a + 1],
                                                in1=tmx[:, :], op=ALU.max))
                    sV.inc(vector.scalar_tensor_tensor(out=lo_t[i % 2][:, :], in0=xnt[i % 2][:, :], scalar=1.0,
                                                       in1=hi_t[i % 2][:, :], op0=ALU.mult, op1=ALU.subtract))

            p1_spills = sS.n
            p1_vec = sV.n

            @block.sync
            def _(sync):
                sync.wait_ge(sS.h, p1_spills)
                for dc_ in range(ND):
                    sL.dma(sync.dma_start(out=xnhi[dc_][:, :],
                                          in_=xnhi_s[:, 128 * dc_:128 * (dc_ + 1)], transpose=True))
                    sL.dma(sync.dma_start(out=xnlo[dc_][:, :],
                                          in_=xnlo_s[:, 128 * dc_:128 * (dc_ + 1)], transpose=True))
            p1_xn_ready = sL.n

        @block.gpsimd
        def _(gpsimd):
            gpsimd.wait_ge(sV.h, p1_vec)
            sG.dma(gpsimd.dma_start(out=row_s[:, :], in_=smax[:, 0:1]))
            gpsimd.wait_ge(sG.h, sG.n)
            sG.dma(gpsimd.dma_start(out=srow[:, :], in_=row_s[:, :]))
        p2_row = sG.n

        @block.vector
        def _(vector):
            vector.wait_ge(sG.h, p2_row)
            vector.memset(sone[:, :], 0.0)
            sV.inc(vector.reduce_max(out=sone[:, 0:1], in_=srow[:, :], axis=AX.X))
        p2_lmax = sV.n

        @block.gpsimd
        def _(gpsimd):
            gpsimd.wait_ge(sV.h, p2_lmax)
            sG.dma(gpsimd.dma_start(out=cc1_i[:, :], in_=sone[:, :]))
            gpsimd.wait_ge(sG.h, sG.n)
            cc.inc(gpsimd.collective_compute(
                "AllReduce", ALU.max, replica_groups=[list(range(N_CORES))],
                ins=[cc1_i.ap().opt()], outs=[cc1_o.ap().opt()]))
        cc1_done = cc.n

        if stage <= 1:
            @block.gpsimd
            def _(gpsimd):
                gpsimd.wait_ge(sL.h, p1_xn_ready)
                for i in range(NTC):
                    sY.dma(gpsimd.dma_start(out=y_d[128 * i:128 * (i + 1), :],
                                            in_=xnhi[i][:, 0:2048]))
                if debug:
                    gpsimd.wait_ge(cc.h, cc1_done)
                    sG.dma(gpsimd.dma_start(out=sone[:, :], in_=cc1_o[:, :]))
                    gpsimd.wait_ge(sG.h, sG.n)
                    sY.dma(gpsimd.dma_start(out=dbg["sc"][:, :], in_=sone[:, :]))
                gpsimd.wait_ge(sY.h, sY.n)
            return nc

        with ExitStack() as pw:
            EW = pw.enter_context
            wf32 = [EW(nc.sbuf_tensor(f"wsp{i}", [128, 2048], F32)) for i in range(2)]
            whib = [EW(nc.sbuf_tensor(f"wsph{i}", [128, 2048], BF16)) for i in range(2)]
            wlob = [EW(nc.sbuf_tensor(f"wspl{i}", [128, 2048], BF16)) for i in range(2)]
            wjk = EW(nc.sbuf_tensor("wspj", [128, 1], F32))
            wjk2 = EW(nc.sbuf_tensor("wspj2", [128, 1], F32))
            splits = [("wdq", wdq_d, D, DCQ), ("wuq", wuq_d, DCQ, NH * DH), ("wqr", wqr_d, DCQ, NH * DHR),
                      ("wdkv", wdkv_d, D, DC), ("wuk", wuk_d, DC, NH * DH), ("wkr", wkr_d, D, NH * DHR),
                      ("wuv", wuv_d, DC, NH * DH), ("wo", wo_d, NH * DH, D)]
            chunks = []
            for nm, dram, rows, cols in splits:
                for r in range(rows // 128):
                    chunks.append((nm, dram, r, cols))
            NWC = len(chunks)
            bL, bS, bV = sL.n, sS.n, sV.n

            @block.sync
            def _(sync):
                for n, (nm, dram, r, cols) in enumerate(chunks):
                    if n >= 2:
                        sync.wait_ge(sV.h, bV + (n - 1) * 2)
                    sL.dma(sync.dma_start(out=wf32[n % 2][:, 0:cols], in_=dram[128 * r:128 * (r + 1), :]))

            @block.vector
            def _(vector):
                vector.memset(wjk2[:, :], 0.0)
                for n, (nm, dram, r, cols) in enumerate(chunks):
                    vector.wait_ge(sL.h, bL + 16 * (n + 1))
                    if n >= 2:
                        vector.wait_ge(sS.h, bS + (n - 1) * 2 * 16)
                    sV.inc(vector.tensor_copy(out=whib[n % 2][:, 0:cols], in_=wf32[n % 2][:, 0:cols]))
                    vector.tensor_copy(out=wjk[:, :], in_=wjk2[:, :])
                    sV.inc(vector.scalar_tensor_tensor(out=wlob[n % 2][:, 0:cols], in0=wf32[n % 2][:, 0:cols],
                                                       scalar=1.0, in1=whib[n % 2][:, 0:cols],
                                                       op0=ALU.mult, op1=ALU.subtract))

            @block.scalar
            def _(scalar):
                for n, (nm, dram, r, cols) in enumerate(chunks):
                    hi_d, lo_d = wsplit[nm]
                    scalar.wait_ge(sV.h, bV + n * 2 + 2)
                    sS.dma(scalar.dma_start(out=hi_d[128 * r:128 * (r + 1), :], in_=whib[n % 2][:, 0:cols]))
                    sS.dma(scalar.dma_start(out=lo_d[128 * r:128 * (r + 1), :], in_=wlob[n % 2][:, 0:cols]))
            wsplit_done = sS.n

        with ExitStack() as p3:
            E3 = p3.enter_context
            rg_hi = [E3(nc.sbuf_tensor(f"p3rh{i}", [128, 512], BF16)) for i in range(2)]
            rg_lo = [E3(nc.sbuf_tensor(f"p3rl{i}", [128, 512], BF16)) for i in range(2)]
            qc_hi = [E3(nc.sbuf_tensor(f"p3qh{i}", [128, 512], BF16)) for i in range(DCQ // 128)]
            qc_lo = [E3(nc.sbuf_tensor(f"p3ql{i}", [128, 512], BF16)) for i in range(DCQ // 128)]
            c_hi = [E3(nc.sbuf_tensor(f"p3ch{i}", [128, 512], BF16)) for i in range(DC // 128)]
            c_lo = [E3(nc.sbuf_tensor(f"p3cl{i}", [128, 512], BF16)) for i in range(DC // 128)]
            stg_hi = [[E3(nc.sbuf_tensor(f"p3sh{ss}_{i}", [128, 512], BF16)) for i in range(4)] for ss in range(2)]
            stg_lo = [[E3(nc.sbuf_tensor(f"p3sl{ss}_{i}", [128, 512], BF16)) for i in range(4)] for ss in range(2)]
            rtmp = [E3(nc.sbuf_tensor(f"p3rt{i}", [128, 512], F32)) for i in range(2)]
            rb = [E3(nc.sbuf_tensor(f"p3rb{i}", [128, 512], F32)) for i in range(2)]
            pj = E3(nc.sbuf_tensor("p3jk", [128, 1], F32))
            pj2 = E3(nc.sbuf_tensor("p3jk2", [128, 1], F32))
            PS4 = [pA, pB, pC, pE]

            @block.vector
            def _(vector):
                sV.inc(vector.memset(pj2[:, :], 0.0))

            chain = dict(V=sV.n, M=sM.n)

            def emit_proj(tau, wname, K, cols, moving_hi, moving_lo, mslice, evac,
                          row0=0, scr=None, dst=None):
                whi_d, wlo_d = wsplit[wname]
                NG = cols // 512
                tok = tau * TT
                bL0, bM0, bV0, bG0 = sL.n, sM.n, sV.n, sG.n
                pre_V, pre_M = chain["V"], chain["M"]
                VPG = 3 if evac == "rope" else 2
                TPG = K + (4 if evac == "rope" else 0)

                @block.sync
                def _(sync):
                    sync.wait_ge(sS.h, wsplit_done)
                    for g in range(NG):
                        for k in range(K):
                            gk = g * K + k
                            if gk >= 2:
                                g2, k2 = divmod(gk - 2, K)
                                sync.wait_ge(sM.h, bM0 + g2 * TPG + k2 + 1)
                            else:
                                sync.wait_ge(sM.h, bM0)
                            sL.dma(sync.dma_start(out=rg_hi[gk % 2][:, :],
                                                  in_=whi_d[128 * k:128 * (k + 1), 512 * g:512 * (g + 1)]))
                            sL.dma(sync.dma_start(out=rg_lo[gk % 2][:, :],
                                                  in_=wlo_d[128 * k:128 * (k + 1), 512 * g:512 * (g + 1)]))

                @block.tensor
                def _(tensor):
                    tensor.wait_ge(sL.h, p1_xn_ready)
                    tensor.wait_ge(sV.h, pre_V)
                    for g in range(NG):
                        if g >= 1:
                            tensor.wait_ge(sV.h, bV0 + g * VPG)
                        for k in range(K):
                            gk = g * K + k
                            tensor.wait_ge(sL.h, bL0 + 32 * (gk + 1))
                            e = gk % 2
                            mh = moving_hi[k][:, mslice]
                            ml = moving_lo[k][:, mslice]
                            for oc in range(4):
                                wh = rg_hi[e][:, 128 * oc:128 * (oc + 1)]
                                wl = rg_lo[e][:, 128 * oc:128 * (oc + 1)]
                                tensor.matmul(PS4[oc][:, :], wh, mh, start=(k == 0), stop=False)
                                tensor.matmul(PS4[oc][:, :], wh, ml, start=False, stop=False)
                                ins = tensor.matmul(PS4[oc][:, :], wl, mh, start=False, stop=(k == K - 1))
                            sM.inc(ins)
                        if evac == "rope":
                            for half in range(2):
                                tensor.wait_ge(sV.h, bV0 + g * VPG + half + 1)
                                for j in range(2):
                                    sM.inc(tensor.matmul(PS4[2 * half + j][:, :], rope_rt[:, :],
                                                         rtmp[j][:, :], start=True, stop=True))

                @block.vector
                def _(vector):
                    vector.wait_ge(sM.h, pre_M)
                    for g in range(NG):
                        ss = g % 2
                        if evac != "local":
                            vector.wait_ge(sG.h, bG0 if g < 2 else bG0 + (g - 1) * 8 * 16)
                        if evac == "rope":
                            cs = cos128[:, tok:tok + TT]
                            sn = sin128[:, tok:tok + TT]
                            for half in range(2):
                                if half == 0:
                                    vector.wait_ge(sM.h, bM0 + g * TPG + K)
                                o0, o1 = 2 * half, 2 * half + 1
                                vector.tensor_copy(out=rtmp[0][:, :], in_=PS4[o0][:, :])
                                vector.tensor_copy(out=rtmp[1][:, :], in_=PS4[o1][:, :])
                                sV.inc(vector.tensor_copy(out=pj[:, :], in_=pj2[:, :]))
                                vector.wait_ge(sM.h, bM0 + g * TPG + K + 2 * (half + 1))
                                vector.tensor_tensor(out=rtmp[0][:, :], in0=rtmp[0][:, :], in1=cs, op=ALU.mult)
                                vector.tensor_tensor(out=rtmp[1][:, :], in0=rtmp[1][:, :], in1=cs, op=ALU.mult)
                                vector.tensor_tensor(out=rb[0][:, :], in0=PS4[o0][:, :], in1=sn, op=ALU.mult)
                                vector.tensor_tensor(out=rb[1][:, :], in0=PS4[o1][:, :], in1=sn, op=ALU.mult)
                                vector.tensor_tensor(out=rtmp[0][:, :], in0=rtmp[0][:, :], in1=rb[0][:, :], op=ALU.add)
                                vector.tensor_tensor(out=rtmp[1][:, :], in0=rtmp[1][:, :], in1=rb[1][:, :], op=ALU.add)
                                vector.tensor_copy(out=stg_hi[ss][o0][:, :], in_=rtmp[0][:, :])
                                vector.tensor_copy(out=stg_hi[ss][o1][:, :], in_=rtmp[1][:, :])
                                vector.scalar_tensor_tensor(out=stg_lo[ss][o0][:, :], in0=rtmp[0][:, :],
                                                            scalar=1.0, in1=stg_hi[ss][o0][:, :],
                                                            op0=ALU.mult, op1=ALU.subtract)
                                vector.scalar_tensor_tensor(out=stg_lo[ss][o1][:, :], in0=rtmp[1][:, :],
                                                            scalar=1.0, in1=stg_hi[ss][o1][:, :],
                                                            op0=ALU.mult, op1=ALU.subtract)
                            sV.inc(vector.tensor_copy(out=pj[:, :], in_=pj2[:, :]))
                        else:
                            vector.wait_ge(sM.h, bM0 + (g + 1) * TPG)
                            if evac == "local":
                                hv = [dst[0][g * 4 + oc] for oc in range(4)]
                                lv = [dst[1][g * 4 + oc] for oc in range(4)]
                            else:
                                hv, lv = stg_hi[ss], stg_lo[ss]
                            for oc in range(4):
                                vector.tensor_copy(out=hv[oc][:, :], in_=PS4[oc][:, :])
                            sV.inc(vector.tensor_copy(out=pj[:, :], in_=pj2[:, :]))
                            for oc in range(4):
                                vector.scalar_tensor_tensor(out=lv[oc][:, :], in0=PS4[oc][:, :],
                                                            scalar=1.0, in1=hv[oc][:, :],
                                                            op0=ALU.mult, op1=ALU.subtract)
                            sV.inc(vector.tensor_copy(out=pj[:, :], in_=pj2[:, :]))

                if evac != "local":
                    @block.gpsimd
                    def _(gpsimd):
                        for g in range(NG):
                            ss = g % 2
                            gpsimd.wait_ge(sV.h, bV0 + (g + 1) * VPG)
                            for oc in range(4):
                                rr = row0 + 512 * g + 128 * oc
                                sG.dma(gpsimd.dma_start(out=scr[0, rr:rr + 128, tok:tok + TT],
                                                        in_=stg_hi[ss][oc][:, :]))
                                sG.dma(gpsimd.dma_start(out=scr[1, rr:rr + 128, tok:tok + TT],
                                                        in_=stg_lo[ss][oc][:, :]))

                chain["V"], chain["M"] = sV.n, sM.n

            def emit_v(tau):
                whi_d, wlo_d = wsplit["wuv"]
                NG = 4
                K = 4
                tok = tau * TT
                bL0, bM0, bV0, bG0 = sL.n, sM.n, sV.n, sG.n
                pre_V, pre_M = chain["V"], chain["M"]
                VPG, TPG = 2, K

                @block.sync
                def _(sync):
                    sync.wait_ge(sS.h, wsplit_done)
                    for g in range(NG):
                        for k in range(K):
                            gk = g * K + k
                            if gk >= 2:
                                g2, k2 = divmod(gk - 2, K)
                                sync.wait_ge(sM.h, bM0 + g2 * TPG + k2 + 1)
                            else:
                                sync.wait_ge(sM.h, bM0)
                            sL.dma(sync.dma_start(out=rg_hi[gk % 2][:, :],
                                                  in_=whi_d[128 * k:128 * (k + 1), 512 * g:512 * (g + 1)]))
                            sL.dma(sync.dma_start(out=rg_lo[gk % 2][:, :],
                                                  in_=wlo_d[128 * k:128 * (k + 1), 512 * g:512 * (g + 1)]))

                @block.tensor
                def _(tensor):
                    tensor.wait_ge(sV.h, pre_V)
                    for g in range(NG):
                        if g >= 1:
                            tensor.wait_ge(sV.h, bV0 + g * VPG)
                        for k in range(K):
                            gk = g * K + k
                            tensor.wait_ge(sL.h, bL0 + 32 * (gk + 1))
                            e = gk % 2
                            for tc in range(4):
                                ch = c_hi[k][:, 128 * tc:128 * (tc + 1)]
                                cl = c_lo[k][:, 128 * tc:128 * (tc + 1)]
                                tensor.matmul(PS4[tc][:, :], ch, rg_hi[e][:, :], start=(k == 0), stop=False)
                                tensor.matmul(PS4[tc][:, :], cl, rg_hi[e][:, :], start=False, stop=False)
                                ins = tensor.matmul(PS4[tc][:, :], ch, rg_lo[e][:, :], start=False, stop=(k == K - 1))
                            sM.inc(ins)

                @block.vector
                def _(vector):
                    vector.wait_ge(sM.h, pre_M)
                    for g in range(NG):
                        ss = g % 2
                        vector.wait_ge(sG.h, bG0 if g < 2 else bG0 + (g - 1) * 8 * 16)
                        vector.wait_ge(sM.h, bM0 + (g + 1) * TPG)
                        for tc in range(4):
                            vector.tensor_copy(out=stg_hi[ss][tc][:, :], in_=PS4[tc][:, :])
                        sV.inc(vector.tensor_copy(out=pj[:, :], in_=pj2[:, :]))
                        for tc in range(4):
                            vector.scalar_tensor_tensor(out=stg_lo[ss][tc][:, :], in0=PS4[tc][:, :],
                                                        scalar=1.0, in1=stg_hi[ss][tc][:, :],
                                                        op0=ALU.mult, op1=ALU.subtract)
                        sV.inc(vector.tensor_copy(out=pj[:, :], in_=pj2[:, :]))

                @block.gpsimd
                def _(gpsimd):
                    for g in range(NG):
                        ss = g % 2
                        gpsimd.wait_ge(sV.h, bV0 + (g + 1) * VPG)
                        for tc in range(4):
                            tr = tok + 128 * tc
                            sG.dma(gpsimd.dma_start(out=v_s[0, tr:tr + 128, 512 * g:512 * (g + 1)],
                                                    in_=stg_hi[ss][tc][:, :]))
                            sG.dma(gpsimd.dma_start(out=v_s[1, tr:tr + 128, 512 * g:512 * (g + 1)],
                                                    in_=stg_lo[ss][tc][:, :]))

                chain["V"], chain["M"] = sV.n, sG.n if False else sM.n

            ts_all = slice(None)
            for tau in range(NTT):
                msl = slice(tau * TT, tau * TT + TT)
                emit_proj(tau, "wdq", ND, DCQ, xnhi, xnlo, msl, "local", dst=(qc_hi, qc_lo))
                emit_proj(tau, "wuq", DCQ // 128, NH * DH, qc_hi, qc_lo, ts_all, "spill", row0=0, scr=q_s)
                emit_proj(tau, "wqr", DCQ // 128, NH * DHR, qc_hi, qc_lo, ts_all, "rope", row0=2048, scr=q_s)
                emit_proj(tau, "wdkv", ND, DC, xnhi, xnlo, msl, "local", dst=(c_hi, c_lo))
                emit_proj(tau, "wuk", DC // 128, NH * DH, c_hi, c_lo, ts_all, "spill", row0=0, scr=k_s)
                emit_proj(tau, "wkr", ND, NH * DHR, xnhi, xnlo, msl, "rope", row0=2048, scr=k_s)
                emit_v(tau)
            p3_done_G = sG.n
            p3_done_V = sV.n
            p3_done_M = sM.n

        def emit_wscan(w_dram, rows, cols, colpair):
            bL0, bS0, bV0 = sL.n, sS.n, sV.n
            n_r, n_c = rows // 128, cols // 2048
            NCH = n_r * n_c
            a0, b0 = colpair

            def src(n):
                return w_dram[128 * (n // n_c):128 * (n // n_c + 1),
                              2048 * (n % n_c):2048 * (n % n_c + 1)]

            @block.sync
            def _(sync):
                sync.wait_ge(sM.h, p3_done_M)
                sync.wait_ge(sG.h, p3_done_G)
                for n in range(0, NCH, 2):
                    if n >= 4:
                        sync.wait_ge(sV.h, bV0 + n - 3)
                    sL.dma(sync.dma_start(out=wsc[n % 4][:, :], in_=src(n)))

            @block.scalar
            def _(scalar):
                scalar.wait_ge(sM.h, p3_done_M)
                scalar.wait_ge(sG.h, p3_done_G)
                for n in range(1, NCH, 2):
                    if n >= 4:
                        scalar.wait_ge(sV.h, bV0 + n - 3)
                    sS.dma(scalar.dma_start(out=wsc[n % 4][:, :], in_=src(n)))

            @block.vector
            def _(vector):
                for n in range(NCH):
                    if n % 2 == 0:
                        vector.wait_ge(sL.h, bL0 + 16 * (n // 2 + 1))
                    else:
                        vector.wait_ge(sS.h, bS0 + 16 * (n // 2 + 1))
                    sV.inc(vector.reduce_max(out=wmx[n % 2][:, :], in_=wsc[n % 4][:, :],
                                             axis=AX.X, apply_absolute_value=True))
                    if n >= 1:
                        c_ = n - 1
                        aa = a0 if c_ % 2 == 0 else b0
                        bb = b0 if c_ % 2 == 0 else a0
                        vector.tensor_tensor(out=smax[:, bb:bb + 1], in0=smax[:, aa:aa + 1],
                                             in1=wmx[(n - 1) % 2][:, :], op=ALU.max)
                vector.tensor_copy(out=pjk[:, :], in_=pjk2[:, :])
                c_ = NCH - 1
                aa = a0 if c_ % 2 == 0 else b0
                bb = b0 if c_ % 2 == 0 else a0
                sV.inc(vector.tensor_tensor(out=smax[:, bb:bb + 1], in0=smax[:, aa:aa + 1],
                                            in1=wmx[(NCH - 1) % 2][:, :], op=ALU.max))
            return b0 if (NCH - 1) % 2 == 0 else a0

        with ExitStack() as pws:
            EW = pws.enter_context
            wsc = [EW(nc.sbuf_tensor(f"wsc{i}", [128, 2048], F32)) for i in range(4)]
            wmx = [EW(nc.sbuf_tensor(f"wmx{i}", [128, 1], F32)) for i in range(2)]
            sgx = EW(nc.sbuf_tensor("sgx", [1, 8], F32))
            wm1 = EW(nc.sbuf_tensor("wm1", [1, 1], F32))
            sxv = EW(nc.sbuf_tensor("sxv", [1, 1], F32))
            sw1v = EW(nc.sbuf_tensor("sw1v", [1, 1], F32))
            rsx = EW(nc.sbuf_tensor("rsx", [1, 1], F32))
            rsw1 = EW(nc.sbuf_tensor("rsw1", [1, 1], F32))
            scrow = EW(nc.sbuf_tensor("scrow", [1, 8], F32))
            pjk = EW(nc.sbuf_tensor("wsjk", [128, 1], F32))
            pjk2 = EW(nc.sbuf_tensor("wsjk2", [128, 1], F32))

            @block.vector
            def _(vector):
                sV.inc(vector.memset(pjk2[:, :], 0.0))
                vector.memset(scrow[:, :], 0.0)
            col_w1 = emit_wscan(wfc_d, J, D, (4, 5))
            ws_done_V = sV.n

            @block.gpsimd
            def _(gpsimd):
                gpsimd.wait_ge(sV.h, ws_done_V)
                sG.dma(gpsimd.dma_start(out=row_s[:, :], in_=smax[:, col_w1:col_w1 + 1]))
                gpsimd.wait_ge(sG.h, sG.n)
                sG.dma(gpsimd.dma_start(out=srow[:, :], in_=row_s[:, :]))
                gpsimd.wait_ge(sG.h, sG.n)
                gpsimd.wait_ge(cc.h, cc1_done)
                sG.dma(gpsimd.dma_start(out=sgx[:, :], in_=cc1_o[:, :]))
            sc1_g = sG.n

            @block.vector
            def _(vector):
                vector.wait_ge(sG.h, sc1_g)
                sV.inc(vector.reduce_max(out=wm1[:, :], in_=srow[:, :], axis=AX.X))
            sc1_v = sV.n

            @block.vector
            def _(vector):
                F = lambda: vector.tensor_copy(out=pjk[:, :], in_=pjk2[:, :])
                vector.wait_ge(sG.h, sc1_g)
                vector.tensor_scalar(out=sxv[:, :], in0=sgx[0:1, 0:1], scalar1=1.0 / 448.0,
                                     scalar2=1e-6, op0=ALU.mult, op1=ALU.max)
                vector.tensor_scalar(out=sw1v[:, :], in0=wm1[:, :], scalar1=1.0 / 448.0,
                                     scalar2=1e-6, op0=ALU.mult, op1=ALU.max)
                F()
                vector.reciprocal(out=rsx[:, :], in_=sxv[:, :])
                vector.reciprocal(out=rsw1[:, :], in_=sw1v[:, :])
                F()
                vector.tensor_scalar(out=scrow[0:1, 0:1], in0=rsx[:, :], scalar1=0.5, scalar2=None, op0=ALU.mult)
                vector.tensor_scalar(out=scrow[0:1, 4:5], in0=rsw1[:, :], scalar1=0.5, scalar2=None, op0=ALU.mult)
                vector.tensor_scalar(out=scrow[0:1, 1:2], in0=sxv[:, :], scalar1=sw1v[0:1, 0:1],
                                     scalar2=4.0, op0=ALU.mult, op1=ALU.mult)
                F()
                sV.inc(vector.tensor_copy(out=pjk[:, :], in_=pjk2[:, :]))
            sc3_v = sV.n

            @block.gpsimd
            def _(gpsimd):
                gpsimd.wait_ge(sV.h, sc3_v)
                sG.dma(gpsimd.dma_start(out=bc_s[:, :], in_=scrow[:, :]))
                gpsimd.wait_ge(sG.h, sG.n)
                sG.dma(gpsimd.dma_start(out=scal[:, :], in_=bc_s.ap().broadcast_to([128, 8])))
            scales1_g = sG.n

        with ExitStack() as p4a:
            E4a = p4a.enter_context
            x32 = [E4a(nc.sbuf_tensor(f"p4x{i}", [128, L], F32)) for i in range(2)]
            xf8 = [E4a(nc.sbuf_tensor(f"p4f{i}", [128, L], F8)) for i in range(2)]
            bV0, bC0 = sV.n, sC.n

            @block.vector
            def _(vector):
                vector.wait_ge(sG.h, scales1_g)
                vector.wait_ge(sM.h, p3_done_M)
                for d in range(ND):
                    if d >= 2:
                        vector.wait_ge(sC.h, bC0 + d - 1)
                    sV.inc(vector.tensor_tensor(out=x32[d % 2][:, :], in0=xnhi[d][:, :],
                                                in1=xnlo[d][:, :], op=ALU.add))
                    if d >= 1:
                        vector.wait_ge(sC.h, bC0 + d)
                        sV.inc(vector.tensor_copy(out=xnhi[d - 1][:, :], in_=xf8[(d - 1) % 2][:, :]))
                vector.wait_ge(sC.h, bC0 + ND)
                sV.inc(vector.tensor_copy(out=xnhi[ND - 1][:, :], in_=xf8[(ND - 1) % 2][:, :]))

            @block.scalar
            def _(scalar):
                for d in range(ND):
                    scalar.wait_ge(sV.h, bV0 + 2 * d + 1)
                    sC.inc(scalar.activation(out=xf8[d % 2][:, :], in_=x32[d % 2][:, :],
                                             func=AF.Copy, scale=scal[:, 0:1]))
            p4a_V = sV.n
        p3_scope.close()

        with ExitStack() as p4:
            E4 = p4.enter_context
            wf32r = [E4(nc.sbuf_tensor(f"p4w{i}", [128, D], F32)) for i in range(2)]
            wf8r = [E4(nc.sbuf_tensor(f"p4q{i}", [128, D], F8)) for i in range(2)]
            wb16r = [E4(nc.sbuf_tensor(f"p4b{i}", [128, D], BF16)) for i in range(2)]
            w8T = [[E4(nc.sbuf_tensor(f"p4t{ss}_{i}", [128, 512], BF16, is_transpose_dest=True))
                    for i in range(ND)] for ss in range(2)]
            hrelu = [E4(nc.sbuf_tensor(f"p4hr{i}", [128, 512], F32)) for i in range(2)]
            htile = [E4(nc.sbuf_tensor(f"p4ht{i}", [128, 512], F32)) for i in range(2)]
            hmx = [E4(nc.sbuf_tensor(f"p4hm{i}", [128, 1], F32)) for i in range(2)]
            pjk = E4(nc.sbuf_tensor("p4jk", [128, 1], F32))
            pjk2 = E4(nc.sbuf_tensor("p4jk2", [128, 1], F32))
            wps = E4(nc.sbuf_tensor("wps", [128, 8192], F32))
            wmx2 = [E4(nc.sbuf_tensor(f"wmx2_{i}", [128, 1], F32)) for i in range(2)]
            wm2 = E4(nc.sbuf_tensor("wm2", [1, 1], F32))
            sw2v = E4(nc.sbuf_tensor("sw2v", [1, 1], F32))
            rsw2 = E4(nc.sbuf_tensor("rsw2", [1, 1], F32))
            bHW2 = [sm.n for sm in sHp]

            def wscan2_issue(eng, p):
                sHp[p % 4].dma(eng.dma_start(
                    out=wps[:, 4096 * (p % 2):4096 * (p % 2 + 1)],
                    in_=wpj_d[128 * (p // 2):128 * (p // 2 + 1),
                              4096 * (p % 2):4096 * (p % 2 + 1)]))
            ws2 = {"p": 0, "pend": [], "cnt": 0}

            def wscan2_chain(vector, widx):
                c_ = ws2["cnt"]
                aa = 6 if c_ % 2 == 0 else 7
                bb = 7 if c_ % 2 == 0 else 6
                sS2.inc(vector.tensor_tensor(out=smax[:, bb:bb + 1], in0=smax[:, aa:aa + 1],
                                             in1=wmx2[widx][:, :], op=ALU.max))
                ws2["cnt"] += 1

            def wscan2_quantum(vector):
                p = ws2["p"]
                if p >= 32:
                    return
                vector.wait_ge(sHp[p % 4].h, bHW2[p % 4] + 16 * (p // 4 + 1))
                sS2.inc(vector.reduce_max(out=wmx2[p % 2][:, :],
                                          in_=wps[:, 4096 * (p % 2):4096 * (p % 2 + 1)],
                                          axis=AX.X, apply_absolute_value=True))
                ws2["pend"].append(p % 2)
                if len(ws2["pend"]) >= 2:
                    wscan2_chain(vector, ws2["pend"].pop(0))
                ws2["p"] = p + 1

            @block.vector
            def _(vector):
                sV.inc(vector.memset(pjk2[:, :], 0.0))

            bL0, bC0b, bV0b, bG0 = sL.n, sC.n, sV.n, sG.n

            @block.sync
            def _(sync):
                sync.wait_ge(sV.h, p4a_V)
                for j in range(J // 128):
                    if j >= 2:
                        sync.wait_ge(sC.h, bC0b + j - 1)
                    sL.dma(sync.dma_start(out=wf32r[j % 2][:, :], in_=wfc_d[128 * j:128 * (j + 1), :]))

            @block.scalar
            def _(scalar):
                scalar.wait_ge(sG.h, scales1_g)
                for j in range(J // 128):
                    scalar.wait_ge(sL.h, bL0 + 16 * (j + 1))
                    if j >= 2:
                        scalar.wait_ge(sV.h, bV0b + j - 1)
                    sC.inc(scalar.activation(out=wf8r[j % 2][:, :], in_=wf32r[j % 2][:, :],
                                             func=AF.Copy, scale=scal[:, 4:5]))

            @block.vector
            def _(vector):
                for j in range(J // 128):
                    vector.wait_ge(sC.h, bC0b + j + 1)
                    if j >= 2:
                        vector.wait_ge(sG.h, bG0 + 16 * (j - 1))
                    sV.inc(vector.tensor_copy(out=wb16r[j % 2][:, :], in_=wf8r[j % 2][:, :]))

            @block.gpsimd
            def _(gpsimd):
                for j in range(J // 128):
                    gpsimd.wait_ge(sV.h, bV0b + j + 1)
                    sG.dma(gpsimd.dma_start(out=w8b_s[128 * j:128 * (j + 1), :], in_=wb16r[j % 2][:, :]))
            p4b_G = sG.n

            NJG = J // 512
            PPG = 16
            MTOT = NJG * PPG
            GPM = 32 if debug else 16
            bL1, bM1, bV1, bC1, bG1 = sL.n, sM.n, sV.n, sC.n, sG.n

            @block.sync
            def _(sync):
                for g in range(NJG):
                    sync.wait_ge(sG.h, bG0 + 16 * min(4 * (g + 1), J // 128))
                    if g >= 2:
                        sync.wait_ge(sM.h, bM1 + (g - 1) * PPG)
                    for d in range(ND):
                        sL.dma(sync.dma_start(out=w8T[g % 2][d][:, :],
                                              in_=w8b_s[512 * g:512 * (g + 1), 128 * d:128 * (d + 1)],
                                              transpose=True))

            @block.tensor
            def _(tensor):
                tensor.wait_ge(sV.h, p4a_V)
                tensor.wait_ge(sV.h, p3_done_V)
                m = 0
                for g in range(NJG):
                    tensor.wait_ge(sL.h, bL1 + 16 * ND * (g + 1))
                    for jc in range(4):
                        for t in range(NTT):
                            if m >= 2:
                                tensor.wait_ge(sC.h, bC1 + 2 * (m - 2) + 1)
                            p = pA if m % 2 == 0 else pB
                            for d in range(ND):
                                ins = tensor.matmul(p[:, :], w8T[g % 2][d][:, 128 * jc:128 * (jc + 1)],
                                                    xnhi[d][:, 512 * t:512 * (t + 1)],
                                                    start=(d == 0), stop=(d == ND - 1))
                            sM.inc(ins)
                            m += 1

            @block.scalar
            def _(scalar):
                for m in range(MTOT):
                    scalar.wait_ge(sM.h, bM1 + m + 1)
                    p = pA if m % 2 == 0 else pB
                    if m >= 3:
                        scalar.wait_ge(sG.h, bG1 + GPM * (m - 2))
                    sC.inc(scalar.activation(out=hrelu[m % 2][:, :], in_=p[:, :],
                                             func=AF.Relu, scale=scal[:, 1:2]))
                    if m >= 1:
                        sC.inc(scalar.activation(out=htile[(m - 1) % 2][:, :],
                                                 in_=hrelu[(m - 1) % 2][:, :], func=AF.Square))
                scalar.wait_ge(sG.h, bG1 + GPM * (MTOT - 2))
                sC.inc(scalar.activation(out=htile[(MTOT - 1) % 2][:, :],
                                         in_=hrelu[(MTOT - 1) % 2][:, :], func=AF.Square))

            @block.vector
            def _(vector):
                for m in range(MTOT):
                    vector.wait_ge(sC.h, bC1 + min(2 * m + 3, 2 * MTOT))
                    vector.reduce_max(out=hmx[m % 2][:, :], in_=htile[m % 2][:, :], axis=AX.X)
                    aa = 2 if m % 2 == 0 else 3
                    bb = 3 if m % 2 == 0 else 2
                    sV.inc(vector.tensor_tensor(out=smax[:, bb:bb + 1], in0=smax[:, aa:aa + 1],
                                                in1=hmx[(m - 1) % 2][:, :] if m >= 1 else smax[:, aa:aa + 1],
                                                op=ALU.max))
                    if m % 8 == 4:
                        wscan2_quantum(vector)
                vector.tensor_copy(out=pjk[:, :], in_=pjk2[:, :])
                m = MTOT
                aa = 2 if m % 2 == 0 else 3
                bb = 3 if m % 2 == 0 else 2
                sV.inc(vector.tensor_tensor(out=smax[:, bb:bb + 1], in0=smax[:, aa:aa + 1],
                                            in1=hmx[(m - 1) % 2][:, :], op=ALU.max))
                while ws2["p"] < 32:
                    wscan2_quantum(vector)
                if ws2["pend"]:
                    vector.tensor_copy(out=pjk[:, :], in_=pjk2[:, :])
                    wscan2_chain(vector, ws2["pend"].pop(0))
            p4c_hcol = 3 if MTOT % 2 == 0 else 2
            scan2_total = sS2.n

            @block.gpsimd
            def _(gpsimd):
                gpsimd.wait_ge(sV.h, p4a_V)
                wscan2_issue(gpsimd, 0)
                wscan2_issue(gpsimd, 1)
                for m in range(MTOT):
                    g, rem = divmod(m, PPG)
                    jc, t = divmod(rem, NTT)
                    gpsimd.wait_ge(sV.h, bV1 + m + 1)
                    rr = 512 * g + 128 * jc
                    sG.dma(gpsimd.dma_start(out=h_s[rr:rr + 128, 512 * t:512 * (t + 1)],
                                            in_=htile[m % 2][:, :]))
                    if debug:
                        sG.dma(gpsimd.dma_start(out=dbg["h"][rr:rr + 128, 512 * t:512 * (t + 1)],
                                                in_=htile[m % 2][:, :]))
                    if m % 8 == 4:
                        q = m // 8 + 2
                        if q < 32:
                            gpsimd.wait_ge(sS2.h, 2 * (q - 2) + 1)
                            wscan2_issue(gpsimd, q)
            @block.gpsimd
            def _(gpsimd):
                gpsimd.wait_ge(sS2.h, scan2_total)
                sG.dma(gpsimd.dma_start(out=row_s[:, :], in_=smax[:, 6:7]))
                gpsimd.wait_ge(sG.h, sG.n)
                sG.dma(gpsimd.dma_start(out=srow[:, :], in_=row_s[:, :]))
            sc2b_g = sG.n

            @block.vector
            def _(vector):
                F = lambda: vector.tensor_copy(out=pjk[:, :], in_=pjk2[:, :])
                vector.wait_ge(sG.h, sc2b_g)
                vector.reduce_max(out=wm2[:, :], in_=srow[:, :], axis=AX.X)
                F()
                vector.tensor_scalar(out=sw2v[:, :], in0=wm2[:, :], scalar1=1.0 / 448.0,
                                     scalar2=1e-6, op0=ALU.mult, op1=ALU.max)
                F()
                vector.reciprocal(out=rsw2[:, :], in_=sw2v[:, :])
                F()
                vector.tensor_copy(out=spers[0:1, 0:1], in_=sw2v[:, :])
                vector.tensor_scalar(out=spers[0:1, 1:2], in0=rsw2[:, :], scalar1=0.5,
                                     scalar2=None, op0=ALU.mult)
                F()
                sV.inc(vector.tensor_copy(out=pjk[:, :], in_=pjk2[:, :]))

            p4_done_G = sG.n
            p4_done_V = sV.n
            p4_done_C = sC.n
            p4_done_M = sM.n

        @block.gpsimd
        def _(gpsimd):
            gpsimd.wait_ge(sV.h, p4_done_V)
            sG.dma(gpsimd.dma_start(out=row_s[:, :], in_=smax[:, p4c_hcol:p4c_hcol + 1]))
            gpsimd.wait_ge(sG.h, sG.n)
            sG.dma(gpsimd.dma_start(out=srow[:, :], in_=row_s[:, :]))
        p5_row = sG.n

        @block.vector
        def _(vector):
            vector.wait_ge(sG.h, p5_row)
            vector.memset(sone[:, :], 0.0)
            sV.inc(vector.reduce_max(out=sone[0:1, 0:1], in_=srow[:, :], axis=AX.X))
        p5_lmax = sV.n

        @block.gpsimd
        def _(gpsimd):
            gpsimd.wait_ge(sV.h, p5_lmax)
            sG.dma(gpsimd.dma_start(out=cc2_i[:, :], in_=sone[:, :]))
            gpsimd.wait_ge(sG.h, sG.n)
            cc.inc(gpsimd.collective_compute(
                "AllReduce", ALU.max, replica_groups=[list(range(N_CORES))],
                ins=[cc2_i.ap().opt()], outs=[cc2_o.ap().opt()]))
        cc2_done = cc.n

        p8p = ExitStack()
        E8p = p8p.enter_context
        sgx2 = E8p(nc.sbuf_tensor("p8gx", [1, 8], F32))
        shv = E8p(nc.sbuf_tensor("p8sh", [1, 1], F32))
        rshv = E8p(nc.sbuf_tensor("p8rs", [1, 1], F32))
        scrow2 = E8p(nc.sbuf_tensor("p8sr", [1, 8], F32))
        pjk8 = E8p(nc.sbuf_tensor("p8jk", [128, 1], F32))
        pjk82 = E8p(nc.sbuf_tensor("p8jk2", [128, 1], F32))
        pf32 = [E8p(nc.sbuf_tensor(f"p8w{i}", [128, D], F32)) for i in range(2)]
        pf8 = [E8p(nc.sbuf_tensor(f"p8q{i}", [128, D], F8)) for i in range(2)]
        pb16 = [E8p(nc.sbuf_tensor(f"p8b{i}", [128, D], BF16)) for i in range(2)]
        bGpre = sG.n

        @block.vector
        def _(vector):
            vector.wait_ge(sG.h, bGpre)
            sV.inc(vector.memset(pjk82[:, :], 0.0))

        @block.gpsimd
        def _(gpsimd):
            gpsimd.wait_ge(cc.h, cc2_done)
            gpsimd.wait_ge(sV.h, p5_lmax)
            sG.dma(gpsimd.dma_start(out=sgx2[:, :], in_=cc2_o[:, :]))
        p8a_g = sG.n

        @block.vector
        def _(vector):
            F = lambda: vector.tensor_copy(out=pjk8[:, :], in_=pjk82[:, :])
            vector.wait_ge(sG.h, p8a_g)
            vector.memset(scrow2[:, :], 0.0)
            vector.tensor_scalar(out=shv[:, :], in0=sgx2[0:1, 0:1], scalar1=1.0 / 448.0,
                                 scalar2=1e-6, op0=ALU.mult, op1=ALU.max)
            F()
            vector.reciprocal(out=rshv[:, :], in_=shv[:, :])
            F()
            vector.tensor_scalar(out=scrow2[0:1, 2:3], in0=rshv[:, :], scalar1=0.5, scalar2=None, op0=ALU.mult)
            vector.tensor_scalar(out=scrow2[0:1, 3:4], in0=shv[:, :], scalar1=spers[0:1, 0:1],
                                 scalar2=4.0, op0=ALU.mult, op1=ALU.mult)
            vector.tensor_copy(out=scrow2[0:1, 5:6], in_=spers[0:1, 1:2])
            F()
            sV.inc(vector.tensor_copy(out=pjk8[:, :], in_=pjk82[:, :]))
        p8a_v = sV.n

        @block.gpsimd
        def _(gpsimd):
            gpsimd.wait_ge(sV.h, p8a_v)
            sG.dma(gpsimd.dma_start(out=bc_s[:, :], in_=scrow2[:, :]))
            gpsimd.wait_ge(sG.h, sG.n)
            sG.dma(gpsimd.dma_start(out=scal[:, :], in_=bc_s.ap().broadcast_to([128, 8])))
        p8a_scal = sG.n

        def prod_src(j):
            if j < 64:
                return h_s[128 * j:128 * (j + 1), :]
            c = j - 64
            return wpj_d[128 * (c // 4):128 * (c // 4 + 1), 2048 * (c % 4):2048 * (c % 4 + 1)]

        def prod_dst(j):
            if j < 64:
                return h8_s[128 * j:128 * (j + 1), :]
            c = j - 64
            return w8pb_s[128 * (c // 4):128 * (c // 4 + 1), 2048 * (c % 4):2048 * (c % 4 + 1)]

        def prod_col(j):
            return 2 if j < 64 else 5
        bSp = sS.n
        NPROD = J // 128 + 64

        with ExitStack() as p6:
            E6 = p6.enter_context
            qh128 = [[E6(nc.sbuf_tensor(f"q128{b}_{hl}", [128, L], BF16)) for hl in range(2)] for b in range(2)]
            qh64 = [[E6(nc.sbuf_tensor(f"q64{b}_{hl}", [64, L], BF16)) for hl in range(2)] for b in range(2)]
            kh128 = [[E6(nc.sbuf_tensor(f"k128{b}_{hl}", [128, L], BF16)) for hl in range(2)] for b in range(2)]
            kh64 = [[E6(nc.sbuf_tensor(f"k64{b}_{hl}", [64, L], BF16)) for hl in range(2)] for b in range(2)]
            vh = [[E6(nc.sbuf_tensor(f"vh{b}_{hl}", [128, L], BF16)) for hl in range(2)] for b in range(2)]
            p32 = [E6(nc.sbuf_tensor(f"p6p{i}", [128, 512], F32)) for i in range(2)]
            phi = [E6(nc.sbuf_tensor(f"p6h{i}", [128, 512], BF16)) for i in range(2)]
            plo = [E6(nc.sbuf_tensor(f"p6l{i}", [128, 512], BF16)) for i in range(2)]
            rbc = E6(nc.sbuf_tensor("p6rb", [128, 512], F32))
            avn = E6(nc.sbuf_tensor("p6av", [128, 512], F32))
            ohhi = [E6(nc.sbuf_tensor(f"p6oh{i}", [128, 512], BF16)) for i in range(2)]
            ohlo = [E6(nc.sbuf_tensor(f"p6ol{i}", [128, 512], BF16)) for i in range(2)]
            pjk = E6(nc.sbuf_tensor("p6jk", [128, 1], F32))
            pjk2 = E6(nc.sbuf_tensor("p6jk2", [128, 1], F32))

            @block.vector
            def _(vector):
                sV.inc(vector.memset(pjk2[:, :], 0.0))

            bL6, bM6, bV6, bC6, bG6 = sL.n, sM.n, sV.n, sC.n, sG.n
            MPH = 4 * 17
            CPH = 4 * 16
            VPQ = 16 + 3
            VPH = 4 * VPQ
            GPH = 8

            @block.sync
            def _(sync):
                sync.wait_ge(sG.h, p4_done_G)
                sync.wait_ge(sM.h, p4_done_M)
                sync.wait_ge(sC.h, p4_done_C)
                for h in range(P6_NH):
                    b = h % 2
                    if h >= 1:
                        sync.wait_ge(sM.h, bM6 + (h - 1) * MPH + 1)
                    for hl in range(2):
                        sL.dma(sync.dma_start(out=qh128[b][hl][:, :], in_=q_s[hl, 128 * h:128 * (h + 1), :]))
                        sL.dma(sync.dma_start(out=qh64[b][hl][:, :], in_=q_s[hl, 2048 + 64 * h:2048 + 64 * (h + 1), :]))
                        sL.dma(sync.dma_start(out=kh128[b][hl][:, :], in_=k_s[hl, 128 * h:128 * (h + 1), :]))
                        sL.dma(sync.dma_start(out=kh64[b][hl][:, :], in_=k_s[hl, 2048 + 64 * h:2048 + 64 * (h + 1), :]))
                        for c4 in range(4):
                            sL.dma(sync.dma_start(
                                out=vh[b][hl][:, 512 * c4:512 * (c4 + 1)].rearrange("p (c f) -> p c f", f=128),
                                in_=v_s[hl, 512 * c4:512 * (c4 + 1), 128 * h:128 * (h + 1)].rearrange("(c p) f -> p c f", p=128)))

            @block.tensor
            def _(tensor):
                tensor.wait_ge(sV.h, p4_done_V)
                tensor.wait_ge(sC.h, p4_done_C)
                for h in range(P6_NH):
                    b = h % 2
                    tensor.wait_ge(sL.h, bL6 + 16 * 16 * (h + 1))
                    for qt in range(4):
                        qsl = slice(512 * qt, 512 * (qt + 1))
                        for kt in range(16):
                            ksl = slice(128 * kt, 128 * (kt + 1))
                            ps = pA if kt % 2 == 0 else pB
                            if kt >= 2:
                                tensor.wait_ge(sC.h, bC6 + h * CPH + qt * 16 + kt - 1)
                            tensor.matmul(ps[:, :], kh128[b][0][:, ksl], qh128[b][0][:, qsl], start=True, stop=False)
                            tensor.matmul(ps[:, :], kh128[b][0][:, ksl], qh128[b][1][:, qsl], start=False, stop=False)
                            tensor.matmul(ps[:, :], kh128[b][1][:, ksl], qh128[b][0][:, qsl], start=False, stop=False)
                            tensor.matmul(ps[:, :], kh64[b][0][:, ksl], qh64[b][0][:, qsl], start=False, stop=False)
                            tensor.matmul(ps[:, :], kh64[b][0][:, ksl], qh64[b][1][:, qsl], start=False, stop=False)
                            sM.inc(tensor.matmul(ps[:, :], kh64[b][1][:, ksl], qh64[b][0][:, qsl], start=False, stop=True))
                            if kt >= 1:
                                kp = kt - 1
                                kpsl = slice(128 * kp, 128 * (kp + 1))
                                tensor.wait_ge(sV.h, bV6 + h * VPH + qt * VPQ + kp + 1)
                                e = kp % 2
                                tensor.matmul(pBC[:, :], ones_cb[:, :], phi[e][:, :],
                                              start=(kp == 0), stop=False)
                                tensor.matmul(pBC[:, :], ones_cb[:, :], plo[e][:, :],
                                              start=False, stop=(kp == 15))
                                tensor.matmul(pC[:, :], vh[b][0][:, kpsl], phi[e][:, :],
                                              start=(kp == 0), stop=False)
                                tensor.matmul(pC[:, :], vh[b][1][:, kpsl], phi[e][:, :], start=False, stop=False)
                                tensor.matmul(pC[:, :], vh[b][0][:, kpsl], plo[e][:, :],
                                              start=False, stop=(kp == 15))
                        kp = 15
                        kpsl = slice(128 * kp, 128 * (kp + 1))
                        tensor.wait_ge(sV.h, bV6 + h * VPH + qt * VPQ + kp + 1)
                        e = kp % 2
                        tensor.matmul(pBC[:, :], ones_cb[:, :], phi[e][:, :], start=(kp == 0), stop=False)
                        tensor.matmul(pBC[:, :], ones_cb[:, :], plo[e][:, :], start=False, stop=(kp == 15))
                        tensor.matmul(pC[:, :], vh[b][0][:, kpsl], phi[e][:, :], start=(kp == 0), stop=False)
                        tensor.matmul(pC[:, :], vh[b][1][:, kpsl], phi[e][:, :], start=False, stop=False)
                        sM.inc(tensor.matmul(pC[:, :], vh[b][0][:, kpsl], plo[e][:, :],
                                             start=False, stop=(kp == 15)))

            @block.scalar
            def _(scalar):
                for h in range(P6_NH):
                    for qt in range(4):
                        for kt in range(16):
                            scalar.wait_ge(sM.h, bM6 + h * MPH + qt * 17 + kt + 1)
                            if kt >= 2:
                                scalar.wait_ge(sV.h, bV6 + h * VPH + qt * VPQ + kt - 1)
                            ps = pA if kt % 2 == 0 else pB
                            sC.inc(scalar.activation(out=p32[kt % 2][:, :], in_=ps[:, :],
                                                     func=AF.Exp, scale=INV_SQRT_DH))
                        s_ = h * 4 + qt
                        if s_ == 0:
                            scalar.wait_ge(sG.h, p8a_scal)
                            for j in (0, 1):
                                sS.dma(scalar.dma_start(out=pf32[j % 2][:, :], in_=prod_src(j)))
                        for j in (2 * s_, 2 * s_ + 1):
                            scalar.wait_ge(sS.h, bSp + 16 * (2 * s_ + 2))
                            if j >= 2:
                                scalar.wait_ge(sPV.h, j - 1)
                            sPC.inc(scalar.activation(out=pf8[j % 2][:, :], in_=pf32[j % 2][:, :],
                                                      func=AF.Copy,
                                                      scale=scal[:, prod_col(j):prod_col(j) + 1]))
                        if s_ + 1 < NPROD // 2:
                            for j in (2 * s_ + 2, 2 * s_ + 3):
                                scalar.wait_ge(sPC.h, j - 1)
                                sS.dma(scalar.dma_start(out=pf32[j % 2][:, :], in_=prod_src(j)))

            @block.vector
            def _(vector):
                for h in range(P6_NH):
                    for qt in range(4):
                        for kt in range(16):
                            vector.wait_ge(sC.h, bC6 + h * CPH + qt * 16 + kt + 1)
                            e = kt % 2
                            vector.tensor_copy(out=phi[e][:, :], in_=p32[e][:, :])
                            vector.tensor_copy(out=pjk[:, :], in_=pjk2[:, :])
                            sV.inc(vector.scalar_tensor_tensor(out=plo[e][:, :], in0=p32[e][:, :],
                                                               scalar=1.0, in1=phi[e][:, :],
                                                               op0=ALU.mult, op1=ALU.subtract))
                        vector.wait_ge(sM.h, bM6 + h * MPH + qt * 17 + 17)
                        sV.inc(vector.reciprocal(out=rbc[:, :], in_=pBC[:, :]))
                        vector.tensor_copy(out=pjk[:, :], in_=pjk2[:, :])
                        sV.inc(vector.tensor_tensor(out=avn[:, :], in0=pC[:, :], in1=rbc[:, :],
                                                    op=ALU.mult))
                        if h >= 1 or qt >= 2:
                            vector.wait_ge(sG.h, bG6 + 2 * 16 * (h * 4 + qt))
                        hq = h * 4 + qt
                        vector.tensor_copy(out=ohhi[hq % 2][:, :], in_=avn[:, :])
                        vector.tensor_copy(out=pjk[:, :], in_=pjk2[:, :])
                        sV.inc(vector.scalar_tensor_tensor(out=ohlo[hq % 2][:, :], in0=avn[:, :], scalar=1.0,
                                                           in1=ohhi[hq % 2][:, :], op0=ALU.mult,
                                                           op1=ALU.subtract))
                        for j in (2 * hq, 2 * hq + 1):
                            vector.wait_ge(sPC.h, j + 1)
                            if j >= 2:
                                vector.wait_ge(sPG.h, 16 * j)
                            sPV.inc(vector.tensor_copy(out=pb16[j % 2][:, :], in_=pf8[j % 2][:, :]))

            @block.gpsimd
            def _(gpsimd):
                for h in range(P6_NH):
                    for qt in range(4):
                        hq = h * 4 + qt
                        gpsimd.wait_ge(sV.h, bV6 + h * VPH + qt * VPQ + VPQ)
                        qsl = slice(512 * qt, 512 * (qt + 1))
                        sG.dma(gpsimd.dma_start(out=oh_s[0, 128 * h:128 * (h + 1), qsl], in_=ohhi[hq % 2][:, :]))
                        sG.dma(gpsimd.dma_start(out=oh_s[1, 128 * h:128 * (h + 1), qsl], in_=ohlo[hq % 2][:, :]))
                        for j in (2 * hq, 2 * hq + 1):
                            gpsimd.wait_ge(sPV.h, j + 1)
                            sPG.dma(gpsimd.dma_start(out=prod_dst(j), in_=pb16[j % 2][:, :]))
            p6_done_G = sG.n
            p6_done_M = sM.n
            p6_done_V = sV.n
            p6_done_C = sC.n
        p8p.close()

        with ExitStack() as p7:
            E7 = p7.enter_context
            oht = [[E7(nc.sbuf_tensor(f"p7o{hl}_{i}", [128, 512], BF16)) for i in range(16)] for hl in range(2)]
            worh = [E7(nc.sbuf_tensor(f"p7wh{i}", [128, 512], BF16)) for i in range(2)]
            worl = [E7(nc.sbuf_tensor(f"p7wl{i}", [128, 512], BF16)) for i in range(2)]
            xch = [E7(nc.sbuf_tensor(f"p7x{i}", [128, D], F32)) for i in range(4)]
            ost = [E7(nc.sbuf_tensor(f"p7s{i}", [128, 512], F32)) for i in range(4)]
            bL7, bM7, bV7, bY7 = sL.n, sM.n, sV.n, sY.n
            wo_hi_d, wo_lo_d = wsplit["wo"]
            LPT = 164 * 16

            @block.sync
            def _(sync):
                sync.wait_ge(sG.h, p6_done_G)
                sync.wait_ge(sM.h, p6_done_M)
                sync.wait_ge(sC.h, p6_done_C)
                sync.wait_ge(sV.h, p6_done_V)
                sync.wait_ge(sPG.h, 16 * NPROD)
                for tau in range(NTT):
                    tok = tau * TT
                    if tau >= 1:
                        sync.wait_ge(sV.h, bV7 + 4 * tau)
                    for hd in range(16):
                        for hl in range(2):
                            sL.dma(sync.dma_start(out=oht[hl][hd][:, :],
                                                  in_=oh_s[hl, 128 * hd:128 * (hd + 1), tok:tok + TT]))
                    for tc in range(4):
                        sL.dma(sync.dma_start(out=xch[tc][:, :],
                                              in_=x_d[tok + 128 * tc:tok + 128 * (tc + 1), :]))
                    for dt in range(4):
                        for hd in range(16):
                            gk = dt * 16 + hd
                            if gk >= 2:
                                g2, k2 = divmod(gk - 2, 16)
                                sync.wait_ge(sM.h, bM7 + tau * 64 + g2 * 16 + k2 + 1)
                            else:
                                sync.wait_ge(sM.h, bM7 + tau * 64)
                            sL.dma(sync.dma_start(out=worh[gk % 2][:, :],
                                                  in_=wo_hi_d[128 * hd:128 * (hd + 1), 512 * dt:512 * (dt + 1)]))
                            sL.dma(sync.dma_start(out=worl[gk % 2][:, :],
                                                  in_=wo_lo_d[128 * hd:128 * (hd + 1), 512 * dt:512 * (dt + 1)]))

            @block.tensor
            def _(tensor):
                tensor.wait_ge(sV.h, p6_done_V)
                tensor.wait_ge(sC.h, p6_done_C)
                for tau in range(NTT):
                    tensor.wait_ge(sL.h, bL7 + LPT * tau + 36 * 16)
                    for dt in range(4):
                        gdt = tau * 4 + dt
                        if gdt >= 1:
                            tensor.wait_ge(sV.h, bV7 + gdt)
                        for hd in range(16):
                            gk = dt * 16 + hd
                            tensor.wait_ge(sL.h, bL7 + LPT * tau + 36 * 16 + 32 * (gk + 1))
                            e = gk % 2
                            last = None
                            for tc in range(4):
                                ohh = oht[0][hd][:, 128 * tc:128 * (tc + 1)]
                                ohl = oht[1][hd][:, 128 * tc:128 * (tc + 1)]
                                tensor.matmul(PS4[tc][:, :], ohh, worh[e][:, :], start=(hd == 0), stop=False)
                                tensor.matmul(PS4[tc][:, :], ohl, worh[e][:, :], start=False, stop=False)
                                last = tensor.matmul(PS4[tc][:, :], ohh, worl[e][:, :],
                                                     start=False, stop=(hd == 15))
                            sM.inc(last)

            @block.vector
            def _(vector):
                for tau in range(NTT):
                    for dt in range(4):
                        gdt = tau * 4 + dt
                        vector.wait_ge(sM.h, bM7 + (gdt + 1) * 16)
                        if gdt >= 1:
                            vector.wait_ge(sY.h, bY7 + 64 * gdt)
                        for tc in range(4):
                            vector.tensor_tensor(out=ost[tc][:, :], in0=PS4[tc][:, :],
                                                 in1=xch[tc][:, 512 * dt:512 * (dt + 1)], op=ALU.add)
                        sV.inc(vector.tensor_copy(out=ost[0][0:1, 0:1], in_=ost[0][0:1, 0:1]))

            @block.gpsimd
            def _(gpsimd):
                for tau in range(NTT):
                    for dt in range(4):
                        gdt = tau * 4 + dt
                        gpsimd.wait_ge(sV.h, bV7 + gdt + 1)
                        for tc in range(4):
                            r0 = tau * TT + 128 * tc
                            sY.dma(gpsimd.dma_start(out=y_d[r0:r0 + 128, 512 * dt:512 * (dt + 1)],
                                                    in_=ost[tc][:, :]))
            p7_Y = sY.n
            p7_M = sM.n
            p7_V = sV.n

        if stage <= 5:
            @block.gpsimd
            def _(gpsimd):
                gpsimd.wait_ge(sY.h, sY.n)
            return nc

        with ExitStack() as p8:
            E8 = p8.enter_context
            w8pT = [[E8(nc.sbuf_tensor(f"p8t{ss}_{i}", [128, 512], BF16, is_transpose_dest=True)) for i in range(32)] for ss in range(2)]
            h8slab = E8(nc.sbuf_tensor("p8hs", [128, 4096], BF16))
            acc = [E8(nc.sbuf_tensor(f"p8a{i}", [128, 512], F32)) for i in range(16)]
            stg8 = [E8(nc.sbuf_tensor(f"p8o{i}", [128, 512], F32)) for i in range(4)]
            tmp8 = [E8(nc.sbuf_tensor(f"p8tm{i}", [128, 512], F32)) for i in range(2)]

            pjk9 = E8(nc.sbuf_tensor("p8jk9", [128, 1], F32))
            pjk92 = E8(nc.sbuf_tensor("p8jk92", [128, 1], F32))

            @block.vector
            def _(vector):
                vector.wait_ge(sV.h, p7_V)
                sV.inc(vector.memset(pjk92[:, :], 0.0))

            bL8, bM8, bV8, bY8 = sL.n, sM.n, sV.n, sY.n
            sSp = [sS, sS2]
            bS8p = [sS.n, sS2.n]
            bHp = [s.n for s in sHp]
            HPT = 32 * 16

            @block.scalar
            def _(scalar):
                scalar.wait_ge(sG.h, p6_done_G)
                scalar.wait_ge(sY.h, p7_Y)
                scalar.wait_ge(sM.h, p7_M)
                scalar.wait_ge(sV.h, p7_V)
                for dt in range(4):
                    for jh in range(2):
                        djh = dt * 2 + jh
                        if djh == 0:
                            scalar.wait_ge(sPG.h, 16 * NPROD)
                        elif djh >= 2:
                            scalar.wait_ge(sM.h, bM8 + (djh - 1) * 128)
                        for jc in range(32):
                            jg = jh * 32 + jc
                            sSp[djh % 2].dma(scalar.dma_start(out=w8pT[djh % 2][jc][:, :],
                                                              in_=w8pb_s[512 * dt:512 * (dt + 1), 128 * jg:128 * (jg + 1)],
                                                              transpose=True))

            @block.sync
            def _(sync):
                sync.wait_ge(sG.h, p6_done_G)
                sync.wait_ge(sY.h, p7_Y)
                sync.wait_ge(sM.h, p7_M)
                sync.wait_ge(sV.h, p7_V)
                sync.wait_ge(sPG.h, 16 * NPROD)
                g = 0
                for dt in range(4):
                    for jh in range(2):
                        for tau in range(NTT):
                            for jc2 in range(16):
                                if g >= 4:
                                    sync.wait_ge(sM.h, bM8 + 2 * g - 6)
                                jg = jh * 32 + 2 * jc2
                                sHp[g % 4].dma(sync.dma_start(
                                    out=h8slab[:, 1024 * (g % 4):1024 * (g % 4 + 1)].rearrange(
                                        "p (c f) -> p c f", f=512),
                                    in_=h8_s[128 * jg:128 * (jg + 2),
                                             512 * tau:512 * (tau + 1)].rearrange(
                                        "(c p) f -> p c f", p=128)))
                                g += 1


            @block.tensor
            def _(tensor):
                tensor.wait_ge(sV.h, p7_V)
                gidx = 0
                ldx = 0
                for dt in range(4):
                    for jh in range(2):
                        djh = dt * 2 + jh
                        tensor.wait_ge(sSp[djh % 2].h, bS8p[djh % 2] + 32 * 16 * (djh // 2 + 1))
                        for tau in range(NTT):
                            if gidx >= 1:
                                tensor.wait_ge(sV.h, bV8 + gidx)
                            for jc in range(32):
                                if ldx % 2 == 0:
                                    gg = ldx // 2
                                    tensor.wait_ge(sHp[gg % 4].h,
                                                   bHp[gg % 4] + 16 * (gg // 4 + 1))
                                last = None
                                sb = 512 * (ldx % 8)
                                for tc in range(4):
                                    last = tensor.matmul(PS4[tc][:, :],
                                                         h8slab[:, sb + 128 * tc:sb + 128 * (tc + 1)],
                                                         w8pT[djh % 2][jc][:, :],
                                                         start=(jc == 0), stop=(jc == 31))
                                sM.inc(last)
                                ldx += 1
                            gidx += 1

            @block.vector
            def _(vector):
                gidx = 0
                for dt in range(4):
                    for jh in range(2):
                        for tau in range(NTT):
                            vector.wait_ge(sM.h, bM8 + (gidx + 1) * 32)
                            if jh == 0:
                                for tc in range(4):
                                    vector.tensor_copy(out=acc[tau * 4 + tc][:, :], in_=PS4[tc][:, :])
                            else:
                                g1 = dt * 4 + tau
                                if g1 >= 1:
                                    vector.wait_ge(sY.h, bY8 + 64 * g1)
                                for pr in range(2):
                                    t0, t1 = 2 * pr, 2 * pr + 1
                                    vector.tensor_tensor(out=tmp8[0][:, :], in0=PS4[t0][:, :],
                                                         in1=acc[tau * 4 + t0][:, :], op=ALU.add)
                                    vector.tensor_tensor(out=tmp8[1][:, :], in0=PS4[t1][:, :],
                                                         in1=acc[tau * 4 + t1][:, :], op=ALU.add)
                                    vector.tensor_scalar(out=stg8[t0][:, :], in0=tmp8[0][:, :],
                                                         scalar1=scal[:, 3:4], scalar2=None, op0=ALU.mult)
                                    vector.tensor_scalar(out=stg8[t1][:, :], in0=tmp8[1][:, :],
                                                         scalar1=scal[:, 3:4], scalar2=None, op0=ALU.mult)
                            sV.inc(vector.tensor_copy(out=pjk9[:, :], in_=pjk92[:, :]))
                            gidx += 1

            @block.gpsimd
            def _(gpsimd):
                gpsimd.wait_ge(sY.h, p7_Y)
                gidx = 0
                for dt in range(4):
                    for jh in range(2):
                        for tau in range(NTT):
                            gidx += 1
                            if jh == 0:
                                continue
                            gpsimd.wait_ge(sV.h, bV8 + gidx)
                            for tc in range(4):
                                r0 = tau * TT + 128 * tc
                                sY.dma(gpsimd.dma_start(out=y_d[r0:r0 + 128, 512 * dt:512 * (dt + 1)],
                                                        in_=stg8[tc][:, :], accum_op=ALU.add))

        @block.gpsimd
        def _(gpsimd):
            gpsimd.wait_ge(sY.h, sY.n)
            if debug:
                for i in range(J // 128):
                    sY.dma(gpsimd.dma_start(out=dbg["h8"][128 * i:128 * (i + 1), :],
                                            in_=h8_s[128 * i:128 * (i + 1), :]))
                for i in range(D // 128):
                    sY.dma(gpsimd.dma_start(out=dbg["w8p"][128 * i:128 * (i + 1), :],
                                            in_=w8pb_s[128 * i:128 * (i + 1), :]))
                sY.dma(gpsimd.dma_start(out=dbg["sc2"][:, :], in_=scal[0:1, 0:8]))
                for hl in range(2):
                    for r in range(NTC):
                        sY.dma(gpsimd.dma_start(out=dbg["oh"][hl, 128 * r:128 * (r + 1), :],
                                                in_=oh_s[hl, 128 * r:128 * (r + 1), :]))
                    for r in range(NH * (DH + DHR) // 128):
                        sY.dma(gpsimd.dma_start(out=dbg["q"][hl, 128 * r:128 * (r + 1), :],
                                                in_=q_s[hl, 128 * r:128 * (r + 1), :]))
                        sY.dma(gpsimd.dma_start(out=dbg["k"][hl, 128 * r:128 * (r + 1), :],
                                                in_=k_s[hl, 128 * r:128 * (r + 1), :]))
                    for r in range(L // 128):
                        sY.dma(gpsimd.dma_start(out=dbg["v"][hl, 128 * r:128 * (r + 1), :],
                                                in_=v_s[hl, 128 * r:128 * (r + 1), :]))
                gpsimd.wait_ge(sY.h, sY.n)
        return nc

        if stage <= 4:
            @block.gpsimd
            def _(gpsimd):
                gpsimd.wait_ge(sG.h, p6_done_G)
                for i in range(NTC):
                    sY.dma(gpsimd.dma_start(out=y_d[128 * i:128 * (i + 1), :],
                                            in_=xnhi[i][:, 0:2048]))
                if debug:
                    for hl in range(2):
                        for r in range(NTC):
                            sY.dma(gpsimd.dma_start(out=dbg["oh"][hl, 128 * r:128 * (r + 1), :],
                                                    in_=oh_s[hl, 128 * r:128 * (r + 1), :]))
                gpsimd.wait_ge(sY.h, sY.n)
            return nc

        if stage <= 3:
            @block.gpsimd
            def _(gpsimd):
                gpsimd.wait_ge(sG.h, p4_done_G)
                for i in range(NTC):
                    sY.dma(gpsimd.dma_start(out=y_d[128 * i:128 * (i + 1), :],
                                            in_=xnhi[i][:, 0:2048]))
                gpsimd.wait_ge(sY.h, sY.n)
            return nc

        if stage <= 2:
            @block.gpsimd
            def _(gpsimd):
                gpsimd.wait_ge(sG.h, p3_done_G)
                for i in range(NTC):
                    sY.dma(gpsimd.dma_start(out=y_d[128 * i:128 * (i + 1), :],
                                            in_=xnhi[i][:, 0:2048]))
                if debug:
                    for hl in range(2):
                        for r in range(NH * (DH + DHR) // 128):
                            sY.dma(gpsimd.dma_start(out=dbg["q"][hl, 128 * r:128 * (r + 1), :],
                                                    in_=q_s[hl, 128 * r:128 * (r + 1), :]))
                            sY.dma(gpsimd.dma_start(out=dbg["k"][hl, 128 * r:128 * (r + 1), :],
                                                    in_=k_s[hl, 128 * r:128 * (r + 1), :]))
                        for r in range(NTC):
                            sY.dma(gpsimd.dma_start(out=dbg["v"][hl, 128 * r:128 * (r + 1), :],
                                                    in_=v_s[hl, 128 * r:128 * (r + 1), :]))
                    gpsimd.wait_ge(cc.h, cc1_done)
                    sG.dma(gpsimd.dma_start(out=sone[:, :], in_=cc1_o[:, :]))
                    gpsimd.wait_ge(sG.h, sG.n)
                    sY.dma(gpsimd.dma_start(out=dbg["sc"][:, :], in_=sone[:, :]))
                gpsimd.wait_ge(sY.h, sY.n)
            return nc

        raise NotImplementedError("stages beyond 1 not yet implemented")

    return nc



def _rope_rt():
    rt = np.zeros((128, 128), np.float32)
    for blk in (0, 64):
        for i in range(32):
            rt[blk + i + 32, blk + i] = -1.0
            rt[blk + i, blk + i + 32] = 1.0
    return rt


def _rope_tables():
    inv = (1.0 / (10000.0 ** (np.arange(0, DHR, 2, dtype=np.float32) / DHR))).astype(np.float32)
    ang = (np.arange(L, dtype=np.float32)[None, :] * inv[:, None]).astype(np.float32)
    return np.cos(ang).astype(np.float32), np.sin(ang).astype(np.float32)


def make_inputs(L0_attn_x, L0_attn_W_D_Q, L0_attn_W_U_Q, L0_attn_W_Q_R, L0_attn_W_D_KV,
                L0_attn_W_U_K, L0_attn_W_K_R, L0_attn_W_U_V, L0_attn_W_O,
                L0_ffn_w_fc, L0_ffn_w_proj):
    cos_t, sin_t = _rope_tables()
    consts = dict(
        w_dq=np.ascontiguousarray(L0_attn_W_D_Q, np.float32),
        w_uq=np.ascontiguousarray(L0_attn_W_U_Q, np.float32),
        w_qr=np.ascontiguousarray(L0_attn_W_Q_R, np.float32),
        w_dkv=np.ascontiguousarray(L0_attn_W_D_KV, np.float32),
        w_uk=np.ascontiguousarray(L0_attn_W_U_K, np.float32),
        w_kr=np.ascontiguousarray(L0_attn_W_K_R, np.float32),
        w_uv=np.ascontiguousarray(L0_attn_W_U_V, np.float32),
        w_o=np.ascontiguousarray(L0_attn_W_O, np.float32),
        w_fc=np.ascontiguousarray(L0_ffn_w_fc, np.float32),
        w_pj=np.ascontiguousarray(L0_ffn_w_proj, np.float32),
        cos_t=cos_t, sin_t=sin_t,
        ones_r=np.ones((1, 128), np.float32),
        ones_cb=np.ones((128, 128), ml_dtypes.bfloat16),
        rope_rt=_rope_rt(),
    )
    x = np.asarray(L0_attn_x, dtype=np.float32)
    return [dict(consts, x=np.ascontiguousarray(x[c])) for c in range(N_CORES)]


def kernel(**inputs):
    in_maps = make_inputs(**inputs)
    nc = build_nc()
    res = run_bass_kernel_spmd(nc, in_maps, core_ids=list(range(N_CORES)))
    return np.stack([res.results[c]["y"] for c in range(N_CORES)], axis=0)


# version 19
# speedup vs baseline: 1.3249x; 1.0227x over previous
import math
import os
import numpy as np
import ml_dtypes
import concourse.bass as bass
import concourse.mybir as mybir
from concourse.bass_utils import run_bass_kernel_spmd
from contextlib import ExitStack

F32 = mybir.dt.float32
BF16 = mybir.dt.bfloat16
F8 = mybir.dt.float8e4
AF = mybir.ActivationFunctionType
ALU = mybir.AluOpType
AX = mybir.AxisListType

N_CORES = 8
L = 2048
D = 2048
NH, DH, DHR = 16, 128, 64
DC, DCQ = 512, 1536
J = 8192
TT = 512
NTT = L // TT
NTC = L // 128
ND = D // 128
INV_SQRT_DH = 1.0 / math.sqrt(DH)

STAGE_FULL = 99
P6_NH = int(os.environ.get("P6_NH", str(NH)))


class Sem:
    def __init__(self, h):
        self.h = h
        self.n = 0

    def inc(self, ins, k=1):
        ins.then_inc(self.h, k)
        self.n += k
        return ins

    def dma(self, ins):
        return self.inc(ins, 16)


def build_nc(stage=STAGE_FULL, debug=False):
    nc = bass.Bass("TRN2", target_bir_lowering=False, debug=False,
                   detect_race_conditions=False)

    x_d = nc.dram_tensor("x", [L, D], F32, kind="ExternalInput")
    wdq_d = nc.dram_tensor("w_dq", [D, DCQ], F32, kind="ExternalInput")
    wuq_d = nc.dram_tensor("w_uq", [DCQ, NH * DH], F32, kind="ExternalInput")
    wqr_d = nc.dram_tensor("w_qr", [DCQ, NH * DHR], F32, kind="ExternalInput")
    wdkv_d = nc.dram_tensor("w_dkv", [D, DC], F32, kind="ExternalInput")
    wuk_d = nc.dram_tensor("w_uk", [DC, NH * DH], F32, kind="ExternalInput")
    wkr_d = nc.dram_tensor("w_kr", [D, NH * DHR], F32, kind="ExternalInput")
    wuv_d = nc.dram_tensor("w_uv", [DC, NH * DH], F32, kind="ExternalInput")
    wo_d = nc.dram_tensor("w_o", [NH * DH, D], F32, kind="ExternalInput")
    wfc_d = nc.dram_tensor("w_fc", [J, D], F32, kind="ExternalInput")
    wpj_d = nc.dram_tensor("w_pj", [D, J], F32, kind="ExternalInput")
    cos_d = nc.dram_tensor("cos_t", [32, L], F32, kind="ExternalInput")
    sin_d = nc.dram_tensor("sin_t", [32, L], F32, kind="ExternalInput")
    ones_r_d = nc.dram_tensor("ones_r", [1, 128], F32, kind="ExternalInput")
    ones_cb_d = nc.dram_tensor("ones_cb", [128, 128], BF16, kind="ExternalInput")
    rope_rt_d = nc.dram_tensor("rope_rt", [128, 128], F32, kind="ExternalInput")
    y_d = nc.dram_tensor("y", [L, D], F32, kind="ExternalOutput")
    dbg = {}
    if debug:
        dbg["xn"] = nc.dram_tensor("dbg_xn", [L, D], F32, kind="ExternalOutput")
        dbg["sc"] = nc.dram_tensor("dbg_sc", [1, 8], F32, kind="ExternalOutput")
        dbg["q"] = nc.dram_tensor("dbg_q", [2, NH * (DH + DHR), L], BF16, kind="ExternalOutput")
        dbg["k"] = nc.dram_tensor("dbg_k", [2, NH * (DH + DHR), L], BF16, kind="ExternalOutput")
        dbg["v"] = nc.dram_tensor("dbg_v", [2, L, NH * DH], BF16, kind="ExternalOutput")
        dbg["h"] = nc.dram_tensor("dbg_h", [J, L], F32, kind="ExternalOutput")
        dbg["oh"] = nc.dram_tensor("dbg_oh", [2, NH * DH, L], BF16, kind="ExternalOutput")
        dbg["h8"] = nc.dram_tensor("dbg_h8", [J, L], BF16, kind="ExternalOutput")
        dbg["w8p"] = nc.dram_tensor("dbg_w8p", [D, J], BF16, kind="ExternalOutput")
        dbg["sc2"] = nc.dram_tensor("dbg_sc2", [1, 8], F32, kind="ExternalOutput")

    xnhi_s = nc.dram_tensor("xnhi_s", [L, D], BF16)
    xnlo_s = nc.dram_tensor("xnlo_s", [L, D], BF16)
    wsplit = {}
    for nm, dd in (("wdq", (D, DCQ)), ("wuq", (DCQ, NH * DH)), ("wqr", (DCQ, NH * DHR)),
                   ("wdkv", (D, DC)), ("wuk", (DC, NH * DH)), ("wkr", (D, NH * DHR)),
                   ("wuv", (DC, NH * DH)), ("wo", (NH * DH, D))):
        wsplit[nm] = (nc.dram_tensor(nm + "_hi", list(dd), BF16),
                      nc.dram_tensor(nm + "_lo", list(dd), BF16))
    q_s = nc.dram_tensor("q_s", [2, NH * (DH + DHR), L], BF16)
    k_s = nc.dram_tensor("k_s", [2, NH * (DH + DHR), L], BF16)
    v_s = nc.dram_tensor("v_s", [2, L, NH * DH], BF16)
    oh_s = nc.dram_tensor("oh_s", [2, NH * DH, L], BF16)
    w8b_s = nc.dram_tensor("w8b_s", [J, D], BF16)
    w8pb_s = nc.dram_tensor("w8pb_s", [D, J], BF16)
    h_s = nc.dram_tensor("h_s", [J, L], F32)
    h8_s = nc.dram_tensor("h8_s", [J, L], BF16)
    cc1_i = nc.dram_tensor("cc1_i", [1, 8], F32)
    cc1_o = nc.dram_tensor("cc1_o", [1, 8], F32)
    cc2_i = nc.dram_tensor("cc2_i", [1, 8], F32)
    cc2_o = nc.dram_tensor("cc2_o", [1, 8], F32)
    row_s = nc.dram_tensor("row_s", [1, 128], F32)
    bc_s = nc.dram_tensor("bc_s", [1, 8], F32)

    with ExitStack() as es:
        E = es.enter_context
        sL = Sem(E(nc.semaphore("sL")))
        sS = Sem(E(nc.semaphore("sS")))
        sG = Sem(E(nc.semaphore("sG")))
        sM = Sem(E(nc.semaphore("sM")))
        sV = Sem(E(nc.semaphore("sV")))
        sC = Sem(E(nc.semaphore("sC")))
        sY = Sem(E(nc.semaphore("sY")))
        cc = Sem(E(nc.semaphore("cc")))
        sPC = Sem(E(nc.semaphore("sPC")))
        sPV = Sem(E(nc.semaphore("sPV")))
        sPG = Sem(E(nc.semaphore("sPG")))
        sS2 = Sem(E(nc.semaphore("sS2")))
        sHp = [Sem(E(nc.semaphore(f"sH{i}"))) for i in range(4)]
        block = E(nc.Block())

        xnhi = [E(nc.sbuf_tensor(f"xnhi{i}", [128, L], BF16, is_transpose_dest=True)) for i in range(ND)]
        ones_r = E(nc.sbuf_tensor("sb_ones_r", [1, 128], F32))
        ones_cb = E(nc.sbuf_tensor("sb_ones_cb", [128, 128], BF16))
        rope_rt = E(nc.sbuf_tensor("sb_rope_rt", [128, 128], F32))
        scal = E(nc.sbuf_tensor("scal", [128, 8], F32))
        smax = E(nc.sbuf_tensor("smax", [128, 8], F32))
        srow = E(nc.sbuf_tensor("srow", [1, 128], F32))
        sone = E(nc.sbuf_tensor("sone", [1, 8], F32))
        spers = E(nc.sbuf_tensor("spers", [1, 2], F32))
        pA = E(nc.psum_tensor("pA", [128, 512], F32))
        pB = E(nc.psum_tensor("pB", [128, 512], F32))
        pC = E(nc.psum_tensor("pC", [128, 512], F32))
        pE = E(nc.psum_tensor("pE", [128, 512], F32))
        pD = E(nc.psum_tensor("pD", [1, 512], F32))
        pBC = E(nc.psum_tensor("pBC", [128, 512], F32))
        p3_scope = ExitStack()
        xnlo = [p3_scope.enter_context(nc.sbuf_tensor(f"xnlo{i}", [128, L], BF16, is_transpose_dest=True)) for i in range(ND)]
        cos128 = p3_scope.enter_context(nc.sbuf_tensor("cos128", [128, L], F32))
        sin128 = p3_scope.enter_context(nc.sbuf_tensor("sin128", [128, L], F32))

        @block.sync
        def _(sync):
            for r in range(4):
                sL.dma(sync.dma_start(out=cos128[32 * r:32 * (r + 1), :], in_=cos_d[:, :]))
                sL.dma(sync.dma_start(out=sin128[32 * r:32 * (r + 1), :], in_=sin_d[:, :]))
            sL.dma(sync.dma_start(out=ones_r[:, :], in_=ones_r_d[:, :]))
            sL.dma(sync.dma_start(out=ones_cb[:, :], in_=ones_cb_d[:, :]))
            sL.dma(sync.dma_start(out=rope_rt[:, :], in_=rope_rt_d[:, :]))
        p0_consts = sL.n

        @block.vector
        def _(vector):
            vector.memset(smax[:, :], 0.0)
            sV.inc(vector.memset(scal[:, :], 0.0))
        p0_vec = sV.n

        with ExitStack() as p1:
            E1 = p1.enter_context
            xt = [E1(nc.sbuf_tensor(f"p1x{i}", [128, D], F32)) for i in range(2)]
            sq = E1(nc.sbuf_tensor("p1sq", [128, D], F32))
            xnt = [E1(nc.sbuf_tensor(f"p1xn{i}", [128, D], F32)) for i in range(2)]
            hi_t = [E1(nc.sbuf_tensor(f"p1hi{i}", [128, D], BF16)) for i in range(2)]
            lo_t = [E1(nc.sbuf_tensor(f"p1lo{i}", [128, D], BF16)) for i in range(2)]
            st1 = E1(nc.sbuf_tensor("p1s1", [128, 1], F32))
            st2 = E1(nc.sbuf_tensor("p1s2", [128, 1], F32))
            st16 = E1(nc.sbuf_tensor("p1s16", [128, 16], F32))
            st16b = E1(nc.sbuf_tensor("p1s16b", [128, 16], F32))
            mm_ = E1(nc.sbuf_tensor("p1m", [128, 1], F32))
            ms_ = E1(nc.sbuf_tensor("p1ms", [128, 1], F32))
            var_ = E1(nc.sbuf_tensor("p1v", [128, 1], F32))
            var2_ = E1(nc.sbuf_tensor("p1v2", [128, 1], F32))
            var3_ = E1(nc.sbuf_tensor("p1v3", [128, 1], F32))
            junk_ = E1(nc.sbuf_tensor("p1jk", [128, 1], F32))
            junk2_ = E1(nc.sbuf_tensor("p1jk2", [128, 1], F32))
            sd_ = E1(nc.sbuf_tensor("p1sd", [128, 1], F32))
            rstd_ = E1(nc.sbuf_tensor("p1rs", [128, 1], F32))
            tmx = E1(nc.sbuf_tensor("p1mx", [128, 1], F32))

            bL, bS, bV, bC = sL.n, sS.n, sV.n, sC.n
            VPI = 6
            SPI = 3 if debug else 2

            @block.sync
            def _(sync):
                for i in range(NTC):
                    if i >= 2:
                        sync.wait_ge(sV.h, bV + (i - 1) * VPI)
                    sL.dma(sync.dma_start(out=xt[i % 2][:, :], in_=x_d[128 * i:128 * (i + 1), :]))

            @block.scalar
            def _(scalar):
                for i in range(NTC):
                    scalar.wait_ge(sV.h, bV + i * VPI + 1)
                    sC.inc(scalar.activation(out=sq[:, :], in_=xnt[i % 2][:, :], func=AF.Square))
                    scalar.wait_ge(sV.h, bV + i * VPI + 2)
                    sC.inc(scalar.activation(out=sd_[:, :], in_=var3_[:, :], func=AF.Sqrt))
                    scalar.wait_ge(sV.h, bV + i * VPI + 6)
                    sS.dma(scalar.dma_start(out=xnhi_s[128 * i:128 * (i + 1), :], in_=hi_t[i % 2][:, :]))
                    sS.dma(scalar.dma_start(out=xnlo_s[128 * i:128 * (i + 1), :], in_=lo_t[i % 2][:, :]))
                    if debug:
                        sS.dma(scalar.dma_start(out=dbg["xn"][128 * i:128 * (i + 1), :], in_=xnt[i % 2][:, :]))

            @block.vector
            def _(vector):
                vector.memset(junk2_[:, :], 0.0)
                for i in range(NTC):
                    vector.wait_ge(sL.h, bL + 16 * (i + 1))
                    vector.tensor_reduce(out=st16[:, :],
                                         in_=xt[i % 2][:, :].rearrange("p (a b) -> p a b", b=128),
                                         axis=AX.X, op=ALU.add)
                    vector.tensor_copy(out=junk_[:, :], in_=junk2_[:, :])
                    vector.reduce_sum(out=st1[:, :], in_=st16[:, :], axis=AX.X)
                    vector.tensor_copy(out=junk_[:, :], in_=junk2_[:, :])
                    vector.tensor_scalar(out=mm_[:, :], in0=st1[:, :], scalar1=1.0 / D, scalar2=None, op0=ALU.mult)
                    vector.tensor_copy(out=junk_[:, :], in_=junk2_[:, :])
                    if i >= 2:
                        vector.wait_ge(sS.h, bS + (i - 1) * SPI * 16)
                    sV.inc(vector.tensor_scalar(out=xnt[i % 2][:, :], in0=xt[i % 2][:, :], scalar1=mm_[:, :],
                                                scalar2=None, op0=ALU.subtract))
                    vector.wait_ge(sC.h, bC + i * 2 + 1)
                    vector.tensor_reduce(out=st16b[:, :],
                                         in_=sq[:, :].rearrange("p (a b) -> p a b", b=128),
                                         axis=AX.X, op=ALU.add)
                    vector.tensor_copy(out=junk_[:, :], in_=junk2_[:, :])
                    vector.reduce_sum(out=st2[:, :], in_=st16b[:, :], axis=AX.X)
                    vector.tensor_copy(out=junk_[:, :], in_=junk2_[:, :])
                    sV.inc(vector.tensor_scalar(out=var3_[:, :], in0=st2[:, :], scalar1=1.0 / D,
                                                scalar2=1e-5, op0=ALU.mult, op1=ALU.add))
                    vector.wait_ge(sC.h, bC + i * 2 + 2)
                    vector.reciprocal(out=rstd_[:, :], in_=sd_[:, :])
                    vector.tensor_copy(out=junk_[:, :], in_=junk2_[:, :])
                    vector.tensor_tensor(out=var_[:, :], in0=sd_[:, :], in1=rstd_[:, :], op=ALU.mult)
                    vector.tensor_copy(out=junk_[:, :], in_=junk2_[:, :])
                    vector.tensor_scalar(out=var2_[:, :], in0=var_[:, :], scalar1=-1.0,
                                         scalar2=2.0, op0=ALU.mult, op1=ALU.add)
                    vector.tensor_copy(out=junk_[:, :], in_=junk2_[:, :])
                    sV.inc(vector.tensor_tensor(out=mm_[:, :], in0=rstd_[:, :], in1=var2_[:, :],
                                                op=ALU.mult))
                    vector.tensor_copy(out=junk_[:, :], in_=junk2_[:, :])
                    vector.tensor_scalar(out=xnt[i % 2][:, :], in0=xnt[i % 2][:, :], scalar1=mm_[:, :],
                                         scalar2=None, op0=ALU.mult)
                    vector.tensor_copy(out=junk_[:, :], in_=junk2_[:, :])
                    vector.reduce_max(out=tmx[:, :], in_=xnt[i % 2][:, :], axis=AX.X, apply_absolute_value=True)
                    a, b = (0, 1) if i % 2 == 0 else (1, 0)
                    sV.inc(vector.tensor_copy(out=hi_t[i % 2][:, :], in_=xnt[i % 2][:, :]))
                    sV.inc(vector.tensor_tensor(out=smax[:, b:b + 1], in0=smax[:, a:a + 1],
                                                in1=tmx[:, :], op=ALU.max))
                    sV.inc(vector.scalar_tensor_tensor(out=lo_t[i % 2][:, :], in0=xnt[i % 2][:, :], scalar=1.0,
                                                       in1=hi_t[i % 2][:, :], op0=ALU.mult, op1=ALU.subtract))

            p1_spills = sS.n
            p1_vec = sV.n

            @block.sync
            def _(sync):
                for q in range(4):
                    sync.wait_ge(sS.h, bS + SPI * 16 * (4 * q + 4))
                    for dc_ in range(ND):
                        sL.dma(sync.dma_start(
                            out=xnhi[dc_][:, 512 * q:512 * (q + 1)],
                            in_=xnhi_s[512 * q:512 * (q + 1), 128 * dc_:128 * (dc_ + 1)],
                            transpose=True))
                        sL.dma(sync.dma_start(
                            out=xnlo[dc_][:, 512 * q:512 * (q + 1)],
                            in_=xnlo_s[512 * q:512 * (q + 1), 128 * dc_:128 * (dc_ + 1)],
                            transpose=True))
            p1_xn_ready = sL.n

        @block.gpsimd
        def _(gpsimd):
            gpsimd.wait_ge(sV.h, p1_vec)
            sG.dma(gpsimd.dma_start(out=row_s[:, :], in_=smax[:, 0:1]))
            gpsimd.wait_ge(sG.h, sG.n)
            sG.dma(gpsimd.dma_start(out=srow[:, :], in_=row_s[:, :]))
        p2_row = sG.n

        @block.vector
        def _(vector):
            vector.wait_ge(sG.h, p2_row)
            vector.memset(sone[:, :], 0.0)
            sV.inc(vector.reduce_max(out=sone[:, 0:1], in_=srow[:, :], axis=AX.X))
        p2_lmax = sV.n

        @block.gpsimd
        def _(gpsimd):
            gpsimd.wait_ge(sV.h, p2_lmax)
            sG.dma(gpsimd.dma_start(out=cc1_i[:, :], in_=sone[:, :]))
            gpsimd.wait_ge(sG.h, sG.n)
            cc.inc(gpsimd.collective_compute(
                "AllReduce", ALU.max, replica_groups=[list(range(N_CORES))],
                ins=[cc1_i.ap().opt()], outs=[cc1_o.ap().opt()]))
        cc1_done = cc.n

        if stage <= 1:
            @block.gpsimd
            def _(gpsimd):
                gpsimd.wait_ge(sL.h, p1_xn_ready)
                for i in range(NTC):
                    sY.dma(gpsimd.dma_start(out=y_d[128 * i:128 * (i + 1), :],
                                            in_=xnhi[i][:, 0:2048]))
                if debug:
                    gpsimd.wait_ge(cc.h, cc1_done)
                    sG.dma(gpsimd.dma_start(out=sone[:, :], in_=cc1_o[:, :]))
                    gpsimd.wait_ge(sG.h, sG.n)
                    sY.dma(gpsimd.dma_start(out=dbg["sc"][:, :], in_=sone[:, :]))
                gpsimd.wait_ge(sY.h, sY.n)
            return nc

        with ExitStack() as pw:
            EW = pw.enter_context
            wf32 = [EW(nc.sbuf_tensor(f"wsp{i}", [128, 2048], F32)) for i in range(2)]
            whib = [EW(nc.sbuf_tensor(f"wsph{i}", [128, 2048], BF16)) for i in range(2)]
            wlob = [EW(nc.sbuf_tensor(f"wspl{i}", [128, 2048], BF16)) for i in range(2)]
            wjk = EW(nc.sbuf_tensor("wspj", [128, 1], F32))
            wjk2 = EW(nc.sbuf_tensor("wspj2", [128, 1], F32))
            splits = [("wdq", wdq_d, D, DCQ), ("wuq", wuq_d, DCQ, NH * DH), ("wqr", wqr_d, DCQ, NH * DHR),
                      ("wdkv", wdkv_d, D, DC), ("wuk", wuk_d, DC, NH * DH), ("wkr", wkr_d, D, NH * DHR),
                      ("wuv", wuv_d, DC, NH * DH), ("wo", wo_d, NH * DH, D)]
            chunks = []
            for nm, dram, rows, cols in splits:
                for r in range(rows // 128):
                    chunks.append((nm, dram, r, cols))
            NWC = len(chunks)
            bL, bS, bV = sL.n, sS.n, sV.n

            @block.sync
            def _(sync):
                for n, (nm, dram, r, cols) in enumerate(chunks):
                    if n >= 2:
                        sync.wait_ge(sV.h, bV + (n - 1) * 2)
                    sL.dma(sync.dma_start(out=wf32[n % 2][:, 0:cols], in_=dram[128 * r:128 * (r + 1), :]))

            @block.vector
            def _(vector):
                vector.memset(wjk2[:, :], 0.0)
                for n, (nm, dram, r, cols) in enumerate(chunks):
                    vector.wait_ge(sL.h, bL + 16 * (n + 1))
                    if n >= 2:
                        vector.wait_ge(sS.h, bS + (n - 1) * 2 * 16)
                    sV.inc(vector.tensor_copy(out=whib[n % 2][:, 0:cols], in_=wf32[n % 2][:, 0:cols]))
                    vector.tensor_copy(out=wjk[:, :], in_=wjk2[:, :])
                    sV.inc(vector.scalar_tensor_tensor(out=wlob[n % 2][:, 0:cols], in0=wf32[n % 2][:, 0:cols],
                                                       scalar=1.0, in1=whib[n % 2][:, 0:cols],
                                                       op0=ALU.mult, op1=ALU.subtract))

            @block.scalar
            def _(scalar):
                for n, (nm, dram, r, cols) in enumerate(chunks):
                    hi_d, lo_d = wsplit[nm]
                    scalar.wait_ge(sV.h, bV + n * 2 + 2)
                    sS.dma(scalar.dma_start(out=hi_d[128 * r:128 * (r + 1), :], in_=whib[n % 2][:, 0:cols]))
                    sS.dma(scalar.dma_start(out=lo_d[128 * r:128 * (r + 1), :], in_=wlob[n % 2][:, 0:cols]))
            wsplit_done = sS.n

        with ExitStack() as p3:
            E3 = p3.enter_context
            rg_hi = [E3(nc.sbuf_tensor(f"p3rh{i}", [128, 512], BF16)) for i in range(2)]
            rg_lo = [E3(nc.sbuf_tensor(f"p3rl{i}", [128, 512], BF16)) for i in range(2)]
            qc_hi = [E3(nc.sbuf_tensor(f"p3qh{i}", [128, 512], BF16)) for i in range(DCQ // 128)]
            qc_lo = [E3(nc.sbuf_tensor(f"p3ql{i}", [128, 512], BF16)) for i in range(DCQ // 128)]
            c_hi = [E3(nc.sbuf_tensor(f"p3ch{i}", [128, 512], BF16)) for i in range(DC // 128)]
            c_lo = [E3(nc.sbuf_tensor(f"p3cl{i}", [128, 512], BF16)) for i in range(DC // 128)]
            stg_hi = [[E3(nc.sbuf_tensor(f"p3sh{ss}_{i}", [128, 512], BF16)) for i in range(4)] for ss in range(2)]
            stg_lo = [[E3(nc.sbuf_tensor(f"p3sl{ss}_{i}", [128, 512], BF16)) for i in range(4)] for ss in range(2)]
            rtmp = [E3(nc.sbuf_tensor(f"p3rt{i}", [128, 512], F32)) for i in range(2)]
            rb = [E3(nc.sbuf_tensor(f"p3rb{i}", [128, 512], F32)) for i in range(2)]
            pj = E3(nc.sbuf_tensor("p3jk", [128, 1], F32))
            pj2 = E3(nc.sbuf_tensor("p3jk2", [128, 1], F32))
            PS4 = [pA, pB, pC, pE]

            @block.vector
            def _(vector):
                sV.inc(vector.memset(pj2[:, :], 0.0))

            chain = dict(V=sV.n, M=sM.n)

            def emit_proj(tau, wname, K, cols, moving_hi, moving_lo, mslice, evac,
                          row0=0, scr=None, dst=None):
                whi_d, wlo_d = wsplit[wname]
                NG = cols // 512
                tok = tau * TT
                bL0, bM0, bV0, bG0 = sL.n, sM.n, sV.n, sG.n
                pre_V, pre_M = chain["V"], chain["M"]
                VPG = 3 if evac == "rope" else 2
                TPG = K + (4 if evac == "rope" else 0)

                @block.sync
                def _(sync):
                    sync.wait_ge(sS.h, wsplit_done)
                    for g in range(NG):
                        for k in range(K):
                            gk = g * K + k
                            if gk >= 2:
                                g2, k2 = divmod(gk - 2, K)
                                sync.wait_ge(sM.h, bM0 + g2 * TPG + k2 + 1)
                            else:
                                sync.wait_ge(sM.h, bM0)
                            sL.dma(sync.dma_start(out=rg_hi[gk % 2][:, :],
                                                  in_=whi_d[128 * k:128 * (k + 1), 512 * g:512 * (g + 1)]))
                            sL.dma(sync.dma_start(out=rg_lo[gk % 2][:, :],
                                                  in_=wlo_d[128 * k:128 * (k + 1), 512 * g:512 * (g + 1)]))

                @block.tensor
                def _(tensor):
                    tensor.wait_ge(sL.h, p1_xn_ready)
                    tensor.wait_ge(sV.h, pre_V)
                    for g in range(NG):
                        if g >= 1:
                            tensor.wait_ge(sV.h, bV0 + g * VPG)
                        for k in range(K):
                            gk = g * K + k
                            tensor.wait_ge(sL.h, bL0 + 32 * (gk + 1))
                            e = gk % 2
                            mh = moving_hi[k][:, mslice]
                            ml = moving_lo[k][:, mslice]
                            for oc in range(4):
                                wh = rg_hi[e][:, 128 * oc:128 * (oc + 1)]
                                wl = rg_lo[e][:, 128 * oc:128 * (oc + 1)]
                                tensor.matmul(PS4[oc][:, :], wh, mh, start=(k == 0), stop=False)
                                tensor.matmul(PS4[oc][:, :], wh, ml, start=False, stop=False)
                                ins = tensor.matmul(PS4[oc][:, :], wl, mh, start=False, stop=(k == K - 1))
                            sM.inc(ins)
                        if evac == "rope":
                            for half in range(2):
                                tensor.wait_ge(sV.h, bV0 + g * VPG + half + 1)
                                for j in range(2):
                                    sM.inc(tensor.matmul(PS4[2 * half + j][:, :], rope_rt[:, :],
                                                         rtmp[j][:, :], start=True, stop=True))

                @block.vector
                def _(vector):
                    vector.wait_ge(sM.h, pre_M)
                    for g in range(NG):
                        ss = g % 2
                        if evac != "local":
                            vector.wait_ge(sG.h, bG0 if g < 2 else bG0 + (g - 1) * 8 * 16)
                        if evac == "rope":
                            cs = cos128[:, tok:tok + TT]
                            sn = sin128[:, tok:tok + TT]
                            for half in range(2):
                                if half == 0:
                                    vector.wait_ge(sM.h, bM0 + g * TPG + K)
                                o0, o1 = 2 * half, 2 * half + 1
                                vector.tensor_copy(out=rtmp[0][:, :], in_=PS4[o0][:, :])
                                vector.tensor_copy(out=rtmp[1][:, :], in_=PS4[o1][:, :])
                                sV.inc(vector.tensor_copy(out=pj[:, :], in_=pj2[:, :]))
                                vector.wait_ge(sM.h, bM0 + g * TPG + K + 2 * (half + 1))
                                vector.tensor_tensor(out=rtmp[0][:, :], in0=rtmp[0][:, :], in1=cs, op=ALU.mult)
                                vector.tensor_tensor(out=rtmp[1][:, :], in0=rtmp[1][:, :], in1=cs, op=ALU.mult)
                                vector.tensor_tensor(out=rb[0][:, :], in0=PS4[o0][:, :], in1=sn, op=ALU.mult)
                                vector.tensor_tensor(out=rb[1][:, :], in0=PS4[o1][:, :], in1=sn, op=ALU.mult)
                                vector.tensor_tensor(out=rtmp[0][:, :], in0=rtmp[0][:, :], in1=rb[0][:, :], op=ALU.add)
                                vector.tensor_tensor(out=rtmp[1][:, :], in0=rtmp[1][:, :], in1=rb[1][:, :], op=ALU.add)
                                vector.tensor_copy(out=stg_hi[ss][o0][:, :], in_=rtmp[0][:, :])
                                vector.tensor_copy(out=stg_hi[ss][o1][:, :], in_=rtmp[1][:, :])
                                vector.scalar_tensor_tensor(out=stg_lo[ss][o0][:, :], in0=rtmp[0][:, :],
                                                            scalar=1.0, in1=stg_hi[ss][o0][:, :],
                                                            op0=ALU.mult, op1=ALU.subtract)
                                vector.scalar_tensor_tensor(out=stg_lo[ss][o1][:, :], in0=rtmp[1][:, :],
                                                            scalar=1.0, in1=stg_hi[ss][o1][:, :],
                                                            op0=ALU.mult, op1=ALU.subtract)
                            sV.inc(vector.tensor_copy(out=pj[:, :], in_=pj2[:, :]))
                        else:
                            vector.wait_ge(sM.h, bM0 + (g + 1) * TPG)
                            if evac == "local":
                                hv = [dst[0][g * 4 + oc] for oc in range(4)]
                                lv = [dst[1][g * 4 + oc] for oc in range(4)]
                            else:
                                hv, lv = stg_hi[ss], stg_lo[ss]
                            for oc in range(4):
                                vector.tensor_copy(out=hv[oc][:, :], in_=PS4[oc][:, :])
                            sV.inc(vector.tensor_copy(out=pj[:, :], in_=pj2[:, :]))
                            for oc in range(4):
                                vector.scalar_tensor_tensor(out=lv[oc][:, :], in0=PS4[oc][:, :],
                                                            scalar=1.0, in1=hv[oc][:, :],
                                                            op0=ALU.mult, op1=ALU.subtract)
                            sV.inc(vector.tensor_copy(out=pj[:, :], in_=pj2[:, :]))

                if evac != "local":
                    @block.gpsimd
                    def _(gpsimd):
                        for g in range(NG):
                            ss = g % 2
                            gpsimd.wait_ge(sV.h, bV0 + (g + 1) * VPG)
                            for oc in range(4):
                                rr = row0 + 512 * g + 128 * oc
                                sG.dma(gpsimd.dma_start(out=scr[0, rr:rr + 128, tok:tok + TT],
                                                        in_=stg_hi[ss][oc][:, :]))
                                sG.dma(gpsimd.dma_start(out=scr[1, rr:rr + 128, tok:tok + TT],
                                                        in_=stg_lo[ss][oc][:, :]))

                chain["V"], chain["M"] = sV.n, sM.n

            def emit_v(tau):
                whi_d, wlo_d = wsplit["wuv"]
                NG = 4
                K = 4
                tok = tau * TT
                bL0, bM0, bV0, bG0 = sL.n, sM.n, sV.n, sG.n
                pre_V, pre_M = chain["V"], chain["M"]
                VPG, TPG = 2, K

                @block.sync
                def _(sync):
                    sync.wait_ge(sS.h, wsplit_done)
                    for g in range(NG):
                        for k in range(K):
                            gk = g * K + k
                            if gk >= 2:
                                g2, k2 = divmod(gk - 2, K)
                                sync.wait_ge(sM.h, bM0 + g2 * TPG + k2 + 1)
                            else:
                                sync.wait_ge(sM.h, bM0)
                            sL.dma(sync.dma_start(out=rg_hi[gk % 2][:, :],
                                                  in_=whi_d[128 * k:128 * (k + 1), 512 * g:512 * (g + 1)]))
                            sL.dma(sync.dma_start(out=rg_lo[gk % 2][:, :],
                                                  in_=wlo_d[128 * k:128 * (k + 1), 512 * g:512 * (g + 1)]))

                @block.tensor
                def _(tensor):
                    tensor.wait_ge(sV.h, pre_V)
                    for g in range(NG):
                        if g >= 1:
                            tensor.wait_ge(sV.h, bV0 + g * VPG)
                        for k in range(K):
                            gk = g * K + k
                            tensor.wait_ge(sL.h, bL0 + 32 * (gk + 1))
                            e = gk % 2
                            for tc in range(4):
                                ch = c_hi[k][:, 128 * tc:128 * (tc + 1)]
                                cl = c_lo[k][:, 128 * tc:128 * (tc + 1)]
                                tensor.matmul(PS4[tc][:, :], ch, rg_hi[e][:, :], start=(k == 0), stop=False)
                                tensor.matmul(PS4[tc][:, :], cl, rg_hi[e][:, :], start=False, stop=False)
                                ins = tensor.matmul(PS4[tc][:, :], ch, rg_lo[e][:, :], start=False, stop=(k == K - 1))
                            sM.inc(ins)

                @block.vector
                def _(vector):
                    vector.wait_ge(sM.h, pre_M)
                    for g in range(NG):
                        ss = g % 2
                        vector.wait_ge(sG.h, bG0 if g < 2 else bG0 + (g - 1) * 8 * 16)
                        vector.wait_ge(sM.h, bM0 + (g + 1) * TPG)
                        for tc in range(4):
                            vector.tensor_copy(out=stg_hi[ss][tc][:, :], in_=PS4[tc][:, :])
                        sV.inc(vector.tensor_copy(out=pj[:, :], in_=pj2[:, :]))
                        for tc in range(4):
                            vector.scalar_tensor_tensor(out=stg_lo[ss][tc][:, :], in0=PS4[tc][:, :],
                                                        scalar=1.0, in1=stg_hi[ss][tc][:, :],
                                                        op0=ALU.mult, op1=ALU.subtract)
                        sV.inc(vector.tensor_copy(out=pj[:, :], in_=pj2[:, :]))

                @block.gpsimd
                def _(gpsimd):
                    for g in range(NG):
                        ss = g % 2
                        gpsimd.wait_ge(sV.h, bV0 + (g + 1) * VPG)
                        for tc in range(4):
                            tr = tok + 128 * tc
                            sG.dma(gpsimd.dma_start(out=v_s[0, tr:tr + 128, 512 * g:512 * (g + 1)],
                                                    in_=stg_hi[ss][tc][:, :]))
                            sG.dma(gpsimd.dma_start(out=v_s[1, tr:tr + 128, 512 * g:512 * (g + 1)],
                                                    in_=stg_lo[ss][tc][:, :]))

                chain["V"], chain["M"] = sV.n, sG.n if False else sM.n

            ts_all = slice(None)
            for tau in range(NTT):
                msl = slice(tau * TT, tau * TT + TT)
                emit_proj(tau, "wdq", ND, DCQ, xnhi, xnlo, msl, "local", dst=(qc_hi, qc_lo))
                emit_proj(tau, "wuq", DCQ // 128, NH * DH, qc_hi, qc_lo, ts_all, "spill", row0=0, scr=q_s)
                emit_proj(tau, "wqr", DCQ // 128, NH * DHR, qc_hi, qc_lo, ts_all, "rope", row0=2048, scr=q_s)
                emit_proj(tau, "wdkv", ND, DC, xnhi, xnlo, msl, "local", dst=(c_hi, c_lo))
                emit_proj(tau, "wuk", DC // 128, NH * DH, c_hi, c_lo, ts_all, "spill", row0=0, scr=k_s)
                emit_proj(tau, "wkr", ND, NH * DHR, xnhi, xnlo, msl, "rope", row0=2048, scr=k_s)
                emit_v(tau)
            p3_done_G = sG.n
            p3_done_V = sV.n
            p3_done_M = sM.n

        def emit_wscan(w_dram, rows, cols, colpair):
            bL0, bV0 = sL.n, sV.n
            n_r, n_c = rows // 128, cols // 2048
            NCH = n_r * n_c
            a0, b0 = colpair

            @block.sync
            def _(sync):
                sync.wait_ge(sM.h, p3_done_M)
                sync.wait_ge(sG.h, p3_done_G)
                n = 0
                for r in range(n_r):
                    for cgrp in range(n_c):
                        if n >= 3:
                            sync.wait_ge(sV.h, bV0 + n - 2)
                        sL.dma(sync.dma_start(out=wsc[n % 3][:, :],
                                              in_=w_dram[128 * r:128 * (r + 1), 2048 * cgrp:2048 * (cgrp + 1)]))
                        n += 1

            @block.vector
            def _(vector):
                for n in range(NCH):
                    vector.wait_ge(sL.h, bL0 + 16 * (n + 1))
                    sV.inc(vector.reduce_max(out=wmx[n % 2][:, :], in_=wsc[n % 3][:, :],
                                             axis=AX.X, apply_absolute_value=True))
                    if n >= 1:
                        c_ = n - 1
                        aa = a0 if c_ % 2 == 0 else b0
                        bb = b0 if c_ % 2 == 0 else a0
                        vector.tensor_tensor(out=smax[:, bb:bb + 1], in0=smax[:, aa:aa + 1],
                                             in1=wmx[(n - 1) % 2][:, :], op=ALU.max)
                vector.tensor_copy(out=pjk[:, :], in_=pjk2[:, :])
                c_ = NCH - 1
                aa = a0 if c_ % 2 == 0 else b0
                bb = b0 if c_ % 2 == 0 else a0
                sV.inc(vector.tensor_tensor(out=smax[:, bb:bb + 1], in0=smax[:, aa:aa + 1],
                                            in1=wmx[(NCH - 1) % 2][:, :], op=ALU.max))
            return b0 if (NCH - 1) % 2 == 0 else a0

        with ExitStack() as pws:
            EW = pws.enter_context
            wsc = [EW(nc.sbuf_tensor(f"wsc{i}", [128, 2048], F32)) for i in range(3)]
            wmx = [EW(nc.sbuf_tensor(f"wmx{i}", [128, 1], F32)) for i in range(2)]
            sgx = EW(nc.sbuf_tensor("sgx", [1, 8], F32))
            wm1 = EW(nc.sbuf_tensor("wm1", [1, 1], F32))
            sxv = EW(nc.sbuf_tensor("sxv", [1, 1], F32))
            sw1v = EW(nc.sbuf_tensor("sw1v", [1, 1], F32))
            rsx = EW(nc.sbuf_tensor("rsx", [1, 1], F32))
            rsw1 = EW(nc.sbuf_tensor("rsw1", [1, 1], F32))
            scrow = EW(nc.sbuf_tensor("scrow", [1, 8], F32))
            pjk = EW(nc.sbuf_tensor("wsjk", [128, 1], F32))
            pjk2 = EW(nc.sbuf_tensor("wsjk2", [128, 1], F32))

            @block.vector
            def _(vector):
                sV.inc(vector.memset(pjk2[:, :], 0.0))
                vector.memset(scrow[:, :], 0.0)
            col_w1 = emit_wscan(wfc_d, J, D, (4, 5))
            ws_done_V = sV.n

            @block.gpsimd
            def _(gpsimd):
                gpsimd.wait_ge(sV.h, ws_done_V)
                sG.dma(gpsimd.dma_start(out=row_s[:, :], in_=smax[:, col_w1:col_w1 + 1]))
                gpsimd.wait_ge(sG.h, sG.n)
                sG.dma(gpsimd.dma_start(out=srow[:, :], in_=row_s[:, :]))
                gpsimd.wait_ge(sG.h, sG.n)
                gpsimd.wait_ge(cc.h, cc1_done)
                sG.dma(gpsimd.dma_start(out=sgx[:, :], in_=cc1_o[:, :]))
            sc1_g = sG.n

            @block.vector
            def _(vector):
                vector.wait_ge(sG.h, sc1_g)
                sV.inc(vector.reduce_max(out=wm1[:, :], in_=srow[:, :], axis=AX.X))
            sc1_v = sV.n

            @block.vector
            def _(vector):
                F = lambda: vector.tensor_copy(out=pjk[:, :], in_=pjk2[:, :])
                vector.wait_ge(sG.h, sc1_g)
                vector.tensor_scalar(out=sxv[:, :], in0=sgx[0:1, 0:1], scalar1=1.0 / 448.0,
                                     scalar2=1e-6, op0=ALU.mult, op1=ALU.max)
                vector.tensor_scalar(out=sw1v[:, :], in0=wm1[:, :], scalar1=1.0 / 448.0,
                                     scalar2=1e-6, op0=ALU.mult, op1=ALU.max)
                F()
                vector.reciprocal(out=rsx[:, :], in_=sxv[:, :])
                vector.reciprocal(out=rsw1[:, :], in_=sw1v[:, :])
                F()
                vector.tensor_scalar(out=scrow[0:1, 0:1], in0=rsx[:, :], scalar1=0.5, scalar2=None, op0=ALU.mult)
                vector.tensor_scalar(out=scrow[0:1, 4:5], in0=rsw1[:, :], scalar1=0.5, scalar2=None, op0=ALU.mult)
                vector.tensor_scalar(out=scrow[0:1, 1:2], in0=sxv[:, :], scalar1=sw1v[0:1, 0:1],
                                     scalar2=4.0, op0=ALU.mult, op1=ALU.mult)
                F()
                sV.inc(vector.tensor_copy(out=pjk[:, :], in_=pjk2[:, :]))
            sc3_v = sV.n

            @block.gpsimd
            def _(gpsimd):
                gpsimd.wait_ge(sV.h, sc3_v)
                sG.dma(gpsimd.dma_start(out=bc_s[:, :], in_=scrow[:, :]))
                gpsimd.wait_ge(sG.h, sG.n)
                sG.dma(gpsimd.dma_start(out=scal[:, :], in_=bc_s.ap().broadcast_to([128, 8])))
            scales1_g = sG.n

        with ExitStack() as p4a:
            E4a = p4a.enter_context
            x32 = [E4a(nc.sbuf_tensor(f"p4x{i}", [128, L], F32)) for i in range(2)]
            xf8 = [E4a(nc.sbuf_tensor(f"p4f{i}", [128, L], F8)) for i in range(2)]
            bV0, bC0 = sV.n, sC.n

            @block.vector
            def _(vector):
                vector.wait_ge(sG.h, scales1_g)
                vector.wait_ge(sM.h, p3_done_M)
                for d in range(ND):
                    if d >= 2:
                        vector.wait_ge(sC.h, bC0 + d - 1)
                    sV.inc(vector.tensor_tensor(out=x32[d % 2][:, :], in0=xnhi[d][:, :],
                                                in1=xnlo[d][:, :], op=ALU.add))
                    if d >= 1:
                        vector.wait_ge(sC.h, bC0 + d)
                        sV.inc(vector.tensor_copy(out=xnhi[d - 1][:, :], in_=xf8[(d - 1) % 2][:, :]))
                vector.wait_ge(sC.h, bC0 + ND)
                sV.inc(vector.tensor_copy(out=xnhi[ND - 1][:, :], in_=xf8[(ND - 1) % 2][:, :]))

            @block.scalar
            def _(scalar):
                for d in range(ND):
                    scalar.wait_ge(sV.h, bV0 + 2 * d + 1)
                    sC.inc(scalar.activation(out=xf8[d % 2][:, :], in_=x32[d % 2][:, :],
                                             func=AF.Copy, scale=scal[:, 0:1]))
            p4a_V = sV.n
        p3_scope.close()

        with ExitStack() as p4:
            E4 = p4.enter_context
            wf32r = [E4(nc.sbuf_tensor(f"p4w{i}", [128, D], F32)) for i in range(2)]
            wf8r = [E4(nc.sbuf_tensor(f"p4q{i}", [128, D], F8)) for i in range(2)]
            wb16r = [E4(nc.sbuf_tensor(f"p4b{i}", [128, D], BF16)) for i in range(2)]
            w8T = [[E4(nc.sbuf_tensor(f"p4t{ss}_{i}", [128, 512], BF16, is_transpose_dest=True))
                    for i in range(ND)] for ss in range(2)]
            hrelu = [E4(nc.sbuf_tensor(f"p4hr{i}", [128, 512], F32)) for i in range(2)]
            htile = [E4(nc.sbuf_tensor(f"p4ht{i}", [128, 512], F32)) for i in range(2)]
            hmx = [E4(nc.sbuf_tensor(f"p4hm{i}", [128, 1], F32)) for i in range(2)]
            pjk = E4(nc.sbuf_tensor("p4jk", [128, 1], F32))
            pjk2 = E4(nc.sbuf_tensor("p4jk2", [128, 1], F32))
            wps = E4(nc.sbuf_tensor("wps", [128, 8192], F32))
            wmx2 = [E4(nc.sbuf_tensor(f"wmx2_{i}", [128, 1], F32)) for i in range(2)]
            wm2 = E4(nc.sbuf_tensor("wm2", [1, 1], F32))
            sw2v = E4(nc.sbuf_tensor("sw2v", [1, 1], F32))
            rsw2 = E4(nc.sbuf_tensor("rsw2", [1, 1], F32))
            bHW2 = [sm.n for sm in sHp]

            def wscan2_issue(eng, p):
                sHp[p % 4].dma(eng.dma_start(
                    out=wps[:, 4096 * (p % 2):4096 * (p % 2 + 1)],
                    in_=wpj_d[128 * (p // 2):128 * (p // 2 + 1),
                              4096 * (p % 2):4096 * (p % 2 + 1)]))
            ws2 = {"p": 0, "pend": [], "cnt": 0}

            def wscan2_chain(vector, widx):
                c_ = ws2["cnt"]
                aa = 6 if c_ % 2 == 0 else 7
                bb = 7 if c_ % 2 == 0 else 6
                sS2.inc(vector.tensor_tensor(out=smax[:, bb:bb + 1], in0=smax[:, aa:aa + 1],
                                             in1=wmx2[widx][:, :], op=ALU.max))
                ws2["cnt"] += 1

            def wscan2_quantum(vector):
                p = ws2["p"]
                if p >= 32:
                    return
                vector.wait_ge(sHp[p % 4].h, bHW2[p % 4] + 16 * (p // 4 + 1))
                sS2.inc(vector.reduce_max(out=wmx2[p % 2][:, :],
                                          in_=wps[:, 4096 * (p % 2):4096 * (p % 2 + 1)],
                                          axis=AX.X, apply_absolute_value=True))
                ws2["pend"].append(p % 2)
                if len(ws2["pend"]) >= 2:
                    wscan2_chain(vector, ws2["pend"].pop(0))
                ws2["p"] = p + 1

            @block.vector
            def _(vector):
                sV.inc(vector.memset(pjk2[:, :], 0.0))

            bL0, bC0b, bV0b, bG0 = sL.n, sC.n, sV.n, sG.n

            @block.sync
            def _(sync):
                sync.wait_ge(sV.h, p4a_V)
                for j in range(J // 128):
                    if j >= 2:
                        sync.wait_ge(sC.h, bC0b + j - 1)
                    sL.dma(sync.dma_start(out=wf32r[j % 2][:, :], in_=wfc_d[128 * j:128 * (j + 1), :]))

            @block.scalar
            def _(scalar):
                scalar.wait_ge(sG.h, scales1_g)
                for j in range(J // 128):
                    scalar.wait_ge(sL.h, bL0 + 16 * (j + 1))
                    if j >= 2:
                        scalar.wait_ge(sV.h, bV0b + j - 1)
                    sC.inc(scalar.activation(out=wf8r[j % 2][:, :], in_=wf32r[j % 2][:, :],
                                             func=AF.Copy, scale=scal[:, 4:5]))

            @block.vector
            def _(vector):
                for j in range(J // 128):
                    vector.wait_ge(sC.h, bC0b + j + 1)
                    if j >= 2:
                        vector.wait_ge(sG.h, bG0 + 16 * (j - 1))
                    sV.inc(vector.tensor_copy(out=wb16r[j % 2][:, :], in_=wf8r[j % 2][:, :]))

            @block.gpsimd
            def _(gpsimd):
                for j in range(J // 128):
                    gpsimd.wait_ge(sV.h, bV0b + j + 1)
                    sG.dma(gpsimd.dma_start(out=w8b_s[128 * j:128 * (j + 1), :], in_=wb16r[j % 2][:, :]))
            p4b_G = sG.n

            NJG = J // 512
            PPG = 16
            MTOT = NJG * PPG
            GPM = 32 if debug else 16
            bL1, bM1, bV1, bC1, bG1 = sL.n, sM.n, sV.n, sC.n, sG.n

            @block.sync
            def _(sync):
                for g in range(NJG):
                    sync.wait_ge(sG.h, bG0 + 16 * min(4 * (g + 1), J // 128))
                    if g >= 2:
                        sync.wait_ge(sM.h, bM1 + (g - 1) * PPG)
                    for d in range(ND):
                        sL.dma(sync.dma_start(out=w8T[g % 2][d][:, :],
                                              in_=w8b_s[512 * g:512 * (g + 1), 128 * d:128 * (d + 1)],
                                              transpose=True))

            @block.tensor
            def _(tensor):
                tensor.wait_ge(sV.h, p4a_V)
                tensor.wait_ge(sV.h, p3_done_V)
                m = 0
                for g in range(NJG):
                    tensor.wait_ge(sL.h, bL1 + 16 * ND * (g + 1))
                    for jc in range(4):
                        for t in range(NTT):
                            if m >= 2:
                                tensor.wait_ge(sC.h, bC1 + 2 * (m - 2) + 1)
                            p = pA if m % 2 == 0 else pB
                            for d in range(ND):
                                ins = tensor.matmul(p[:, :], w8T[g % 2][d][:, 128 * jc:128 * (jc + 1)],
                                                    xnhi[d][:, 512 * t:512 * (t + 1)],
                                                    start=(d == 0), stop=(d == ND - 1))
                            sM.inc(ins)
                            m += 1

            @block.scalar
            def _(scalar):
                for m in range(MTOT):
                    scalar.wait_ge(sM.h, bM1 + m + 1)
                    p = pA if m % 2 == 0 else pB
                    if m >= 3:
                        scalar.wait_ge(sG.h, bG1 + GPM * (m - 2))
                    sC.inc(scalar.activation(out=hrelu[m % 2][:, :], in_=p[:, :],
                                             func=AF.Relu, scale=scal[:, 1:2]))
                    if m >= 1:
                        sC.inc(scalar.activation(out=htile[(m - 1) % 2][:, :],
                                                 in_=hrelu[(m - 1) % 2][:, :], func=AF.Square))
                scalar.wait_ge(sG.h, bG1 + GPM * (MTOT - 2))
                sC.inc(scalar.activation(out=htile[(MTOT - 1) % 2][:, :],
                                         in_=hrelu[(MTOT - 1) % 2][:, :], func=AF.Square))

            @block.vector
            def _(vector):
                for m in range(MTOT):
                    vector.wait_ge(sC.h, bC1 + min(2 * m + 3, 2 * MTOT))
                    vector.reduce_max(out=hmx[m % 2][:, :], in_=htile[m % 2][:, :], axis=AX.X)
                    aa = 2 if m % 2 == 0 else 3
                    bb = 3 if m % 2 == 0 else 2
                    sV.inc(vector.tensor_tensor(out=smax[:, bb:bb + 1], in0=smax[:, aa:aa + 1],
                                                in1=hmx[(m - 1) % 2][:, :] if m >= 1 else smax[:, aa:aa + 1],
                                                op=ALU.max))
                    if m % 8 == 4:
                        wscan2_quantum(vector)
                vector.tensor_copy(out=pjk[:, :], in_=pjk2[:, :])
                m = MTOT
                aa = 2 if m % 2 == 0 else 3
                bb = 3 if m % 2 == 0 else 2
                sV.inc(vector.tensor_tensor(out=smax[:, bb:bb + 1], in0=smax[:, aa:aa + 1],
                                            in1=hmx[(m - 1) % 2][:, :], op=ALU.max))
                while ws2["p"] < 32:
                    wscan2_quantum(vector)
                if ws2["pend"]:
                    vector.tensor_copy(out=pjk[:, :], in_=pjk2[:, :])
                    wscan2_chain(vector, ws2["pend"].pop(0))
            p4c_hcol = 3 if MTOT % 2 == 0 else 2
            scan2_total = sS2.n

            @block.gpsimd
            def _(gpsimd):
                gpsimd.wait_ge(sV.h, p4a_V)
                wscan2_issue(gpsimd, 0)
                wscan2_issue(gpsimd, 1)
                for m in range(MTOT):
                    g, rem = divmod(m, PPG)
                    jc, t = divmod(rem, NTT)
                    gpsimd.wait_ge(sV.h, bV1 + m + 1)
                    rr = 512 * g + 128 * jc
                    sG.dma(gpsimd.dma_start(out=h_s[rr:rr + 128, 512 * t:512 * (t + 1)],
                                            in_=htile[m % 2][:, :]))
                    if debug:
                        sG.dma(gpsimd.dma_start(out=dbg["h"][rr:rr + 128, 512 * t:512 * (t + 1)],
                                                in_=htile[m % 2][:, :]))
                    if m % 8 == 4:
                        q = m // 8 + 2
                        if q < 32:
                            gpsimd.wait_ge(sS2.h, 2 * (q - 2) + 1)
                            wscan2_issue(gpsimd, q)
            @block.gpsimd
            def _(gpsimd):
                gpsimd.wait_ge(sS2.h, scan2_total)
                sG.dma(gpsimd.dma_start(out=row_s[:, :], in_=smax[:, 6:7]))
                gpsimd.wait_ge(sG.h, sG.n)
                sG.dma(gpsimd.dma_start(out=srow[:, :], in_=row_s[:, :]))
            sc2b_g = sG.n

            @block.vector
            def _(vector):
                F = lambda: vector.tensor_copy(out=pjk[:, :], in_=pjk2[:, :])
                vector.wait_ge(sG.h, sc2b_g)
                vector.reduce_max(out=wm2[:, :], in_=srow[:, :], axis=AX.X)
                F()
                vector.tensor_scalar(out=sw2v[:, :], in0=wm2[:, :], scalar1=1.0 / 448.0,
                                     scalar2=1e-6, op0=ALU.mult, op1=ALU.max)
                F()
                vector.reciprocal(out=rsw2[:, :], in_=sw2v[:, :])
                F()
                vector.tensor_copy(out=spers[0:1, 0:1], in_=sw2v[:, :])
                vector.tensor_scalar(out=spers[0:1, 1:2], in0=rsw2[:, :], scalar1=0.5,
                                     scalar2=None, op0=ALU.mult)
                F()
                sV.inc(vector.tensor_copy(out=pjk[:, :], in_=pjk2[:, :]))

            p4_done_G = sG.n
            p4_done_V = sV.n
            p4_done_C = sC.n
            p4_done_M = sM.n

        @block.gpsimd
        def _(gpsimd):
            gpsimd.wait_ge(sV.h, p4_done_V)
            sG.dma(gpsimd.dma_start(out=row_s[:, :], in_=smax[:, p4c_hcol:p4c_hcol + 1]))
            gpsimd.wait_ge(sG.h, sG.n)
            sG.dma(gpsimd.dma_start(out=srow[:, :], in_=row_s[:, :]))
        p5_row = sG.n

        @block.vector
        def _(vector):
            vector.wait_ge(sG.h, p5_row)
            vector.memset(sone[:, :], 0.0)
            sV.inc(vector.reduce_max(out=sone[0:1, 0:1], in_=srow[:, :], axis=AX.X))
        p5_lmax = sV.n

        @block.gpsimd
        def _(gpsimd):
            gpsimd.wait_ge(sV.h, p5_lmax)
            sG.dma(gpsimd.dma_start(out=cc2_i[:, :], in_=sone[:, :]))
            gpsimd.wait_ge(sG.h, sG.n)
            cc.inc(gpsimd.collective_compute(
                "AllReduce", ALU.max, replica_groups=[list(range(N_CORES))],
                ins=[cc2_i.ap().opt()], outs=[cc2_o.ap().opt()]))
        cc2_done = cc.n

        p8p = ExitStack()
        E8p = p8p.enter_context
        sgx2 = E8p(nc.sbuf_tensor("p8gx", [1, 8], F32))
        shv = E8p(nc.sbuf_tensor("p8sh", [1, 1], F32))
        rshv = E8p(nc.sbuf_tensor("p8rs", [1, 1], F32))
        scrow2 = E8p(nc.sbuf_tensor("p8sr", [1, 8], F32))
        pjk8 = E8p(nc.sbuf_tensor("p8jk", [128, 1], F32))
        pjk82 = E8p(nc.sbuf_tensor("p8jk2", [128, 1], F32))
        pf32 = [E8p(nc.sbuf_tensor(f"p8w{i}", [128, D], F32)) for i in range(2)]
        pf8 = [E8p(nc.sbuf_tensor(f"p8q{i}", [128, D], F8)) for i in range(2)]
        pb16 = [E8p(nc.sbuf_tensor(f"p8b{i}", [128, D], BF16)) for i in range(2)]
        bGpre = sG.n

        @block.vector
        def _(vector):
            vector.wait_ge(sG.h, bGpre)
            sV.inc(vector.memset(pjk82[:, :], 0.0))

        @block.gpsimd
        def _(gpsimd):
            gpsimd.wait_ge(cc.h, cc2_done)
            gpsimd.wait_ge(sV.h, p5_lmax)
            sG.dma(gpsimd.dma_start(out=sgx2[:, :], in_=cc2_o[:, :]))
        p8a_g = sG.n

        @block.vector
        def _(vector):
            F = lambda: vector.tensor_copy(out=pjk8[:, :], in_=pjk82[:, :])
            vector.wait_ge(sG.h, p8a_g)
            vector.memset(scrow2[:, :], 0.0)
            vector.tensor_scalar(out=shv[:, :], in0=sgx2[0:1, 0:1], scalar1=1.0 / 448.0,
                                 scalar2=1e-6, op0=ALU.mult, op1=ALU.max)
            F()
            vector.reciprocal(out=rshv[:, :], in_=shv[:, :])
            F()
            vector.tensor_scalar(out=scrow2[0:1, 2:3], in0=rshv[:, :], scalar1=0.5, scalar2=None, op0=ALU.mult)
            vector.tensor_scalar(out=scrow2[0:1, 3:4], in0=shv[:, :], scalar1=spers[0:1, 0:1],
                                 scalar2=4.0, op0=ALU.mult, op1=ALU.mult)
            vector.tensor_copy(out=scrow2[0:1, 5:6], in_=spers[0:1, 1:2])
            F()
            sV.inc(vector.tensor_copy(out=pjk8[:, :], in_=pjk82[:, :]))
        p8a_v = sV.n

        @block.gpsimd
        def _(gpsimd):
            gpsimd.wait_ge(sV.h, p8a_v)
            sG.dma(gpsimd.dma_start(out=bc_s[:, :], in_=scrow2[:, :]))
            gpsimd.wait_ge(sG.h, sG.n)
            sG.dma(gpsimd.dma_start(out=scal[:, :], in_=bc_s.ap().broadcast_to([128, 8])))
        p8a_scal = sG.n

        def prod_src(j):
            if j < 64:
                return h_s[128 * j:128 * (j + 1), :]
            c = j - 64
            return wpj_d[128 * (c // 4):128 * (c // 4 + 1), 2048 * (c % 4):2048 * (c % 4 + 1)]

        def prod_dst(j):
            if j < 64:
                return h8_s[128 * j:128 * (j + 1), :]
            c = j - 64
            return w8pb_s[128 * (c // 4):128 * (c // 4 + 1), 2048 * (c % 4):2048 * (c % 4 + 1)]

        def prod_col(j):
            return 2 if j < 64 else 5
        bSp = sS.n
        NPROD = J // 128 + 64

        with ExitStack() as p6:
            E6 = p6.enter_context
            qh128 = [[E6(nc.sbuf_tensor(f"q128{b}_{hl}", [128, L], BF16)) for hl in range(2)] for b in range(2)]
            qh64 = [[E6(nc.sbuf_tensor(f"q64{b}_{hl}", [64, L], BF16)) for hl in range(2)] for b in range(2)]
            kh128 = [[E6(nc.sbuf_tensor(f"k128{b}_{hl}", [128, L], BF16)) for hl in range(2)] for b in range(2)]
            kh64 = [[E6(nc.sbuf_tensor(f"k64{b}_{hl}", [64, L], BF16)) for hl in range(2)] for b in range(2)]
            vh = [[E6(nc.sbuf_tensor(f"vh{b}_{hl}", [128, L], BF16)) for hl in range(2)] for b in range(2)]
            p32 = [E6(nc.sbuf_tensor(f"p6p{i}", [128, 512], F32)) for i in range(2)]
            phi = [E6(nc.sbuf_tensor(f"p6h{i}", [128, 512], BF16)) for i in range(2)]
            plo = [E6(nc.sbuf_tensor(f"p6l{i}", [128, 512], BF16)) for i in range(2)]
            rbc = E6(nc.sbuf_tensor("p6rb", [128, 512], F32))
            avn = E6(nc.sbuf_tensor("p6av", [128, 512], F32))
            ohhi = [E6(nc.sbuf_tensor(f"p6oh{i}", [128, 512], BF16)) for i in range(2)]
            ohlo = [E6(nc.sbuf_tensor(f"p6ol{i}", [128, 512], BF16)) for i in range(2)]
            pjk = E6(nc.sbuf_tensor("p6jk", [128, 1], F32))
            pjk2 = E6(nc.sbuf_tensor("p6jk2", [128, 1], F32))

            @block.vector
            def _(vector):
                sV.inc(vector.memset(pjk2[:, :], 0.0))

            bL6, bM6, bV6, bC6, bG6 = sL.n, sM.n, sV.n, sC.n, sG.n
            MPH = 4 * 17
            CPH = 4 * 16
            VPQ = 16 + 3
            VPH = 4 * VPQ
            GPH = 8

            @block.sync
            def _(sync):
                sync.wait_ge(sG.h, p4_done_G)
                sync.wait_ge(sM.h, p4_done_M)
                sync.wait_ge(sC.h, p4_done_C)
                for h in range(P6_NH):
                    b = h % 2
                    if h >= 1:
                        sync.wait_ge(sM.h, bM6 + (h - 1) * MPH + 1)
                    for hl in range(2):
                        sL.dma(sync.dma_start(out=qh128[b][hl][:, :], in_=q_s[hl, 128 * h:128 * (h + 1), :]))
                        sL.dma(sync.dma_start(out=qh64[b][hl][:, :], in_=q_s[hl, 2048 + 64 * h:2048 + 64 * (h + 1), :]))
                        sL.dma(sync.dma_start(out=kh128[b][hl][:, :], in_=k_s[hl, 128 * h:128 * (h + 1), :]))
                        sL.dma(sync.dma_start(out=kh64[b][hl][:, :], in_=k_s[hl, 2048 + 64 * h:2048 + 64 * (h + 1), :]))
                        for c4 in range(4):
                            sL.dma(sync.dma_start(
                                out=vh[b][hl][:, 512 * c4:512 * (c4 + 1)].rearrange("p (c f) -> p c f", f=128),
                                in_=v_s[hl, 512 * c4:512 * (c4 + 1), 128 * h:128 * (h + 1)].rearrange("(c p) f -> p c f", p=128)))

            @block.tensor
            def _(tensor):
                tensor.wait_ge(sV.h, p4_done_V)
                tensor.wait_ge(sC.h, p4_done_C)
                for h in range(P6_NH):
                    b = h % 2
                    tensor.wait_ge(sL.h, bL6 + 16 * 16 * (h + 1))
                    for qt in range(4):
                        qsl = slice(512 * qt, 512 * (qt + 1))
                        for kt in range(16):
                            ksl = slice(128 * kt, 128 * (kt + 1))
                            ps = pA if kt % 2 == 0 else pB
                            if kt >= 2:
                                tensor.wait_ge(sC.h, bC6 + h * CPH + qt * 16 + kt - 1)
                            tensor.matmul(ps[:, :], kh128[b][0][:, ksl], qh128[b][0][:, qsl], start=True, stop=False)
                            tensor.matmul(ps[:, :], kh128[b][0][:, ksl], qh128[b][1][:, qsl], start=False, stop=False)
                            tensor.matmul(ps[:, :], kh128[b][1][:, ksl], qh128[b][0][:, qsl], start=False, stop=False)
                            tensor.matmul(ps[:, :], kh64[b][0][:, ksl], qh64[b][0][:, qsl], start=False, stop=False)
                            tensor.matmul(ps[:, :], kh64[b][0][:, ksl], qh64[b][1][:, qsl], start=False, stop=False)
                            sM.inc(tensor.matmul(ps[:, :], kh64[b][1][:, ksl], qh64[b][0][:, qsl], start=False, stop=True))
                            if kt >= 1:
                                kp = kt - 1
                                kpsl = slice(128 * kp, 128 * (kp + 1))
                                tensor.wait_ge(sV.h, bV6 + h * VPH + qt * VPQ + kp + 1)
                                e = kp % 2
                                tensor.matmul(pBC[:, :], ones_cb[:, :], phi[e][:, :],
                                              start=(kp == 0), stop=False)
                                tensor.matmul(pBC[:, :], ones_cb[:, :], plo[e][:, :],
                                              start=False, stop=(kp == 15))
                                tensor.matmul(pC[:, :], vh[b][0][:, kpsl], phi[e][:, :],
                                              start=(kp == 0), stop=False)
                                tensor.matmul(pC[:, :], vh[b][1][:, kpsl], phi[e][:, :], start=False, stop=False)
                                tensor.matmul(pC[:, :], vh[b][0][:, kpsl], plo[e][:, :],
                                              start=False, stop=(kp == 15))
                        kp = 15
                        kpsl = slice(128 * kp, 128 * (kp + 1))
                        tensor.wait_ge(sV.h, bV6 + h * VPH + qt * VPQ + kp + 1)
                        e = kp % 2
                        tensor.matmul(pBC[:, :], ones_cb[:, :], phi[e][:, :], start=(kp == 0), stop=False)
                        tensor.matmul(pBC[:, :], ones_cb[:, :], plo[e][:, :], start=False, stop=(kp == 15))
                        tensor.matmul(pC[:, :], vh[b][0][:, kpsl], phi[e][:, :], start=(kp == 0), stop=False)
                        tensor.matmul(pC[:, :], vh[b][1][:, kpsl], phi[e][:, :], start=False, stop=False)
                        sM.inc(tensor.matmul(pC[:, :], vh[b][0][:, kpsl], plo[e][:, :],
                                             start=False, stop=(kp == 15)))

            @block.scalar
            def _(scalar):
                for h in range(P6_NH):
                    for qt in range(4):
                        for kt in range(16):
                            scalar.wait_ge(sM.h, bM6 + h * MPH + qt * 17 + kt + 1)
                            if kt >= 2:
                                scalar.wait_ge(sV.h, bV6 + h * VPH + qt * VPQ + kt - 1)
                            ps = pA if kt % 2 == 0 else pB
                            sC.inc(scalar.activation(out=p32[kt % 2][:, :], in_=ps[:, :],
                                                     func=AF.Exp, scale=INV_SQRT_DH))
                        s_ = h * 4 + qt
                        if s_ == 0:
                            scalar.wait_ge(sG.h, p8a_scal)
                            for j in (0, 1):
                                sS.dma(scalar.dma_start(out=pf32[j % 2][:, :], in_=prod_src(j)))
                        for j in (2 * s_, 2 * s_ + 1):
                            scalar.wait_ge(sS.h, bSp + 16 * (2 * s_ + 2))
                            if j >= 2:
                                scalar.wait_ge(sPV.h, j - 1)
                            sPC.inc(scalar.activation(out=pf8[j % 2][:, :], in_=pf32[j % 2][:, :],
                                                      func=AF.Copy,
                                                      scale=scal[:, prod_col(j):prod_col(j) + 1]))
                        if s_ + 1 < NPROD // 2:
                            for j in (2 * s_ + 2, 2 * s_ + 3):
                                scalar.wait_ge(sPC.h, j - 1)
                                sS.dma(scalar.dma_start(out=pf32[j % 2][:, :], in_=prod_src(j)))

            @block.vector
            def _(vector):
                for h in range(P6_NH):
                    for qt in range(4):
                        for kt in range(16):
                            vector.wait_ge(sC.h, bC6 + h * CPH + qt * 16 + kt + 1)
                            e = kt % 2
                            vector.tensor_copy(out=phi[e][:, :], in_=p32[e][:, :])
                            vector.tensor_copy(out=pjk[:, :], in_=pjk2[:, :])
                            sV.inc(vector.scalar_tensor_tensor(out=plo[e][:, :], in0=p32[e][:, :],
                                                               scalar=1.0, in1=phi[e][:, :],
                                                               op0=ALU.mult, op1=ALU.subtract))
                        vector.wait_ge(sM.h, bM6 + h * MPH + qt * 17 + 17)
                        sV.inc(vector.reciprocal(out=rbc[:, :], in_=pBC[:, :]))
                        vector.tensor_copy(out=pjk[:, :], in_=pjk2[:, :])
                        sV.inc(vector.tensor_tensor(out=avn[:, :], in0=pC[:, :], in1=rbc[:, :],
                                                    op=ALU.mult))
                        if h >= 1 or qt >= 2:
                            vector.wait_ge(sG.h, bG6 + 2 * 16 * (h * 4 + qt))
                        hq = h * 4 + qt
                        vector.tensor_copy(out=ohhi[hq % 2][:, :], in_=avn[:, :])
                        vector.tensor_copy(out=pjk[:, :], in_=pjk2[:, :])
                        sV.inc(vector.scalar_tensor_tensor(out=ohlo[hq % 2][:, :], in0=avn[:, :], scalar=1.0,
                                                           in1=ohhi[hq % 2][:, :], op0=ALU.mult,
                                                           op1=ALU.subtract))
                        for j in (2 * hq, 2 * hq + 1):
                            vector.wait_ge(sPC.h, j + 1)
                            if j >= 2:
                                vector.wait_ge(sPG.h, 16 * j)
                            sPV.inc(vector.tensor_copy(out=pb16[j % 2][:, :], in_=pf8[j % 2][:, :]))

            @block.gpsimd
            def _(gpsimd):
                for h in range(P6_NH):
                    for qt in range(4):
                        hq = h * 4 + qt
                        gpsimd.wait_ge(sV.h, bV6 + h * VPH + qt * VPQ + VPQ)
                        qsl = slice(512 * qt, 512 * (qt + 1))
                        sG.dma(gpsimd.dma_start(out=oh_s[0, 128 * h:128 * (h + 1), qsl], in_=ohhi[hq % 2][:, :]))
                        sG.dma(gpsimd.dma_start(out=oh_s[1, 128 * h:128 * (h + 1), qsl], in_=ohlo[hq % 2][:, :]))
                        for j in (2 * hq, 2 * hq + 1):
                            gpsimd.wait_ge(sPV.h, j + 1)
                            sPG.dma(gpsimd.dma_start(out=prod_dst(j), in_=pb16[j % 2][:, :]))
            p6_done_G = sG.n
            p6_done_M = sM.n
            p6_done_V = sV.n
            p6_done_C = sC.n
        p8p.close()

        with ExitStack() as p7:
            E7 = p7.enter_context
            oht = [[E7(nc.sbuf_tensor(f"p7o{hl}_{i}", [128, 512], BF16)) for i in range(16)] for hl in range(2)]
            worh = [E7(nc.sbuf_tensor(f"p7wh{i}", [128, 512], BF16)) for i in range(2)]
            worl = [E7(nc.sbuf_tensor(f"p7wl{i}", [128, 512], BF16)) for i in range(2)]
            xch = [E7(nc.sbuf_tensor(f"p7x{i}", [128, D], F32)) for i in range(4)]
            ost = [E7(nc.sbuf_tensor(f"p7s{i}", [128, 512], F32)) for i in range(4)]
            bL7, bM7, bV7, bY7 = sL.n, sM.n, sV.n, sY.n
            wo_hi_d, wo_lo_d = wsplit["wo"]
            LPT = 164 * 16

            @block.sync
            def _(sync):
                sync.wait_ge(sG.h, p6_done_G)
                sync.wait_ge(sM.h, p6_done_M)
                sync.wait_ge(sC.h, p6_done_C)
                sync.wait_ge(sV.h, p6_done_V)
                sync.wait_ge(sPG.h, 16 * NPROD)
                for tau in range(NTT):
                    tok = tau * TT
                    if tau >= 1:
                        sync.wait_ge(sV.h, bV7 + 4 * tau)
                    for hd in range(16):
                        for hl in range(2):
                            sL.dma(sync.dma_start(out=oht[hl][hd][:, :],
                                                  in_=oh_s[hl, 128 * hd:128 * (hd + 1), tok:tok + TT]))
                    for tc in range(4):
                        sL.dma(sync.dma_start(out=xch[tc][:, :],
                                              in_=x_d[tok + 128 * tc:tok + 128 * (tc + 1), :]))
                    for dt in range(4):
                        for hd in range(16):
                            gk = dt * 16 + hd
                            if gk >= 2:
                                g2, k2 = divmod(gk - 2, 16)
                                sync.wait_ge(sM.h, bM7 + tau * 64 + g2 * 16 + k2 + 1)
                            else:
                                sync.wait_ge(sM.h, bM7 + tau * 64)
                            sL.dma(sync.dma_start(out=worh[gk % 2][:, :],
                                                  in_=wo_hi_d[128 * hd:128 * (hd + 1), 512 * dt:512 * (dt + 1)]))
                            sL.dma(sync.dma_start(out=worl[gk % 2][:, :],
                                                  in_=wo_lo_d[128 * hd:128 * (hd + 1), 512 * dt:512 * (dt + 1)]))

            @block.tensor
            def _(tensor):
                tensor.wait_ge(sV.h, p6_done_V)
                tensor.wait_ge(sC.h, p6_done_C)
                for tau in range(NTT):
                    tensor.wait_ge(sL.h, bL7 + LPT * tau + 36 * 16)
                    for dt in range(4):
                        gdt = tau * 4 + dt
                        if gdt >= 1:
                            tensor.wait_ge(sV.h, bV7 + gdt)
                        for hd in range(16):
                            gk = dt * 16 + hd
                            tensor.wait_ge(sL.h, bL7 + LPT * tau + 36 * 16 + 32 * (gk + 1))
                            e = gk % 2
                            last = None
                            for tc in range(4):
                                ohh = oht[0][hd][:, 128 * tc:128 * (tc + 1)]
                                ohl = oht[1][hd][:, 128 * tc:128 * (tc + 1)]
                                tensor.matmul(PS4[tc][:, :], ohh, worh[e][:, :], start=(hd == 0), stop=False)
                                tensor.matmul(PS4[tc][:, :], ohl, worh[e][:, :], start=False, stop=False)
                                last = tensor.matmul(PS4[tc][:, :], ohh, worl[e][:, :],
                                                     start=False, stop=(hd == 15))
                            sM.inc(last)

            @block.vector
            def _(vector):
                for tau in range(NTT):
                    for dt in range(4):
                        gdt = tau * 4 + dt
                        vector.wait_ge(sM.h, bM7 + (gdt + 1) * 16)
                        if gdt >= 1:
                            vector.wait_ge(sY.h, bY7 + 64 * gdt)
                        for tc in range(4):
                            vector.tensor_tensor(out=ost[tc][:, :], in0=PS4[tc][:, :],
                                                 in1=xch[tc][:, 512 * dt:512 * (dt + 1)], op=ALU.add)
                        sV.inc(vector.tensor_copy(out=ost[0][0:1, 0:1], in_=ost[0][0:1, 0:1]))

            @block.gpsimd
            def _(gpsimd):
                for tau in range(NTT):
                    for dt in range(4):
                        gdt = tau * 4 + dt
                        gpsimd.wait_ge(sV.h, bV7 + gdt + 1)
                        for tc in range(4):
                            r0 = tau * TT + 128 * tc
                            sY.dma(gpsimd.dma_start(out=y_d[r0:r0 + 128, 512 * dt:512 * (dt + 1)],
                                                    in_=ost[tc][:, :]))
            p7_Y = sY.n
            p7_M = sM.n
            p7_V = sV.n

        if stage <= 5:
            @block.gpsimd
            def _(gpsimd):
                gpsimd.wait_ge(sY.h, sY.n)
            return nc

        with ExitStack() as p8:
            E8 = p8.enter_context
            w8pT = [[E8(nc.sbuf_tensor(f"p8t{ss}_{i}", [128, 512], BF16, is_transpose_dest=True)) for i in range(32)] for ss in range(2)]
            h8slab = E8(nc.sbuf_tensor("p8hs", [128, 4096], BF16))
            acc = [E8(nc.sbuf_tensor(f"p8a{i}", [128, 512], F32)) for i in range(16)]
            stg8 = [E8(nc.sbuf_tensor(f"p8o{i}", [128, 512], F32)) for i in range(4)]
            tmp8 = [E8(nc.sbuf_tensor(f"p8tm{i}", [128, 512], F32)) for i in range(2)]

            pjk9 = E8(nc.sbuf_tensor("p8jk9", [128, 1], F32))
            pjk92 = E8(nc.sbuf_tensor("p8jk92", [128, 1], F32))

            @block.vector
            def _(vector):
                vector.wait_ge(sV.h, p7_V)
                sV.inc(vector.memset(pjk92[:, :], 0.0))

            bL8, bM8, bV8, bY8 = sL.n, sM.n, sV.n, sY.n
            sSp = [sS, sS2]
            bS8p = [sS.n, sS2.n]
            bHp = [s.n for s in sHp]
            HPT = 32 * 16

            @block.scalar
            def _(scalar):
                scalar.wait_ge(sG.h, p6_done_G)
                scalar.wait_ge(sY.h, p7_Y)
                scalar.wait_ge(sM.h, p7_M)
                scalar.wait_ge(sV.h, p7_V)
                for dt in range(4):
                    for jh in range(2):
                        djh = dt * 2 + jh
                        if djh == 0:
                            scalar.wait_ge(sPG.h, 16 * NPROD)
                        elif djh >= 2:
                            scalar.wait_ge(sM.h, bM8 + (djh - 1) * 128)
                        for jc in range(32):
                            jg = jh * 32 + jc
                            sSp[djh % 2].dma(scalar.dma_start(out=w8pT[djh % 2][jc][:, :],
                                                              in_=w8pb_s[512 * dt:512 * (dt + 1), 128 * jg:128 * (jg + 1)],
                                                              transpose=True))

            @block.sync
            def _(sync):
                sync.wait_ge(sG.h, p6_done_G)
                sync.wait_ge(sY.h, p7_Y)
                sync.wait_ge(sM.h, p7_M)
                sync.wait_ge(sV.h, p7_V)
                sync.wait_ge(sPG.h, 16 * NPROD)
                g = 0
                for dt in range(4):
                    for jh in range(2):
                        for tau in range(NTT):
                            for jc2 in range(16):
                                if g >= 4:
                                    sync.wait_ge(sM.h, bM8 + 2 * g - 6)
                                jg = jh * 32 + 2 * jc2
                                sHp[g % 4].dma(sync.dma_start(
                                    out=h8slab[:, 1024 * (g % 4):1024 * (g % 4 + 1)].rearrange(
                                        "p (c f) -> p c f", f=512),
                                    in_=h8_s[128 * jg:128 * (jg + 2),
                                             512 * tau:512 * (tau + 1)].rearrange(
                                        "(c p) f -> p c f", p=128)))
                                g += 1


            @block.tensor
            def _(tensor):
                tensor.wait_ge(sV.h, p7_V)
                gidx = 0
                ldx = 0
                for dt in range(4):
                    for jh in range(2):
                        djh = dt * 2 + jh
                        tensor.wait_ge(sSp[djh % 2].h, bS8p[djh % 2] + 32 * 16 * (djh // 2 + 1))
                        for tau in range(NTT):
                            if gidx >= 1:
                                tensor.wait_ge(sV.h, bV8 + gidx)
                            for jc in range(32):
                                if ldx % 2 == 0:
                                    gg = ldx // 2
                                    tensor.wait_ge(sHp[gg % 4].h,
                                                   bHp[gg % 4] + 16 * (gg // 4 + 1))
                                last = None
                                sb = 512 * (ldx % 8)
                                for tc in range(4):
                                    last = tensor.matmul(PS4[tc][:, :],
                                                         h8slab[:, sb + 128 * tc:sb + 128 * (tc + 1)],
                                                         w8pT[djh % 2][jc][:, :],
                                                         start=(jc == 0), stop=(jc == 31))
                                sM.inc(last)
                                ldx += 1
                            gidx += 1

            @block.vector
            def _(vector):
                gidx = 0
                for dt in range(4):
                    for jh in range(2):
                        for tau in range(NTT):
                            vector.wait_ge(sM.h, bM8 + (gidx + 1) * 32)
                            if jh == 0:
                                for tc in range(4):
                                    vector.tensor_copy(out=acc[tau * 4 + tc][:, :], in_=PS4[tc][:, :])
                            else:
                                g1 = dt * 4 + tau
                                if g1 >= 1:
                                    vector.wait_ge(sY.h, bY8 + 64 * g1)
                                for pr in range(2):
                                    t0, t1 = 2 * pr, 2 * pr + 1
                                    vector.tensor_tensor(out=tmp8[0][:, :], in0=PS4[t0][:, :],
                                                         in1=acc[tau * 4 + t0][:, :], op=ALU.add)
                                    vector.tensor_tensor(out=tmp8[1][:, :], in0=PS4[t1][:, :],
                                                         in1=acc[tau * 4 + t1][:, :], op=ALU.add)
                                    vector.tensor_scalar(out=stg8[t0][:, :], in0=tmp8[0][:, :],
                                                         scalar1=scal[:, 3:4], scalar2=None, op0=ALU.mult)
                                    vector.tensor_scalar(out=stg8[t1][:, :], in0=tmp8[1][:, :],
                                                         scalar1=scal[:, 3:4], scalar2=None, op0=ALU.mult)
                            sV.inc(vector.tensor_copy(out=pjk9[:, :], in_=pjk92[:, :]))
                            gidx += 1

            @block.gpsimd
            def _(gpsimd):
                gpsimd.wait_ge(sY.h, p7_Y)
                gidx = 0
                for dt in range(4):
                    for jh in range(2):
                        for tau in range(NTT):
                            gidx += 1
                            if jh == 0:
                                continue
                            gpsimd.wait_ge(sV.h, bV8 + gidx)
                            for tc in range(4):
                                r0 = tau * TT + 128 * tc
                                sY.dma(gpsimd.dma_start(out=y_d[r0:r0 + 128, 512 * dt:512 * (dt + 1)],
                                                        in_=stg8[tc][:, :], accum_op=ALU.add))

        @block.gpsimd
        def _(gpsimd):
            gpsimd.wait_ge(sY.h, sY.n)
            if debug:
                for i in range(J // 128):
                    sY.dma(gpsimd.dma_start(out=dbg["h8"][128 * i:128 * (i + 1), :],
                                            in_=h8_s[128 * i:128 * (i + 1), :]))
                for i in range(D // 128):
                    sY.dma(gpsimd.dma_start(out=dbg["w8p"][128 * i:128 * (i + 1), :],
                                            in_=w8pb_s[128 * i:128 * (i + 1), :]))
                sY.dma(gpsimd.dma_start(out=dbg["sc2"][:, :], in_=scal[0:1, 0:8]))
                for hl in range(2):
                    for r in range(NTC):
                        sY.dma(gpsimd.dma_start(out=dbg["oh"][hl, 128 * r:128 * (r + 1), :],
                                                in_=oh_s[hl, 128 * r:128 * (r + 1), :]))
                    for r in range(NH * (DH + DHR) // 128):
                        sY.dma(gpsimd.dma_start(out=dbg["q"][hl, 128 * r:128 * (r + 1), :],
                                                in_=q_s[hl, 128 * r:128 * (r + 1), :]))
                        sY.dma(gpsimd.dma_start(out=dbg["k"][hl, 128 * r:128 * (r + 1), :],
                                                in_=k_s[hl, 128 * r:128 * (r + 1), :]))
                    for r in range(L // 128):
                        sY.dma(gpsimd.dma_start(out=dbg["v"][hl, 128 * r:128 * (r + 1), :],
                                                in_=v_s[hl, 128 * r:128 * (r + 1), :]))
                gpsimd.wait_ge(sY.h, sY.n)
        return nc

        if stage <= 4:
            @block.gpsimd
            def _(gpsimd):
                gpsimd.wait_ge(sG.h, p6_done_G)
                for i in range(NTC):
                    sY.dma(gpsimd.dma_start(out=y_d[128 * i:128 * (i + 1), :],
                                            in_=xnhi[i][:, 0:2048]))
                if debug:
                    for hl in range(2):
                        for r in range(NTC):
                            sY.dma(gpsimd.dma_start(out=dbg["oh"][hl, 128 * r:128 * (r + 1), :],
                                                    in_=oh_s[hl, 128 * r:128 * (r + 1), :]))
                gpsimd.wait_ge(sY.h, sY.n)
            return nc

        if stage <= 3:
            @block.gpsimd
            def _(gpsimd):
                gpsimd.wait_ge(sG.h, p4_done_G)
                for i in range(NTC):
                    sY.dma(gpsimd.dma_start(out=y_d[128 * i:128 * (i + 1), :],
                                            in_=xnhi[i][:, 0:2048]))
                gpsimd.wait_ge(sY.h, sY.n)
            return nc

        if stage <= 2:
            @block.gpsimd
            def _(gpsimd):
                gpsimd.wait_ge(sG.h, p3_done_G)
                for i in range(NTC):
                    sY.dma(gpsimd.dma_start(out=y_d[128 * i:128 * (i + 1), :],
                                            in_=xnhi[i][:, 0:2048]))
                if debug:
                    for hl in range(2):
                        for r in range(NH * (DH + DHR) // 128):
                            sY.dma(gpsimd.dma_start(out=dbg["q"][hl, 128 * r:128 * (r + 1), :],
                                                    in_=q_s[hl, 128 * r:128 * (r + 1), :]))
                            sY.dma(gpsimd.dma_start(out=dbg["k"][hl, 128 * r:128 * (r + 1), :],
                                                    in_=k_s[hl, 128 * r:128 * (r + 1), :]))
                        for r in range(NTC):
                            sY.dma(gpsimd.dma_start(out=dbg["v"][hl, 128 * r:128 * (r + 1), :],
                                                    in_=v_s[hl, 128 * r:128 * (r + 1), :]))
                    gpsimd.wait_ge(cc.h, cc1_done)
                    sG.dma(gpsimd.dma_start(out=sone[:, :], in_=cc1_o[:, :]))
                    gpsimd.wait_ge(sG.h, sG.n)
                    sY.dma(gpsimd.dma_start(out=dbg["sc"][:, :], in_=sone[:, :]))
                gpsimd.wait_ge(sY.h, sY.n)
            return nc

        raise NotImplementedError("stages beyond 1 not yet implemented")

    return nc



def _rope_rt():
    rt = np.zeros((128, 128), np.float32)
    for blk in (0, 64):
        for i in range(32):
            rt[blk + i + 32, blk + i] = -1.0
            rt[blk + i, blk + i + 32] = 1.0
    return rt


def _rope_tables():
    inv = (1.0 / (10000.0 ** (np.arange(0, DHR, 2, dtype=np.float32) / DHR))).astype(np.float32)
    ang = (np.arange(L, dtype=np.float32)[None, :] * inv[:, None]).astype(np.float32)
    return np.cos(ang).astype(np.float32), np.sin(ang).astype(np.float32)


def make_inputs(L0_attn_x, L0_attn_W_D_Q, L0_attn_W_U_Q, L0_attn_W_Q_R, L0_attn_W_D_KV,
                L0_attn_W_U_K, L0_attn_W_K_R, L0_attn_W_U_V, L0_attn_W_O,
                L0_ffn_w_fc, L0_ffn_w_proj):
    cos_t, sin_t = _rope_tables()
    consts = dict(
        w_dq=np.ascontiguousarray(L0_attn_W_D_Q, np.float32),
        w_uq=np.ascontiguousarray(L0_attn_W_U_Q, np.float32),
        w_qr=np.ascontiguousarray(L0_attn_W_Q_R, np.float32),
        w_dkv=np.ascontiguousarray(L0_attn_W_D_KV, np.float32),
        w_uk=np.ascontiguousarray(L0_attn_W_U_K, np.float32),
        w_kr=np.ascontiguousarray(L0_attn_W_K_R, np.float32),
        w_uv=np.ascontiguousarray(L0_attn_W_U_V, np.float32),
        w_o=np.ascontiguousarray(L0_attn_W_O, np.float32),
        w_fc=np.ascontiguousarray(L0_ffn_w_fc, np.float32),
        w_pj=np.ascontiguousarray(L0_ffn_w_proj, np.float32),
        cos_t=cos_t, sin_t=sin_t,
        ones_r=np.ones((1, 128), np.float32),
        ones_cb=np.ones((128, 128), ml_dtypes.bfloat16),
        rope_rt=_rope_rt(),
    )
    x = np.asarray(L0_attn_x, dtype=np.float32)
    return [dict(consts, x=np.ascontiguousarray(x[c])) for c in range(N_CORES)]


def kernel(**inputs):
    in_maps = make_inputs(**inputs)
    nc = build_nc()
    res = run_bass_kernel_spmd(nc, in_maps, core_ids=list(range(N_CORES)))
    return np.stack([res.results[c]["y"] for c in range(N_CORES)], axis=0)
